# version 01
# baseline (speedup 1.0000x reference)
import numpy as np
import ml_dtypes
from collections import defaultdict
from contextlib import ExitStack

import concourse.bass as bass
import concourse.mybir as mybir
from concourse.bass_utils import run_bass_kernel_spmd

F32 = mybir.dt.float32
BF16 = mybir.dt.bfloat16
I32 = mybir.dt.int32
I32 = mybir.dt.int32
AF = mybir.ActivationFunctionType
ALU = mybir.AluOpType
AX = mybir.AxisListType
bf16 = ml_dtypes.bfloat16

COMPUTE = ("t", "v", "s", "g", "y")
ASYNC = ("yd", "gd", "sd", "gc")
ISSUER = {"yd": "y", "gd": "g", "sd": "s", "gc": "g"}


class _Op:
    __slots__ = ("eng", "fns", "reads", "writes", "seq", "amount", "tag",
                 "dep_ops", "waits", "inc_needed", "val", "is_ckpt",
                 "ckpt_queue", "ckpt_cover")

    def __init__(self, eng, fns, reads, writes, seq, amount, tag):
        self.eng = eng; self.fns = fns; self.reads = reads; self.writes = writes
        self.seq = seq; self.amount = amount; self.tag = tag
        self.dep_ops = []; self.waits = {}
        self.inc_needed = False; self.val = None
        self.is_ckpt = False; self.ckpt_queue = None; self.ckpt_cover = 0


class Seq:
    def __init__(self):
        self.ops = []
        self.per_eng = defaultdict(list)
        self.writer = {}
        self.readers = defaultdict(dict)
        self.ckpts = defaultdict(list)

    def _mkop(self, eng, fns, reads, writes, amount, tag):
        op = _Op(eng, fns, list(reads), list(writes),
                 len(self.per_eng[eng]) + 1, amount, tag)
        self.ops.append(op)
        self.per_eng[eng].append(op)
        return op

    def checkpoint(self, queue, tag=None):
        assert queue in ASYNC
        cover = len(self.per_eng[queue])
        if cover == 0:
            return None
        if self.ckpts[queue] and self.ckpts[queue][-1].ckpt_cover >= cover:
            return self.ckpts[queue][-1]
        op = self._mkop(ISSUER[queue], [lambda e: e.nop()], [], [], 1,
                        tag or f"ckpt_{queue}_{cover}")
        op.is_ckpt = True; op.ckpt_queue = queue; op.ckpt_cover = cover
        self.ckpts[queue].append(op)
        return op

    def _dep_for(self, dep_op):
        if dep_op.eng in ASYNC:
            for cp in self.ckpts[dep_op.eng]:
                if cp.ckpt_cover >= dep_op.seq:
                    return cp
            return self.checkpoint(dep_op.eng)
        return dep_op

    def _record(self, eng, fns, reads, writes, amount, tag):
        deps = {}

        def need(d):
            if d is None:
                return
            prev = deps.get(d.eng)
            if prev is None or d.seq > prev.seq:
                deps[d.eng] = d

        for b in reads:
            w = self.writer.get(b)
            if w is not None:
                need(self._dep_for(w))
        for b in writes:
            w = self.writer.get(b)
            if w is not None:
                need(self._dep_for(w))
            for rop in self.readers.get(b, {}).values():
                need(self._dep_for(rop))
        op = self._mkop(eng, fns, reads, writes, amount, tag)
        op.dep_ops = [d for d in deps.values()
                      if not (d.eng == eng and d.seq >= op.seq)]
        for b in op.writes:
            self.writer[b] = op
            self.readers[b] = {}
        for b in op.reads:
            self.readers[b][eng] = op
        return op

    def op(self, eng, fn, reads=(), writes=(), tag=None):
        assert eng in COMPUTE
        return self._record(eng, [fn], reads, writes, 1, tag)

    def group(self, eng, fns, reads=(), writes=(), tag=None):
        assert eng in COMPUTE
        return self._record(eng, list(fns), reads, writes, 1, tag)

    def dma(self, queue, fn, reads=(), writes=(), tag=None):
        assert queue in ("yd", "gd", "sd")
        return self._record(queue, [fn], reads, writes, 16, tag)

    def cc(self, fn, reads=(), writes=(), tag=None):
        return self._record("gc", [fn], reads, writes, 1, tag)

    def finalize(self, nc, block):
        targets = defaultdict(set)
        for op in self.ops:
            for d in op.dep_ops:
                targets[d.eng].add(d.seq)
        for e, ops in self.per_eng.items():
            v = 0
            always = e in ASYNC
            for o in ops:
                if always or o.seq in targets[e] or o.is_ckpt:
                    o.inc_needed = True
                    v += o.amount
                o.val = v
        for op in self.ops:
            rw = {}
            for d in op.dep_ops:
                assert d.inc_needed
                if d.val > 0 and rw.get(d.eng, 0) < d.val:
                    rw[d.eng] = d.val
            op.waits = rw
            if op.is_ckpt:
                q = op.ckpt_queue
                op.waits[q] = self.per_eng[q][op.ckpt_cover - 1].val

        stack = ExitStack()
        sems = {e: stack.enter_context(nc.semaphore(f"sem_{e}"))
                for e in list(self.per_eng)}
        self._semstack = stack
        phys_of = {"t": "t", "v": "v", "s": "s", "g": "g", "y": "y",
                   "yd": "y", "gd": "g", "sd": "s", "gc": "g"}
        by_phys = defaultdict(list)
        for op in self.ops:
            by_phys[phys_of[op.eng]].append(op)
        decorators = {"t": block.tensor, "v": block.vector, "s": block.scalar,
                      "g": block.gpsimd, "y": block.sync}
        stats = {"n_ops": len(self.ops), "n_incs": 0, "n_waits": 0}
        for pe, ops_list in by_phys.items():
            tail = [(q, self.per_eng[q][-1].val) for q in ASYNC
                    if self.per_eng[q] and ISSUER[q] == pe]

            def body(eng, _ops=ops_list, _tail=tail):
                obs = defaultdict(int)
                for o in _ops:
                    for e2 in sorted(o.waits, key=str):
                        v = o.waits[e2]
                        if obs[e2] < v:
                            eng.wait_ge(sems[e2], v)
                            obs[e2] = v
                            stats["n_waits"] += 1
                    inst = None
                    for fn in o.fns:
                        inst = fn(eng)
                    if o.inc_needed:
                        inst.then_inc(sems[o.eng], o.amount)
                        stats["n_incs"] += 1
                for q, v in _tail:
                    if obs[q] < v:
                        eng.wait_ge(sems[q], v)
            decorators[pe](body)
        self.stats = stats
        return sems


NCORES = 8
B, L, D = 2, 8192, 1024
TC = 2048
DT = 8
NEG = -1e30
PHASE_LIMIT = 99

CB_ONE = 0
CB_EPS5 = 1
CB_I127 = 3


class KB:

    def __init__(self, nc, S, ctx):
        self.nc = nc
        self.S = S
        self.ctx = ctx
        self.ap = {}

    def inp(self, name, shape, dtype=F32):
        self.ap[name] = self.nc.dram_tensor(name, list(shape), dtype,
                                            kind="ExternalInput").ap()
        return self.ap[name]

    def sb(self, name, shape, dtype=F32):
        return self.ctx.enter_context(self.nc.sbuf_tensor(name, shape, dtype))

    def psum(self, name, shape, dtype=F32):
        return self.ctx.enter_context(self.nc.psum_tensor(name, shape, dtype))

    def load(self, queue, dst_ap, src_ap, key, reads=(), tag=None):
        wk = key if isinstance(key, list) else [key]
        self.S.dma(queue, lambda g, d=dst_ap, s=src_ap: g.dma_start(out=d, in_=s),
                   writes=wk, reads=list(reads), tag=tag)

    def store(self, queue, dst_ap, src_ap, reads, key=None, tag=None):
        wk = key if isinstance(key, list) else ([key] if key else [])
        self.S.dma(queue, lambda g, d=dst_ap, s=src_ap: g.dma_start(out=d, in_=s),
                   reads=list(reads), writes=wk, tag=tag)


def wkeys(i, c0=0, c1=4):
    return [("W", i, c) for c in range(c0, c1)]


def xkeys(d, c0=0, c1=4):
    return [("xT", d, c) for c in range(c0, c1)]


def build_nc(phase_limit=99):
    nc = bass.Bass("TRN2", target_bir_lowering=False, dynamic_dma_scratch_size=16384)
    S = Seq()
    with ExitStack() as ctx:
        kb = KB(nc, S, ctx)
        inp, sb = kb.inp, kb.sb

        inp("x_tok", [TC, D])
        inp("m_xprojT", [D, 96])
        inp("m_dtwT", [64, D])
        inp("m_dtb", [128, 8])
        inp("m_Alog", [128, 128])
        inp("m_D", [128, 8])
        inp("m_lna", [128, 8])
        inp("m_lnb", [128, 8])
        inp("identity", [128, 128])
        inp("ones_row", [128, 128])
        inp("cbank", [128, 8])
        inp("selBC", [32, 32 * 128])
        inp("ccpre", [128, 4])
        inp("f_wqkv", [6, D, 3072])
        inp("f_wo", [6, D, D])
        inp("f_w1", [6, D, 5632])
        inp("f_w2", [6, 2816, D])
        inp("f_b1", [6, 128, 44])
        inp("f_lng", [6, 128, 8])
        inp("f_lnb2", [6, 128, 8])
        inp("f_b2", [6, 128, 8])
        inp("f_qs8", [6, 128, 1])
        inp("f_ks", [6, 128, 1])
        inp("f_ropeqA", [128, TC])
        inp("f_ropeqB", [128, TC])
        inp("f_ropekA", [128, TC])
        inp("f_ropekB", [128, TC])
        inp("f_ropekAh", [128, 256])
        inp("f_ropekBh", [128, 256])
        inp("f_maskup", [128, 128])
        inp("f_masklo", [128, 128])
        inp("f_m00", [128, 128])
        inp("f_m01", [128, 128])
        inp("f_m10", [128, 128])
        inp("f_selE2", [128, 16 * 64])
        inp("f_selN", [64, 16 * 128])
        inp("f_ccm", [128, 4])
        inp("f_l1w", [D, 3072])
        inp("f_l1proj", [D, D])
        inp("f_l1bq", [128, 24])
        inp("f_l1bp", [128, 8])
        inp("f_l1bias", [128, 2048])
        inp("f_ccn", [128, 4])
        inp("f_ccpw", [128, 4])
        inp("f_wgT", [D, 3072], I32)
        inp("f_wuT", [D, 3072], I32)
        inp("f_wdT", [3072, D], I32)
        y_tok = nc.dram_tensor("y_tok", [TC, D], F32, kind="ExternalOutput").ap()

        ag_scan_in = nc.dram_tensor("ag_scan_in", [1, 128 * 136], F32)
        kb.y_scr = nc.dram_tensor("y_scr", [8, 128, 2048], F32, kind="ExternalOutput")
        ag_scan_out = nc.dram_tensor("ag_scan_out", [4, 128 * 136], F32)
        groups4 = [[0, 1, 2, 3], [4, 5, 6, 7]]

        kb.xT = [sb(f"xT{d}", [128, TC]) for d in range(DT)]
        kb.s_id = sb("s_id", [128, 128])
        kb.s_idb = sb("s_idb", [128, 128], BF16)
        kb.s_cb = sb("s_cb", [128, 8])
        kb.s_cbb = sb("s_cbb", [128, 8], BF16)
        kb.s_ones = sb("s_ones", [128, 128])
        kb.W = [sb(f"W{i}", [128, TC], BF16) for i in range(16)]
        kb.s_stats = sb("s_stats", [128, TC])
        kb.s_delta = sb("s_delta", [128, TC])
        kb.s_aT = sb("s_aT", [128, TC])
        kb.pA = kb.psum("pA", [128, 2048])
        kb.pB = kb.psum("pB", [128, 2048])
        kb.P = ([kb.pA[:, 512 * i:512 * (i + 1)] for i in range(4)]
                + [kb.pB[:, 512 * i:512 * (i + 1)] for i in range(4)])

        block = ctx.enter_context(nc.Block())
        xT, W, P = kb.xT, kb.W, kb.P
        s_id, s_cb = kb.s_id, kb.s_cb

        kb.load("yd", s_id[:, :], kb.ap["identity"], "s_id")
        kb.load("yd", kb.s_ones[:, :], kb.ap["ones_row"], "s_ones")
        kb.load("yd", s_cb[:, :], kb.ap["cbank"], "s_cb")
        S.checkpoint("yd")
        S.op("s", lambda sc: sc.activation(kb.s_idb[:, :], s_id[:, :], AF.Copy),
             reads=["s_id"], writes=["s_idb"])
        S.op("s", lambda sc: sc.activation(kb.s_cbb[:, :], s_cb[:, :], AF.Copy),
             reads=["s_cb"], writes=["s_cbb"])

        for tt in range(16):
            stg = kb.W[tt % 2][:, :].bitcast(F32)
            kb.load("yd", stg[:, :], kb.ap["x_tok"][128 * tt:128 * (tt + 1), :],
                    ("W", tt % 2, 0), tag=f"ldx{tt}")
            for d in range(DT):
                bk = (tt * 8 + d) % 4
                S.op("t", lambda t, stg=stg, d=d, bk=bk:
                     t.transpose(P[bk][:, 0:128], stg[:, 128 * d:128 * (d + 1)], s_id[:, :]),
                     reads=[("W", tt % 2, 0), "s_id"], writes=[("P", bk)],
                     tag=f"xtr{tt}_{d}")
                S.op("s", lambda sc, d=d, tt=tt, bk=bk:
                     sc.activation(xT[d][:, 128 * tt:128 * (tt + 1)], P[bk][:, 0:128],
                                   AF.Copy),
                     reads=[("P", bk)], writes=[("xT", d, tt // 4)],
                     tag=f"xev{tt}_{d}")

        if phase_limit >= 1:
            build_mamba(kb, groups4, ag_scan_in, ag_scan_out)

        n_l0 = max(0, min(6, phase_limit - 1))
        if n_l0 > 0:
            for d in range(DT):
                kb.store("gd", kb.y_scr.ap()[d, :, 0:1024], xT[d][:, 0:1024],
                         reads=xkeys(d), key=("yscr", d, 0), tag=f"x1sp{d}a")
                kb.store("gd", kb.y_scr.ap()[d, :, 1024:2048], xT[d][:, 1024:2048],
                         reads=xkeys(d), key=("yscr", d, 1), tag=f"x1sp{d}b")
            build_l0ffn(kb, groups4, n_l0)
        if phase_limit >= 8:
            for d in range(DT):
                w0, w1 = W[2 * (d % 2)], W[2 * (d % 2) + 1]
                k0 = ("W", 2 * (d % 2), 0)
                k1 = ("W", 2 * (d % 2) + 1, 0)
                kb.load("gd", w0[:, :].bitcast(F32), kb.y_scr.ap()[d, :, 0:1024],
                        k0, reads=[("yscr", d, 0)], tag=f"x1ld{d}a")
                kb.load("gd", w1[:, :].bitcast(F32), kb.y_scr.ap()[d, :, 1024:2048],
                        k1, reads=[("yscr", d, 1)], tag=f"x1ld{d}b")
                S.op("v", lambda v, d=d, w0=w0: v.tensor_tensor(
                    xT[d][:, 0:1024], xT[d][:, 0:1024], w0[:, :].bitcast(F32),
                    ALU.add),
                    reads=[("xT", d, 0), ("xT", d, 1), k0],
                    writes=[("xT", d, 0), ("xT", d, 1)], tag=f"x1ad{d}a")
                S.op("v", lambda v, d=d, w1=w1: v.tensor_tensor(
                    xT[d][:, 1024:2048], xT[d][:, 1024:2048], w1[:, :].bitcast(F32),
                    ALU.add),
                    reads=[("xT", d, 2), ("xT", d, 3), k1],
                    writes=[("xT", d, 2), ("xT", d, 3)], tag=f"x1ad{d}b")
            build_topln(kb, tag="ln2")
        if phase_limit >= 9:
            build_l1attn(kb, groups4)
        if phase_limit >= 10:
            build_l1ffn(kb)

        for tt in range(16):
            stg = kb.W[tt % 2][:, :].bitcast(F32)
            for d in range(DT):
                bk = 4 + (tt * 8 + d) % 4
                S.op("t", lambda t, d=d, tt=tt, bk=bk:
                     t.transpose(P[bk][:, 0:128], xT[d][:, 128 * tt:128 * (tt + 1)],
                                 s_id[:, :]),
                     reads=[("xT", d, tt // 4), "s_id"], writes=[("P", bk)],
                     tag=f"otr{tt}_{d}")
                S.op("s", lambda sc, d=d, bk=bk, stg=stg, tt=tt:
                     sc.activation(stg[:, 128 * d:128 * (d + 1)], P[bk][:, 0:128],
                                   AF.Copy),
                     reads=[("P", bk)], writes=[("W", tt % 2, 0)],
                     tag=f"oev{tt}_{d}")
            kb.store("yd", y_tok[128 * tt:128 * (tt + 1), :], stg[:, :],
                     reads=[("W", tt % 2, 0)],
                     key=("ytok", tt), tag=f"sty{tt}")

        S.finalize(nc, block)
    return nc, S.stats


def build_mamba(kb, groups4, ag_scan_in, ag_scan_out):
    nc, S = kb.nc, kb.S
    xT, W, P = kb.xT, kb.W, kb.P
    s_cb, s_cbb, s_ones, s_stats = kb.s_cb, kb.s_cbb, kb.s_ones, kb.s_stats
    mstack = ExitStack()
    sb = lambda name, shape, dtype=F32: mstack.enter_context(
        nc.sbuf_tensor(name, shape, dtype))
    kb._mamba_stack = mstack

    s_xproj = sb("s_xproj", [128, 8 * 96], BF16)
    s_dtw = sb("s_dtw", [64, D], BF16)
    s_dtb = sb("s_dtb", [128, 8])
    s_Aneg = sb("s_Aneg", [128, 128])
    s_mD = sb("s_mD", [128, 8])
    s_lna = sb("s_lna", [128, 8])
    s_lnb = sb("s_lnb", [128, 8])
    s_selBC = sb("s_selBC", [32, 32 * 128], BF16)
    s_bc = sb("s_bc", [32, TC], BF16)
    s_ccp = sb("s_ccp", [128, 4])
    s_om = sb("s_om", [128, 4])
    s_hlast = sb("s_hlast", [128, 128])
    s_sumd = sb("s_sumd", [128, 8])
    s_hin = sb("s_hin", [128, 128])
    s_agscan = sb("s_agscan", [128, 4 * 136])
    s_dbcd = sb("s_dbcd", [64, TC], BF16)
    s_delta = kb.s_delta
    s_aT = kb.s_aT
    s_h32 = sb("s_h32", [128, TC])
    s_Ptmp = sb("s_Ptmp", [128, 128])

    for k in range(8):
        kb.load("gd", s_xproj[:, 96 * k:96 * (k + 1)],
                kb.ap["m_xprojT"][128 * k:128 * (k + 1), :], ("xproj", k))
    kb.load("gd", s_dtw[:, :], kb.ap["m_dtwT"], "dtw")
    kb.load("gd", s_dtb[:, :], kb.ap["m_dtb"], "dtb")
    kb.load("gd", s_Aneg[:, :], kb.ap["m_Alog"], "Alog")
    kb.load("gd", s_mD[:, :], kb.ap["m_D"], "mD")
    kb.load("gd", s_lna[:, :], kb.ap["m_lna"], "lna")
    kb.load("gd", s_lnb[:, :], kb.ap["m_lnb"], "lnb")
    kb.load("gd", s_selBC[:, :], kb.ap["selBC"], "selBC")
    kb.load("gd", s_ccp[:, :], kb.ap["ccpre"], "ccp")
    S.checkpoint("gd")

    S.op("s", lambda sc: sc.activation(s_Aneg[:, :], s_Aneg[:, :], AF.Exp),
         reads=["Alog"], writes=["Aneg_t"])
    S.op("v", lambda v: v.tensor_scalar(s_Aneg[:, :], s_Aneg[:, :], -1.0, None,
                                        ALU.mult),
         reads=["Aneg_t"], writes=["Aneg"])
    S.op("v", lambda v: v.tensor_scalar(s_om[:, :], s_ccp[:, :], -1.0, 1.0,
                                        ALU.mult, ALU.add),
         reads=["ccp"], writes=["s_om"])

    for ch in range(4):
        for k in range(8):
            wslot = W[0 + k % 2]
            S.op("s", lambda sc, k=k, ch=ch, wslot=wslot:
                 sc.activation(wslot[:, 0:512], xT[k][:, 512 * ch:512 * (ch + 1)],
                               AF.Copy),
                 reads=[("xT", k, ch)], writes=[("W", 0 + k % 2, 0)],
                 tag=f"xb{ch}_{k}")
            S.group("t",
                    [lambda t, k=k, ch=ch, wslot=wslot:
                     t.matmul(P[ch][0:64, 0:512], s_xproj[:, 96 * k:96 * k + 64],
                              wslot[:, 0:512], start=(k == 0), stop=(k == 7)),
                     lambda t, k=k, ch=ch, wslot=wslot:
                     t.matmul(P[4 + ch][0:32, 0:512],
                              s_xproj[:, 96 * k + 64:96 * (k + 1)],
                              wslot[:, 0:512], start=(k == 0), stop=(k == 7))],
                    reads=[("xproj", k), ("W", 0 + k % 2, 0)],
                    writes=[("P", ch), ("P", 4 + ch)], tag=f"dbc{ch}_{k}")
        S.op("s", lambda sc, ch=ch: sc.activation(
            s_dbcd[:, 512 * ch:512 * (ch + 1)], P[ch][0:64, 0:512], AF.Copy),
            reads=[("P", ch)], writes=[("dbcd", ch)], tag=f"dbcev{ch}")
        S.op("v", lambda v, ch=ch: v.tensor_copy(
            s_bc[:, 512 * ch:512 * (ch + 1)], P[4 + ch][0:32, 0:512]),
            reads=[("P", 4 + ch)], writes=[("stats",)], tag=f"bcev{ch}")

    dX = [W[8 + i] for i in range(8)]

    def emit_delta(d, p):
        for ch in range(4):
            bk = ch % 2
            S.op("t", lambda t, d=d, ch=ch, bk=bk:
                 t.matmul(P[bk][:, 0:512], s_dtw[:, 128 * d:128 * (d + 1)],
                          s_dbcd[:, 512 * ch:512 * (ch + 1)], start=True, stop=True),
                 reads=["dtw", ("dbcd", ch)], writes=[("P", bk)], tag=f"dt{p}{d}_{ch}")
            S.op("s", lambda sc, d=d, bk=bk:
                 sc.activation(P[bk][:, 0:512], P[bk][:, 0:512], AF.Exp,
                               bias=s_dtb[:, d:d + 1]),
                 reads=[("P", bk), "dtb"], writes=[("P", bk)], tag=f"dte{p}{d}_{ch}")
            S.op("s", lambda sc, d=d, ch=ch, bk=bk:
                 sc.activation(s_delta[:, 512 * ch:512 * (ch + 1)], P[bk][:, 0:512],
                               AF.Ln, bias=s_cb[:, CB_ONE:CB_ONE + 1]),
                 reads=[("P", bk), "s_cb"], writes=[("delta", ch)], tag=f"dtl{p}{d}_{ch}")
        if p == 1:
            S.op("v", lambda v, d=d: v.tensor_tensor(dX[d][:, :], s_delta[:, :],
                                                     xT[d][:, :], ALU.mult),
                 reads=[("delta", c) for c in range(4)] + xkeys(d),
                 writes=wkeys(8 + d), tag=f"dx{d}")
            S.op("v", lambda v, d=d: v.tensor_reduce(s_sumd[:, d:d + 1],
                                                     s_delta[:, :], AX.X, ALU.add),
                 reads=[("delta", c) for c in range(4)], writes=[("sumd", d)],
                 tag=f"sumd{d}")

    aT = s_aT
    hT, bT = W[5], W[6]
    for d in range(8):
        emit_delta(d, 1)
        for n in range(16):
            S.op("s", lambda sc, d=d, n=n:
                 sc.activation(aT[:, :], s_delta[:, :], AF.Exp,
                               scale=s_Aneg[:, 16 * d + n:16 * d + n + 1]),
                 reads=[("delta", c) for c in range(4)] + ["Aneg"],
                 writes=[("aT", c) for c in range(4)], tag=f"p1a{d}_{n}")
            for ch in range(4):
                S.op("t", lambda t, n=n, ch=ch:
                     t.matmul(P[4 + ch][:, 0:512], s_selBC[:, 128 * n:128 * (n + 1)],
                              s_bc[:, 512 * ch:512 * (ch + 1)],
                              start=True, stop=True),
                     reads=["selBC"] + [("stats",)],
                     writes=[("P", 4 + ch)], tag=f"p1B{d}_{n}_{ch}")
                S.op("v", lambda v, d=d, ch=ch:
                     v.tensor_tensor(bT[:, 512 * ch:512 * (ch + 1)],
                                     dX[d][:, 512 * ch:512 * (ch + 1)],
                                     P[4 + ch][:, 0:512], ALU.mult),
                     reads=[("W", 8 + d, ch)] + [("P", 4 + ch)],
                     writes=[("W", 6, ch)], tag=f"p1b{d}_{n}_{ch}")
            S.op("v", lambda v:
                 v.tensor_tensor_scan(s_h32[:, :], aT[:, :], bT[:, :], 0.0,
                                      ALU.mult, ALU.add),
                 reads=[("aT", c) for c in range(4)] + wkeys(6), writes=[("h32",)],
                 tag=f"p1s{d}_{n}")
            S.op("v", lambda v, d=d, n=n:
                 v.tensor_copy(s_hlast[:, 16 * d + n:16 * d + n + 1],
                               s_h32[:, TC - 1:TC]),
                 reads=[("h32",)], writes=[("hlast", d, n)], tag=f"p1h{d}_{n}")

    kb.store("gd", ag_scan_in.ap()[0, 0:128 * 128].rearrange("(p c) -> p c", p=128),
             s_hlast[:, :],
             reads=[("hlast", d, n) for d in range(8) for n in range(16)],
             key="agin_h")
    kb.store("gd", ag_scan_in.ap()[0, 128 * 128:128 * 136].rearrange("(p c) -> p c", p=128),
             s_sumd[:, :], reads=[("sumd", d) for d in range(8)], key="agin_s")
    S.checkpoint("gd")
    S.cc(lambda g: g.collective_compute(
        "AllGather", ALU.bypass, replica_groups=groups4,
        ins=[ag_scan_in.ap()[:, :]], outs=[ag_scan_out.ap()[:, :]]),
        reads=["agin_h", "agin_s"], writes=["ag_scan_out"], tag="ag_scan")
    for j in range(4):
        kb.load("gd", s_agscan[:, 136 * j:136 * j + 128],
                ag_scan_out.ap()[j, 0:128 * 128].rearrange("(p c) -> p c", p=128),
                ("agscan", j), reads=["ag_scan_out"])
        kb.load("gd", s_agscan[:, 136 * j + 128:136 * (j + 1)],
                ag_scan_out.ap()[j, 128 * 128:128 * 136].rearrange("(p c) -> p c", p=128),
                ("agscan", j, "s"), reads=["ag_scan_out"])

    S.op("v", lambda v: v.memset(s_hin[:, :], 0.0), writes=[("hin",)], tag="hin0")
    for j in range(4):
        for d in range(8):
            S.op("s", lambda sc, j=j, d=d:
                 sc.activation(s_Ptmp[:, 16 * d:16 * (d + 1)],
                               s_Aneg[:, 16 * d:16 * (d + 1)], AF.Exp,
                               scale=s_agscan[:, 136 * j + 128 + d:136 * j + 129 + d]),
                 reads=[("agscan", j), ("agscan", j, "s"), "Aneg"], writes=[("Ptmp", d)], tag=f"cmP{j}_{d}")
        PK = [("Ptmp", d) for d in range(8)]
        S.op("v", lambda v, j=j: v.tensor_scalar(
            s_Ptmp[:, :], s_Ptmp[:, :], s_ccp[:, j:j + 1], None, ALU.mult),
            reads=PK + ["ccp"], writes=PK, tag=f"cmb{j}")
        S.op("v", lambda v, j=j: v.tensor_scalar(
            s_Ptmp[:, :], s_Ptmp[:, :], s_om[:, j:j + 1], None, ALU.add),
            reads=PK + ["s_om"], writes=PK, tag=f"cmb2{j}")
        S.op("v", lambda v: v.tensor_tensor(s_hin[:, :], s_hin[:, :],
                                            s_Ptmp[:, :], ALU.mult),
             reads=[("hin",)] + PK, writes=[("hin",)], tag=f"cmm{j}")
        S.op("v", lambda v, j=j: v.scalar_tensor_tensor(
            s_hin[:, :], s_agscan[:, 136 * j:136 * j + 128], s_ccp[:, j:j + 1],
            s_hin[:, :], ALU.mult, ALU.add),
            reads=[("agscan", j), "ccp", ("hin",)], writes=[("hin",)], tag=f"cma{j}")


    hC = W[7]
    for d in range(8):
        emit_delta(d, 2)
        for n in range(16):
            S.op("s", lambda sc, d=d, n=n:
                 sc.activation(aT[:, :], s_delta[:, :], AF.Exp,
                               scale=s_Aneg[:, 16 * d + n:16 * d + n + 1]),
                 reads=[("delta", c) for c in range(4)] + ["Aneg"],
                 writes=[("aT", c) for c in range(4)], tag=f"p2a{d}_{n}")
            for ch in range(4):
                S.op("t", lambda t, n=n, ch=ch:
                     t.matmul(P[4 + ch][:, 0:512], s_selBC[:, 128 * n:128 * (n + 1)],
                              s_bc[:, 512 * ch:512 * (ch + 1)],
                              start=True, stop=True),
                     reads=["selBC"] + [("stats",)],
                     writes=[("P", 4 + ch)], tag=f"p2B{d}_{n}_{ch}")
                S.op("v", lambda v, d=d, ch=ch:
                     v.tensor_tensor(bT[:, 512 * ch:512 * (ch + 1)],
                                     dX[d][:, 512 * ch:512 * (ch + 1)],
                                     P[4 + ch][:, 0:512], ALU.mult),
                     reads=[("W", 8 + d, ch)] + [("P", 4 + ch)],
                     writes=[("W", 6, ch)], tag=f"p2b{d}_{n}_{ch}")
            S.op("v", lambda v, d=d, n=n:
                 v.tensor_tensor_scan(hT[:, :], aT[:, :], bT[:, :],
                                      s_hin[:, 16 * d + n:16 * d + n + 1],
                                      ALU.mult, ALU.add),
                 reads=[("aT", c) for c in range(4)] + wkeys(6) + [("hin",)],
                 writes=wkeys(5), tag=f"p2s{d}_{n}")
            for ch in range(4):
                S.op("t", lambda t, n=n, ch=ch:
                     t.matmul(P[4 + ch][:, 0:512],
                              s_selBC[:, 128 * (16 + n):128 * (17 + n)],
                              s_bc[:, 512 * ch:512 * (ch + 1)],
                              start=True, stop=True),
                     reads=["selBC"] + [("stats",)],
                     writes=[("P", 4 + ch)], tag=f"p2C{d}_{n}_{ch}")
                S.op("v", lambda v, ch=ch:
                     v.tensor_tensor(hC[:, 512 * ch:512 * (ch + 1)],
                                     hT[:, 512 * ch:512 * (ch + 1)],
                                     P[4 + ch][:, 0:512], ALU.mult),
                     reads=wkeys(5, ch, ch + 1) + [("P", 4 + ch)],
                     writes=[("W", 7, ch)], tag=f"p2hc{d}_{n}_{ch}")
            yv0 = W[2].bitcast(F32)
            yv1 = W[3].bitcast(F32)
            if n == 0:
                S.op("v", lambda v, yv0=yv0: v.tensor_copy(yv0[:, :], hC[:, 0:1024]),
                     reads=[("W", 7, 0), ("W", 7, 1)],
                     writes=[("W", 2, c) for c in range(4)], tag=f"p2y0{d}")
                S.op("v", lambda v, yv1=yv1: v.tensor_copy(yv1[:, :], hC[:, 1024:2048]),
                     reads=[("W", 7, 2), ("W", 7, 3)],
                     writes=[("W", 3, c) for c in range(4)], tag=f"p2y1{d}")
            else:
                S.op("v", lambda v, yv0=yv0: v.tensor_tensor(
                    yv0[:, :], yv0[:, :], hC[:, 0:1024], ALU.add),
                    reads=[("W", 7, 0), ("W", 7, 1)] + [("W", 2, c) for c in range(4)],
                    writes=[("W", 2, c) for c in range(4)], tag=f"p2ya{d}_{n}")
                S.op("v", lambda v, yv1=yv1: v.tensor_tensor(
                    yv1[:, :], yv1[:, :], hC[:, 1024:2048], ALU.add),
                    reads=[("W", 7, 2), ("W", 7, 3)] + [("W", 3, c) for c in range(4)],
                    writes=[("W", 3, c) for c in range(4)], tag=f"p2yb{d}_{n}")
        S.op("v", lambda v, d=d: v.scalar_tensor_tensor(
            W[2].bitcast(F32)[:, :], xT[d][:, 0:1024], s_mD[:, d:d + 1],
            W[2].bitcast(F32)[:, :], ALU.mult, ALU.add),
            reads=xkeys(d, 0, 2) + ["mD"] + [("W", 2, c) for c in range(4)],
            writes=[("W", 2, c) for c in range(4)], tag=f"yD{d}a")
        S.op("v", lambda v, d=d: v.scalar_tensor_tensor(
            W[3].bitcast(F32)[:, :], xT[d][:, 1024:2048], s_mD[:, d:d + 1],
            W[3].bitcast(F32)[:, :], ALU.mult, ALU.add),
            reads=xkeys(d, 2, 4) + ["mD"] + [("W", 3, c) for c in range(4)],
            writes=[("W", 3, c) for c in range(4)], tag=f"yD{d}b")
        for ch in range(4):
            yv = W[2].bitcast(F32) if ch < 2 else W[3].bitcast(F32)
            sl = slice(512 * (ch % 2), 512 * (ch % 2 + 1))
            S.op("t", lambda t, yv=yv, sl=sl: t.matmul(
                P[2][0:1, 0:512], s_cb[:, CB_ONE:CB_ONE + 1], yv[:, sl],
                start=True, stop=True),
                reads=[("W", 2 + ch // 2, c) for c in range(4)] + ["s_cb"],
                writes=[("P", 2)], tag=f"yst{d}_{ch}")
            if d == 0:
                S.op("s", lambda sc, ch=ch: sc.activation(
                    s_stats[32:33, 512 * ch:512 * (ch + 1)], P[2][0:1, 0:512],
                    AF.Copy),
                    reads=[("P", 2)], writes=[("stats",)], tag=f"ysc{d}_{ch}")
            else:
                S.op("v", lambda v, ch=ch: v.tensor_tensor(
                    s_stats[32:33, 512 * ch:512 * (ch + 1)],
                    s_stats[32:33, 512 * ch:512 * (ch + 1)], P[2][0:1, 0:512],
                    ALU.add),
                    reads=[("P", 2), ("stats",)], writes=[("stats",)],
                    tag=f"ysa{d}_{ch}")
            S.op("s", lambda sc, yv=yv, sl=sl, ch=ch: sc.activation(
                s_aT[:, 512 * ch:512 * (ch + 1)], yv[:, sl], AF.Square),
                reads=[("W", 2 + ch // 2, c) for c in range(4)],
                writes=[("aT", ch)], tag=f"ysq{d}_{ch}")
            S.op("t", lambda t, ch=ch: t.matmul(
                P[3][0:1, 0:512], s_cb[:, CB_ONE:CB_ONE + 1],
                s_aT[:, 512 * ch:512 * (ch + 1)], start=True, stop=True),
                reads=[("aT", ch), "s_cb"], writes=[("P", 3)], tag=f"ys2{d}_{ch}")
            if d == 0:
                S.op("s", lambda sc, ch=ch: sc.activation(
                    s_stats[64:65, 512 * ch:512 * (ch + 1)], P[3][0:1, 0:512],
                    AF.Copy),
                    reads=[("P", 3)], writes=[("stats",)], tag=f"y2c{d}_{ch}")
            else:
                S.op("v", lambda v, ch=ch: v.tensor_tensor(
                    s_stats[64:65, 512 * ch:512 * (ch + 1)],
                    s_stats[64:65, 512 * ch:512 * (ch + 1)], P[3][0:1, 0:512],
                    ALU.add),
                    reads=[("P", 3), ("stats",)], writes=[("stats",)],
                    tag=f"y2a{d}_{ch}")
        kb.store("gd", kb.y_scr.ap()[d, :, 0:1024], W[2].bitcast(F32)[:, :],
                 reads=[("W", 2, c) for c in range(4)], key=("yscr", d, 0),
                 tag=f"yspill{d}a")
        kb.store("gd", kb.y_scr.ap()[d, :, 1024:2048], W[3].bitcast(F32)[:, :],
                 reads=[("W", 3, c) for c in range(4)], key=("yscr", d, 1),
                 tag=f"yspill{d}b")

    S.op("v", lambda v: v.tensor_scalar(s_stats[32:33, :], s_stats[32:33, :],
                                        1.0 / 1024.0, None, ALU.mult),
         reads=[("stats",) for c in range(4)], writes=[("stats",) for c in range(4)],
         tag="mu")
    for ch in range(4):
        S.op("s", lambda sc, ch=ch: sc.activation(
            P[ch][0:1, 0:512], s_stats[32:33, 512 * ch:512 * (ch + 1)], AF.Square,
            scale=s_cb[0:1, 2:3]),
            reads=[("stats",), "s_cb"], writes=[("P", ch)], tag=f"mu2_{ch}")
    S.op("v", lambda v: v.tensor_scalar(s_stats[64:65, :], s_stats[64:65, :],
                                        1.0 / 1023.0, None, ALU.mult),
         reads=[("stats",) for c in range(4)], writes=[("stats",) for c in range(4)],
         tag="sqs")
    for ch in range(4):
        S.op("v", lambda v, ch=ch: v.tensor_tensor(
            s_stats[64:65, 512 * ch:512 * (ch + 1)],
            s_stats[64:65, 512 * ch:512 * (ch + 1)], P[ch][0:1, 0:512], ALU.subtract),
            reads=[("stats",), ("P", ch)], writes=[("stats",)],
            tag=f"var{ch}")
    S.op("s", lambda sc: sc.activation(s_stats[64:65, :], s_stats[64:65, :], AF.Sqrt),
         reads=[("stats",) for c in range(4)],
         writes=[("stats",) for c in range(4)], tag="std")
    S.op("v", lambda v: v.tensor_scalar(s_stats[64:65, :], s_stats[64:65, :],
                                        1e-12, None, ALU.add),
         reads=[("stats",) for c in range(4)],
         writes=[("stats",) for c in range(4)], tag="stde")
    S.op("v", lambda v: v.reciprocal(s_stats[64:65, :], s_stats[64:65, :]),
         reads=[("stats",) for c in range(4)],
         writes=[("stats",) for c in range(4)], tag="dinv")
    for ch in range(4):
        S.op("t", lambda t, ch=ch: t.matmul(
            P[ch][:, 0:512], s_ones[32:33, :], s_stats[32:33, 512 * ch:512 * (ch + 1)],
            start=True, stop=True),
            reads=[("stats",), "s_ones"], writes=[("P", ch)], tag=f"bmu{ch}")
        S.op("t", lambda t, ch=ch: t.matmul(
            P[4 + ch][:, 0:512], s_ones[64:65, :], s_stats[64:65, 512 * ch:512 * (ch + 1)],
            start=True, stop=True),
            reads=[("stats",), "s_ones"], writes=[("P", 4 + ch)], tag=f"bdi{ch}")
    for d in range(8):
        kb.load("gd", s_h32[:, 0:1024], kb.y_scr.ap()[d, :, 0:1024],
                ("h32",), reads=[("yscr", d, 0)], tag=f"yld{d}a")
        kb.load("gd", s_h32[:, 1024:2048], kb.y_scr.ap()[d, :, 1024:2048],
                ("h32",), reads=[("yscr", d, 1)], tag=f"yld{d}b")
        for ch in range(4):
            S.op("v", lambda v, ch=ch: v.tensor_tensor(
                s_delta[:, 512 * ch:512 * (ch + 1)],
                s_h32[:, 512 * ch:512 * (ch + 1)],
                P[ch][:, 0:512], ALU.subtract),
                reads=[("h32",), ("P", ch)], writes=[("delta", ch)],
                tag=f"ap1{d}_{ch}")
            S.op("v", lambda v, ch=ch: v.tensor_tensor(
                s_aT[:, 512 * ch:512 * (ch + 1)], s_delta[:, 512 * ch:512 * (ch + 1)],
                P[4 + ch][:, 0:512], ALU.mult),
                reads=[("delta", ch), ("P", 4 + ch)], writes=[("aT", ch)],
                tag=f"ap2{d}_{ch}")
        S.op("v", lambda v, d=d: v.scalar_tensor_tensor(
            xT[d][:, :], s_aT[:, :], s_lna[:, d:d + 1], xT[d][:, :],
            ALU.mult, ALU.add),
            reads=[("aT", c) for c in range(4)] + ["lna"] + xkeys(d),
            writes=xkeys(d), tag=f"ap3{d}")
        S.op("v", lambda v, d=d: v.tensor_scalar(
            xT[d][:, :], xT[d][:, :], s_lnb[:, d:d + 1], None, ALU.add),
            reads=["lnb"] + xkeys(d), writes=xkeys(d), tag=f"ap4{d}")

    build_topln(kb, tag="ln1")
    mstack.close()


def build_topln(kb, tag):
    S = kb.S
    xT, W, P = kb.xT, kb.W, kb.P
    s_cb, s_cbb, s_ones, s_stats = kb.s_cb, kb.s_cbb, kb.s_ones, kb.s_stats
    for ch in range(4):
        for d in range(8):
            S.op("s", lambda sc, d=d, ch=ch: sc.activation(
                W[4][:, 0:512], xT[d][:, 512 * ch:512 * (ch + 1)], AF.Copy),
                reads=[("xT", d, ch)], writes=[("W", 4, 0)], tag=f"{tag}c{ch}_{d}")
            S.op("s", lambda sc, d=d, ch=ch: sc.activation(
                W[5][:, 0:512], xT[d][:, 512 * ch:512 * (ch + 1)], AF.Square),
                reads=[("xT", d, ch)], writes=[("W", 5, 0)], tag=f"{tag}q{ch}_{d}")
            S.op("t", lambda t, d=d, ch=ch: t.matmul(
                P[ch][0:1, 0:512], s_cbb[:, 0:1], W[4][:, 0:512],
                start=(d == 0), stop=(d == 7)),
                reads=[("W", 4, 0), "s_cbb"], writes=[("P", ch)], tag=f"{tag}s{ch}_{d}")
            S.op("t", lambda t, d=d, ch=ch: t.matmul(
                P[4 + ch][0:1, 0:512], s_cbb[:, 0:1], W[5][:, 0:512],
                start=(d == 0), stop=(d == 7)),
                reads=[("W", 5, 0), "s_cbb"], writes=[("P", 4 + ch)],
                tag=f"{tag}s2{ch}_{d}")
    for ch in range(4):
        S.op("s", lambda sc, ch=ch: sc.activation(
            s_stats[32:33, 512 * ch:512 * (ch + 1)], P[ch][0:1, 0:512], AF.Copy),
            reads=[("P", ch)], writes=[("stats",)], tag=f"{tag}e{ch}")
        S.op("s", lambda sc, ch=ch: sc.activation(
            s_stats[64:65, 512 * ch:512 * (ch + 1)], P[4 + ch][0:1, 0:512], AF.Copy),
            reads=[("P", 4 + ch)], writes=[("stats",)], tag=f"{tag}e2{ch}")
    S.op("v", lambda v: v.tensor_scalar(s_stats[32:33, :], s_stats[32:33, :],
                                        1.0 / 1024.0, None, ALU.mult),
         reads=[("stats",) for c in range(4)],
         writes=[("stats",) for c in range(4)], tag=f"{tag}mu")
    for ch in range(4):
        S.op("s", lambda sc, ch=ch: sc.activation(
            P[ch][0:1, 0:512], s_stats[32:33, 512 * ch:512 * (ch + 1)], AF.Square),
            reads=[("stats",)], writes=[("P", ch)], tag=f"{tag}mu2_{ch}")
    S.op("v", lambda v: v.tensor_scalar(s_stats[64:65, :], s_stats[64:65, :],
                                        1.0 / 1024.0, None, ALU.mult),
         reads=[("stats",) for c in range(4)],
         writes=[("stats",) for c in range(4)], tag=f"{tag}ex2")
    for ch in range(4):
        S.op("v", lambda v, ch=ch: v.tensor_tensor(
            s_stats[64:65, 512 * ch:512 * (ch + 1)],
            s_stats[64:65, 512 * ch:512 * (ch + 1)], P[ch][0:1, 0:512], ALU.subtract),
            reads=[("stats",), ("P", ch)], writes=[("stats",)],
            tag=f"{tag}var{ch}")
    S.op("s", lambda sc: sc.activation(s_stats[64:65, :], s_stats[64:65, :],
                                       AF.Sqrt, bias=s_cb[0:1, CB_EPS5:CB_EPS5 + 1]),
         reads=[("stats",) for c in range(4)] + ["s_cb"],
         writes=[("stats",) for c in range(4)], tag=f"{tag}sd")
    S.op("v", lambda v: v.reciprocal(s_stats[64:65, :], s_stats[64:65, :]),
         reads=[("stats",) for c in range(4)],
         writes=[("stats",) for c in range(4)], tag=f"{tag}rs")
    for ch in range(4):
        S.op("t", lambda t, ch=ch: t.matmul(
            P[ch][:, 0:512], s_ones[32:33, :], s_stats[32:33, 512 * ch:512 * (ch + 1)],
            start=True, stop=True),
            reads=[("stats",), "s_ones"], writes=[("P", ch)], tag=f"{tag}bm{ch}")
        S.op("t", lambda t, ch=ch: t.matmul(
            P[4 + ch][:, 0:512], s_ones[64:65, :], s_stats[64:65, 512 * ch:512 * (ch + 1)],
            start=True, stop=True),
            reads=[("stats",), "s_ones"], writes=[("P", 4 + ch)],
            tag=f"{tag}br{ch}")
    for d in range(8):
        for ch in range(4):
            S.op("v", lambda v, d=d, ch=ch: v.tensor_tensor(
                xT[d][:, 512 * ch:512 * (ch + 1)], xT[d][:, 512 * ch:512 * (ch + 1)],
                P[ch][:, 0:512], ALU.subtract),
                reads=[("xT", d, ch), ("P", ch)], writes=[("xT", d, ch)],
                tag=f"{tag}a{d}_{ch}")
            S.op("v", lambda v, d=d, ch=ch: v.tensor_tensor(
                xT[d][:, 512 * ch:512 * (ch + 1)], xT[d][:, 512 * ch:512 * (ch + 1)],
                P[4 + ch][:, 0:512], ALU.mult),
                reads=[("xT", d, ch), ("P", 4 + ch)], writes=[("xT", d, ch)],
                tag=f"{tag}b{d}_{ch}")


def _host_prep(inputs):
    x = np.ascontiguousarray(inputs["L0_attn_x"])
    ident = np.eye(128, dtype=np.float32)
    ones_row = np.ones((128, 128), np.float32)
    cbank = np.zeros((128, 8), np.float32)
    cbank[:, CB_ONE] = 1.0
    cbank[:, CB_EPS5] = 1e-5
    cbank[:, 2] = np.sqrt(1024.0 / 1023.0)
    cbank[:, CB_I127] = 1.0 / 127.0
    selBC = np.zeros((32, 32 * 128), np.float32)
    for n in range(32):
        selBC[n, 128 * n:128 * (n + 1)] = 1.0

    base = {
        "m_xprojT": np.ascontiguousarray(inputs["L0_attn_x_proj_w"].T),
        "m_dtwT": np.ascontiguousarray(inputs["L0_attn_dt_proj_w"].T),
        "m_dtb": np.ascontiguousarray(inputs["L0_attn_dt_proj_b"].reshape(8, 128).T),
        "m_Alog": np.ascontiguousarray(
            inputs["L0_attn_A_log"].reshape(8, 128, 16).transpose(1, 0, 2).reshape(128, 128)),
        "m_D": np.ascontiguousarray(inputs["L0_attn_D_param"].reshape(8, 128).T),
        "m_lna": np.ascontiguousarray(inputs["L0_attn_ln_a"].reshape(8, 128).T),
        "m_lnb": np.ascontiguousarray(inputs["L0_attn_ln_b"].reshape(8, 128).T),
        "identity": ident,
        "ones_row": ones_row,
        "cbank": cbank,
        "selBC": selBC,
    }
    NEGC = np.float32(-1e30)
    W1 = inputs["L0_ffn_W1"]
    W2 = inputs["L0_ffn_W2"]
    b1 = inputs["L0_ffn_b1"]
    w1p = np.zeros((6, D, 5632), np.float32)
    w1p[:, :, 0:2730] = W1[:, :, 0:2730]
    w1p[:, :, 2816:5546] = W1[:, :, 2730:5460]
    w2p = np.zeros((6, 2816, D), np.float32)
    w2p[:, 0:2730, :] = W2
    b1p = np.zeros((6, 5632), np.float32)
    b1p[:, 0:2730] = b1[:, 0:2730]
    b1p[:, 2816:5546] = b1[:, 2730:5460]
    qs = inputs["L0_ffn_q_scale"]
    ks = inputs["L0_ffn_k_scale"]
    base.update({
        "f_wqkv": np.ascontiguousarray(inputs["L0_ffn_Wqkv"]),
        "f_wo": np.ascontiguousarray(inputs["L0_ffn_Wo"]),
        "f_w1": w1p,
        "f_w2": w2p,
        "f_b1": np.ascontiguousarray(b1p.reshape(6, 44, 128).transpose(0, 2, 1)),
        "f_lng": np.ascontiguousarray(inputs["L0_ffn_ln_g"].reshape(6, 8, 128).transpose(0, 2, 1)),
        "f_lnb2": np.ascontiguousarray(inputs["L0_ffn_ln_b"].reshape(6, 8, 128).transpose(0, 2, 1)),
        "f_b2": np.ascontiguousarray(inputs["L0_ffn_b2"].reshape(6, 8, 128).transpose(0, 2, 1)),
        "f_qs8": np.ascontiguousarray((8.0 * np.concatenate([qs, qs], 1))[:, :, None]),
        "f_ks": np.ascontiguousarray(np.concatenate([ks, ks], 1)[:, :, None]),
    })
    selE2 = np.zeros((128, 16 * 64), np.float32)
    selN = np.zeros((64, 16 * 128), np.float32)
    for p in range(8):
        for side in range(2):
            bidx = p + 8 * side
            for k in range(128):
                selE2[k, 64 * bidx + 32 * side + 2 * p + (k >= 64)] = 1.0
            for m in range(128):
                selN[32 * side + 2 * p + (m >= 64), 128 * bidx + m] = 1.0
    base["f_selE2"] = selE2
    base["f_selN"] = selN
    base["f_l1w"] = np.ascontiguousarray(inputs["L1_attn_w_qkv"])
    base["f_l1proj"] = np.ascontiguousarray(inputs["L1_attn_w_proj"])
    base["f_l1bq"] = np.ascontiguousarray(
        inputs["L1_attn_b_qkv"].reshape(24, 128).T)
    base["f_l1bp"] = np.ascontiguousarray(
        inputs["L1_attn_b_proj"].reshape(8, 128).T)
    tbl = inputs["L1_attn_bias_table"]
    coords = np.stack(np.meshgrid(np.arange(8), np.arange(8), indexing="ij"))
    cf = coords.reshape(2, -1)
    rel = cf[:, :, None] - cf[:, None, :]
    rel = rel.transpose(1, 2, 0).astype(np.int64)
    rel[..., 0] += 7
    rel[..., 1] += 7
    rel[..., 0] *= 15
    idx = rel.sum(-1)[:8, :8]
    l1b = np.full((128, 2048), -1e30, np.float32)
    for h in range(16):
        bias8 = tbl[idx, h]
        for a in range(16):
            l1b[8 * a:8 * a + 8, 128 * h + 8 * a:128 * h + 8 * a + 8] = bias8.T
    base["f_l1bias"] = l1b
    base["f_wgT"] = np.ascontiguousarray(inputs["L1_ffn_w_gate"].T)
    base["f_wuT"] = np.ascontiguousarray(inputs["L1_ffn_w_up"].T)
    base["f_wdT"] = np.ascontiguousarray(inputs["L1_ffn_w_down"].T)
    jj, ii = np.meshgrid(np.arange(128), np.arange(128), indexing="ij")
    mku = np.where(jj >= ii, 0.0, NEGC).astype(np.float32)
    mkl = np.where(jj <= ii, 0.0, NEGC).astype(np.float32)
    allneg = np.full((128, 128), NEGC, np.float32)
    zeros128 = np.zeros((128, 128), np.float32)
    base["f_maskup"] = mku
    base["f_masklo"] = mkl

    def rope_tabs(tglob):
        half = np.arange(0, 64, 2, dtype=np.float64)
        inv_freq = 1.0 / 10000.0 ** (half / 64.0)
        fr = tglob[:, None] * inv_freq[None, :]
        fr = np.concatenate([fr, fr], 1)
        base_scale = (half + 0.4 * 64) / (1.4 * 64)
        power = (tglob - 8192 // 2) / (256 // 2)
        sc = base_scale[None, :] ** power[:, None]
        sc = np.concatenate([sc, sc], 1)
        cs, sn = np.cos(fr), np.sin(fr)
        sgn = np.ones(64); sgn[:32] = -1.0
        Aq = (cs * sc).T
        Bq = (sn * sc * sgn[None, :]).T
        inv = 1.0 / sc
        Ak = (cs * inv).T
        Bk = (sn * inv * sgn[None, :]).T
        dup = lambda M: np.concatenate([M, M], 0).astype(np.float32)
        return dup(Aq), dup(Bq), dup(Ak), dup(Bk)

    in_maps = []
    for c in range(NCORES):
        b, r = divmod(c, 4)
        m = dict(base)
        m["x_tok"] = np.ascontiguousarray(x[b, TC * r:TC * (r + 1), :])
        ccp = np.zeros((128, 4), np.float32)
        ccp[:, :r] = 1.0
        m["ccpre"] = ccp
        ccm = np.zeros((128, 4), np.float32)
        if r > 0:
            ccm[:, r - 1] = 1.0
        m["f_ccm"] = ccm
        tglob = np.arange(TC * r, TC * (r + 1), dtype=np.float64)
        qA, qB, kA, kB = rope_tabs(tglob)
        m["f_ropeqA"], m["f_ropeqB"] = qA, qB
        m["f_ropekA"], m["f_ropekB"] = kA, kB
        th = np.arange(TC * r - 256, TC * r, dtype=np.float64)
        _, _, kAh, kBh = rope_tabs(th)
        m["f_ropekAh"], m["f_ropekBh"] = kAh, kBh
        ccn = np.zeros((128, 4), np.float32)
        ccn[:, (r + 1) % 4] = 1.0
        m["f_ccn"] = ccn
        ccw = np.zeros((128, 4), np.float32)
        ccw[:, (r - 1) % 4] = 1.0
        m["f_ccpw"] = ccw
        m["f_m00"] = allneg if r == 0 else mku
        m["f_m01"] = allneg if r == 0 else zeros128
        m["f_m10"] = allneg if r == 0 else mku
        in_maps.append(m)
    return in_maps


def kernel(**inputs):
    in_maps = _host_prep(inputs)
    nc, stats = build_nc(PHASE_LIMIT)
    res = run_bass_kernel_spmd(nc, in_maps, core_ids=list(range(NCORES)))
    out = np.empty((B, L, D), np.float32)
    for c in range(NCORES):
        b, r = divmod(c, 4)
        out[b, TC * r:TC * (r + 1), :] = res.results[c]["y_tok"]
    return out


def build_l0ffn(kb, groups4, n_layers):
    nc, S = kb.nc, kb.S
    xT, W, P = kb.xT, kb.W, kb.P
    pA, pB = kb.pA, kb.pB
    s_cb, s_cbb, s_ones, s_stats = kb.s_cb, kb.s_cbb, kb.s_ones, kb.s_stats
    s_delta, s_aT = kb.s_delta, kb.s_aT
    sb = kb.sb
    STK = ("stats",)
    DK = lambda c: ("delta", c)
    AK = lambda c: ("aT", c)

    s_v = kb.s_v = sb("s_v", [128, 8 * 1040], BF16)
    s_wbuf = kb.s_wbuf = sb("s_wbuf", [128, 2816], BF16)
    s_zinv = kb.s_zinv = sb("s_zinv", [128, 16])
    l0stack = ExitStack()
    sb = lambda name, shape, dtype=F32: l0stack.enter_context(
        nc.sbuf_tensor(name, shape, dtype))
    s_khalo = sb("s_khalo", [128, 8 * 256], BF16)
    s_kcar = s_khalo
    s_xcol = sb("s_xcol", [128, 8])
    s_rqA = sb("s_rqA", [128, TC], BF16)
    s_rqB = sb("s_rqB", [128, TC], BF16)
    s_rkA = sb("s_rkA", [128, TC], BF16)
    s_rkB = sb("s_rkB", [128, TC], BF16)
    s_rkAh = sb("s_rkAh", [128, 256], BF16)
    s_rkBh = sb("s_rkBh", [128, 256], BF16)
    s_mku = sb("s_mku", [128, 128], BF16)
    s_mkl = sb("s_mkl", [128, 128], BF16)
    s_m00 = sb("s_m00", [128, 128], BF16)
    s_m01 = sb("s_m01", [128, 128], BF16)
    s_m10 = sb("s_m10", [128, 128], BF16)
    s_selE2 = sb("s_selE2", [128, 16 * 64], BF16)
    s_selN = sb("s_selN", [64, 16 * 128], BF16)
    s_ninv = sb("s_ninv", [64, TC], BF16)
    s_b1 = sb("s_b1", [128, 44])
    s_lng = sb("s_lng", [128, 8])
    s_lnbv = sb("s_lnbv", [128, 8])
    s_b2v = sb("s_b2v", [128, 8])
    s_qs8 = sb("s_qs8", [128, 1])
    s_ksc = sb("s_ksc", [128, 1])
    s_ccm = sb("s_ccm", [128, 4])

    ag_kv_in = nc.dram_tensor("ag_kv_in", [1, 8 * 128 * 256 + 128 * 4], BF16)
    ag_kv_out = nc.dram_tensor("ag_kv_out", [4, 8 * 128 * 256 + 128 * 4], BF16)
    ag_x2_in = nc.dram_tensor("ag_x2_in", [1, 128 * 4], F32)
    ag_x2_out = nc.dram_tensor("ag_x2_out", [4, 128 * 4], F32)

    FENCE = [("xT", d, c) for d in range(8) for c in range(4)]
    for nm, dst in [("f_ropeqA", s_rqA), ("f_ropeqB", s_rqB), ("f_ropekA", s_rkA),
                    ("f_ropekB", s_rkB), ("f_ropekAh", s_rkAh), ("f_ropekBh", s_rkBh),
                    ("f_maskup", s_mku), ("f_masklo", s_mkl), ("f_m00", s_m00),
                    ("f_m01", s_m01), ("f_m10", s_m10), ("f_selE2", s_selE2),
                    ("f_selN", s_selN), ("f_ccm", s_ccm)]:
        kb.load("gd", dst[:, :], kb.ap[nm], (nm,), reads=FENCE)
    S.checkpoint("gd")

    for t8 in range(8):
        S.op("v", lambda v, t8=t8: v.memset(
            s_v[:, 1040 * t8:1040 * (t8 + 1)]
            .rearrange("p (h e) -> p h e", e=65)[:, :, 64:65], 1.0),
            reads=FENCE, writes=[("sv", t8)], tag=f"vone{t8}")

    WBH = lambda par: [("wb", 2 * par), ("wb", 2 * par + 1)]

    def wload(tag, dram2d, par):
        kb.load("gd", s_wbuf[:, 1024 * par:1024 * (par + 1)]
                .rearrange("p (k c) -> p k c", k=8),
                dram2d.rearrange("(k p) c -> p k c", p=128),
                WBH(par), tag=tag)

    def ain(d):
        sl = W[d % 4][:, 1024 * (d // 4):1024 * (d // 4) + 1024]
        return sl, [("W", d % 4, 2 * (d // 4) + i) for i in range(2)]

    def qsl(p):
        return W[8 + p // 2][:, 1024 * (p % 2):1024 * (p % 2) + 1024]

    def qkeys(p):
        return [("W", 8 + p // 2, 2 * (p % 2) + i) for i in range(2)]

    def ksl(p):
        return W[12 + p // 2][:, 1024 * (p % 2):1024 * (p % 2) + 1024]

    def kkeys(p):
        return [("W", 12 + p // 2, 2 * (p % 2) + i) for i in range(2)]

    s_halo = W[4]
    HALOK = [("W", 4, c) for c in range(4)]
    otok = W[4]
    OTK = HALOK
    s_attnT = W[5]
    ATK = [("W", 5, c) for c in range(4)]

    def vhalo_ap(t2, h):
        return W[6 + t2][:, 65 * h:65 * h + 65]

    def vhk(t2):
        return [("W", 6 + t2, c) for c in range(3)]

    def emit_ain_half(l, hh, pref):
        c0 = 1024 * hh
        for d in range(8):
            sl, keys = ain(d)
            if d < 4:
                S.op("s", lambda sc, sl=sl, d=d, c0=c0: sc.activation(
                    sl[:, :], xT[d][:, c0:c0 + 1024], AF.Copy),
                    reads=[("xT", d, 2 * hh), ("xT", d, 2 * hh + 1)],
                    writes=keys, tag=f"{pref}ain{d}")
            else:
                S.op("s", lambda sc, sl=sl, d=d, c0=c0: sc.activation(
                    sl[:, 1:1024], xT[d][:, c0:c0 + 1023], AF.Copy),
                    reads=[("xT", d, 2 * hh), ("xT", d, 2 * hh + 1)],
                    writes=keys, tag=f"{pref}ain{d}")
                S.op("v", lambda v, sl=sl, d=d, hh=hh: v.tensor_copy(
                    sl[:, 0:1], s_xcol[:, 4 * hh + d - 4:4 * hh + d - 3]),
                    reads=[("xcol",)], writes=[keys[0]], tag=f"{pref}c0{d}")

    def rot_pair(l, tagp, src, skeys, scc, sck, bcast_ap, bck, tA, tAk, tB, tBk,
                 scr, klo, khi, ncols):
        h = ncols
        S.op("v", lambda v: v.scalar_tensor_tensor(
            scr[:, 0:h], src[:, :], scc[:, 0:1], bcast_ap, ALU.mult, ALU.mult),
            reads=skeys + [sck] + bck, writes=klo, tag=f"{tagp}qn")
        for qq, qs_ in [(0, 32), (32, 0), (64, 96), (96, 64)]:
            S.op("s", lambda sc, qq=qq, qs_=qs_: sc.activation(
                scr[qq:qq + 32, h:2 * h], scr[qs_:qs_ + 32, 0:h], AF.Copy),
                reads=klo, writes=khi, tag=f"{tagp}sw{qq}")
        S.op("v", lambda v: v.tensor_tensor(scr[:, h:2 * h], scr[:, h:2 * h],
                                            tB, ALU.mult),
             reads=khi + [tBk], writes=khi, tag=f"{tagp}rB")
        S.op("v", lambda v: v.tensor_tensor(src[:, :], scr[:, 0:h], tA, ALU.mult),
             reads=klo + [tAk], writes=skeys, tag=f"{tagp}rA")
        S.op("v", lambda v: v.tensor_tensor(src[:, :], src[:, :], scr[:, h:2 * h],
                                            ALU.add),
             reads=skeys + khi, writes=skeys, tag=f"{tagp}rS")

    def qk_prep(l, hh):
        c0 = 1024 * hh
        for p in range(8):
            for side in range(2):
                src = qsl(p) if side == 0 else ksl(p)
                keys = qkeys(p) if side == 0 else kkeys(p)
                scr = otok if (2 * p + side) % 2 == 0 else s_attnT
                sk = OTK if (2 * p + side) % 2 == 0 else ATK
                S.op("s", lambda sc, src=src, scr=scr: sc.activation(
                    scr[:, 0:1024], src[:, :], AF.Square),
                    reads=keys, writes=sk[:2], tag=f"L{l}h{hh}sq{side}_{p}")
                for ch in range(2):
                    S.op("t", lambda t, p=p, side=side, ch=ch, scr=scr: t.matmul(
                        pB[0:64, 512 * ch:512 * (ch + 1)],
                        s_selE2[:, 64 * (p + 8 * side):64 * (p + 8 * side) + 64],
                        scr[:, 512 * ch:512 * ch + 512],
                        start=(p == 0 and side == 0), stop=(p == 7 and side == 1)),
                        reads=[sk[ch], ("f_selE2",)], writes=[("P", 4 + ch)],
                        tag=f"L{l}h{hh}nm{side}_{p}_{ch}")
        S.op("s", lambda sc: sc.activation(s_stats[0:64, c0:c0 + 1024],
                                           pB[0:64, 0:1024], AF.Sqrt),
             reads=[("P", 4), ("P", 5)], writes=[STK], tag=f"L{l}h{hh}nsq")
        S.op("v", lambda v: v.tensor_scalar(s_stats[0:64, c0:c0 + 1024],
                                            s_stats[0:64, c0:c0 + 1024],
                                            1e-12, None, ALU.max),
             reads=[STK], writes=[STK], tag=f"L{l}h{hh}nmx")
        S.op("v", lambda v: v.reciprocal(s_stats[0:64, c0:c0 + 1024],
                                         s_stats[0:64, c0:c0 + 1024]),
             reads=[STK], writes=[STK], tag=f"L{l}h{hh}nrc")
        S.op("v", lambda v: v.tensor_copy(s_ninv[:, c0:c0 + 1024],
                                          s_stats[0:64, c0:c0 + 1024]),
             reads=[STK], writes=[("ninv",)], tag=f"L{l}h{hh}ncp")
        for p in range(8):
            for side in range(2):
                idx = 2 * p + side
                src = qsl(p) if side == 0 else ksl(p)
                keys = qkeys(p) if side == 0 else kkeys(p)
                tA = (s_rqA if side == 0 else s_rkA)[:, c0:c0 + 1024]
                tB = (s_rqB if side == 0 else s_rkB)[:, c0:c0 + 1024]
                tAk = ("f_ropeqA",) if side == 0 else ("f_ropekA",)
                tBk = ("f_ropeqB",) if side == 0 else ("f_ropekB",)
                scc = s_qs8 if side == 0 else s_ksc
                sck = "lv_qs" if side == 0 else "lv_ks"
                scr = otok if idx % 2 == 0 else s_attnT
                scrk = OTK if idx % 2 == 0 else ATK
                pbk = 2 * (idx % 2)
                S.op("t", lambda t, p=p, side=side, pbk=pbk: t.matmul(
                    pB[:, 512 * pbk:512 * pbk + 512],
                    s_selN[:, 128 * (p + 8 * side):128 * (p + 8 * side) + 128],
                    s_ninv[:, c0:c0 + 512], start=True, stop=True),
                    reads=[("ninv",), ("f_selN",)], writes=[("P", 4 + pbk)],
                    tag=f"L{l}h{hh}nb{idx}_0")
                S.op("t", lambda t, p=p, side=side, pbk=pbk: t.matmul(
                    pB[:, 512 * (pbk + 1):512 * (pbk + 1) + 512],
                    s_selN[:, 128 * (p + 8 * side):128 * (p + 8 * side) + 128],
                    s_ninv[:, c0 + 512:c0 + 1024], start=True, stop=True),
                    reads=[("ninv",), ("f_selN",)], writes=[("P", 4 + pbk + 1)],
                    tag=f"L{l}h{hh}nb{idx}_1")
                rot_pair(l, f"L{l}h{hh}p{idx}", src, keys, scc, sck,
                         pB[:, 512 * pbk:512 * pbk + 1024],
                         [("P", 4 + pbk), ("P", 5 + pbk)], tA, tAk, tB, tBk,
                         scr, scrk[:2], scrk[2:], 1024)

    def halo_k_prep(l):
        for p in range(8):
            S.op("s", lambda sc, p=p: sc.activation(
                s_attnT[:, 0:256], s_khalo[:, 256 * p:256 * (p + 1)], AF.Square),
                reads=[("khalo",)], writes=[ATK[0]], tag=f"L{l}hsq{p}")
            S.op("t", lambda t, p=p: t.matmul(
                pB[0:64, 0:256],
                s_selE2[:, 64 * (p + 8):64 * (p + 8) + 64],
                s_attnT[:, 0:256], start=(p == 0), stop=(p == 7)),
                reads=[ATK[0], ("f_selE2",)], writes=[("P", 4)],
                tag=f"L{l}hnm{p}")
        S.op("s", lambda sc: sc.activation(s_stats[0:64, 0:256],
                                           pB[0:64, 0:256], AF.Sqrt),
             reads=[("P", 4)], writes=[STK], tag=f"L{l}hnsq")
        S.op("v", lambda v: v.tensor_scalar(s_stats[0:64, 0:256],
                                            s_stats[0:64, 0:256], 1e-12, None,
                                            ALU.max),
             reads=[STK], writes=[STK], tag=f"L{l}hnmx")
        S.op("v", lambda v: v.reciprocal(s_stats[0:64, 0:256], s_stats[0:64, 0:256]),
             reads=[STK], writes=[STK], tag=f"L{l}hnrc")
        S.op("v", lambda v: v.tensor_copy(s_ninv[:, 0:256], s_stats[0:64, 0:256]),
             reads=[STK], writes=[("ninvh",)], tag=f"L{l}hncp")
        for p in range(8):
            S.op("t", lambda t, p=p: t.matmul(
                pB[:, 1024:1280],
                s_selN[:, 128 * (p + 8):128 * (p + 8) + 128],
                s_ninv[:, 0:256], start=True, stop=True),
                reads=[("ninvh",), ("f_selN",)], writes=[("P", 6)],
                tag=f"L{l}hnb{p}")
            rot_pair(l, f"L{l}hp{p}",
                     s_khalo[:, 256 * p:256 * (p + 1)], [("khalo",)],
                     s_ksc, "lv_ks", pB[:, 1024:1280], [("P", 6)],
                     s_rkAh[:, :], ("f_ropekAh",), s_rkBh[:, :], ("f_ropekBh",),
                     s_attnT, [ATK[0]], [ATK[0]], 256)

    for l in range(n_layers):
        Lw = kb.ap["f_wqkv"][l]
        Lo = kb.ap["f_wo"][l]
        L1w = kb.ap["f_w1"][l]
        L2w = kb.ap["f_w2"][l]
        for nm, dst, key in [("f_b1", s_b1, "lv_b1"), ("f_lng", s_lng, "lv_g"),
                             ("f_lnb2", s_lnbv, "lv_b"), ("f_b2", s_b2v, "lv_b2"),
                             ("f_qs8", s_qs8, "lv_qs"), ("f_ks", s_ksc, "lv_ks")]:
            kb.load("gd", dst[:, :], kb.ap[nm][l], key,
                    reads=FENCE if l == 0 else [], tag=f"L{l}{key}")

        for d in range(8):
            if d < 4:
                S.op("s", lambda sc, d=d: sc.activation(
                    s_khalo[:, 256 * d:256 * (d + 1)], xT[d][:, 1792:2048], AF.Copy),
                    reads=[("xT", d, 3)], writes=[("khalo",)], tag=f"L{l}tl{d}")
            else:
                S.op("s", lambda sc, d=d: sc.activation(
                    s_khalo[:, 256 * d:256 * (d + 1)], xT[d][:, 1791:2047], AF.Copy),
                    reads=[("xT", d, 3)], writes=[("khalo",)], tag=f"L{l}tl{d}")
        for i, d in enumerate(range(4, 8)):
            S.op("s", lambda sc, d=d, i=i: sc.activation(
                s_attnT[:, i:i + 1], xT[d][:, TC - 1:TC], AF.Copy),
                reads=[("xT", d, 3)], writes=[ATK[0]], tag=f"L{l}xc{i}")
            S.op("v", lambda v, d=d, i=i: v.tensor_copy(
                s_xcol[:, 4 + i:5 + i], xT[d][:, 1023:1024]),
                reads=[("xT", d, 1)], writes=[("xcol",)], tag=f"L{l}xm{i}")
        kb.store("gd", ag_kv_in.ap()[0, 0:8 * 128 * 256]
                 .rearrange("(p c) -> p c", p=128),
                 s_khalo[:, :], reads=[("khalo",)], key=("agkv",), tag=f"L{l}agst")
        kb.store("gd", ag_kv_in.ap()[0, 8 * 128 * 256:]
                 .rearrange("(p c) -> p c", p=128),
                 s_attnT[:, 0:4], reads=[ATK[0]], key=("agkvx",), tag=f"L{l}agstx")
        S.checkpoint("gd")
        S.cc(lambda g: g.collective_compute(
            "AllGather", ALU.bypass, replica_groups=groups4,
            ins=[ag_kv_in.ap()[:, :]], outs=[ag_kv_out.ap()[:, :]]),
            reads=[("agkv",), ("agkvx",)], writes=["ag_kv_out"], tag=f"L{l}agkv")
        S.op("v", lambda v: v.memset(s_halo[:, :], 0.0), writes=HALOK, tag=f"L{l}hz")
        S.op("v", lambda v: v.memset(s_xcol[:, 0:4], 0.0), writes=[("xcol",)],
             tag=f"L{l}xz")
        for j in range(4):
            kb.load("gd", s_attnT[:, :],
                    ag_kv_out.ap()[j, 0:8 * 128 * 256]
                    .rearrange("(p c) -> p c", p=128),
                    list(ATK), reads=["ag_kv_out"], tag=f"L{l}agld{j}")
            kb.load("gd", s_khalo[:, 2044:2048],
                    ag_kv_out.ap()[j, 8 * 128 * 256:]
                    .rearrange("(p c) -> p c", p=128),
                    ("khalo",), reads=["ag_kv_out"], tag=f"L{l}agldx{j}")
            S.op("v", lambda v, j=j: v.scalar_tensor_tensor(
                s_halo[:, :], s_attnT[:, :], s_ccm[:, j:j + 1], s_halo[:, :],
                ALU.mult, ALU.add),
                reads=list(ATK) + [("f_ccm",)] + HALOK, writes=HALOK,
                tag=f"L{l}hsel{j}")
            S.op("v", lambda v, j=j: v.scalar_tensor_tensor(
                s_xcol[:, 0:4], s_khalo[:, 2044:2048], s_ccm[:, j:j + 1],
                s_xcol[:, 0:4], ALU.mult, ALU.add),
                reads=[("khalo",), ("f_ccm",), ("xcol",)], writes=[("xcol",)],
                tag=f"L{l}xsel{j}")

        for p in range(8):
            par = p % 2
            wload(f"L{l}hkw{p}", Lw[:, 1024 + 128 * p:1024 + 128 * (p + 1)], par)
            fns = [lambda t, k=k, par=par: t.matmul(
                pB[:, 0:256],
                s_wbuf[:, 1024 * par + 128 * k:1024 * par + 128 * (k + 1)],
                s_halo[:, 256 * k:256 * (k + 1)],
                start=(k == 0), stop=(k == 7)) for k in range(8)]
            S.group("t", fns, reads=WBH(par) + HALOK, writes=[("P", 4)],
                    tag=f"L{l}hk{p}")
            S.op("s", lambda sc, p=p: sc.activation(
                s_khalo[:, 256 * p:256 * (p + 1)], pB[:, 0:256], AF.Copy),
                reads=[("P", 4)], writes=[("khalo",)], tag=f"L{l}hke{p}")
        for t2 in range(2):
            S.op("v", lambda v, t2=t2: v.memset(
                W[6 + t2][:, 0:1040].rearrange("p (h e) -> p h e", e=65)[:, :, 64:65],
                1.0),
                writes=vhk(t2), tag=f"L{l}vhone{t2}")
        def vw(k):
            if k < 4:
                return s_wbuf[:, 512 * k:512 * (k + 1)], ("wb", k)
            return W[5][:, 512 * (k - 4):512 * (k - 4) + 512], ATK[k - 4]
        for fc in range(2):
            for k in range(8):
                dst, key = vw(k)
                kb.load("gd", dst,
                        Lw[128 * k:128 * (k + 1),
                           2048 + 512 * fc:2048 + 512 * (fc + 1)],
                        key, tag=f"L{l}hvw{fc}_{k}")
            wrd = [vw(k)[1] for k in range(8)]
            for t2 in range(2):
                fns = [lambda t, t2=t2, k=k: t.matmul(
                    pB[:, 512 * (1 + t2):512 * (2 + t2)],
                    s_halo[:, 256 * k + 128 * t2:256 * k + 128 * t2 + 128],
                    vw(k)[0], start=(k == 0), stop=(k == 7))
                    for k in range(8)]
                S.group("t", fns, reads=wrd + HALOK,
                        writes=[("P", 5 + t2)], tag=f"L{l}hv{fc}_{t2}")
                S.op("s", lambda sc, t2=t2, fc=fc: sc.activation(
                    W[6 + t2][:, 520 * fc:520 * fc + 520]
                    .rearrange("p (h e) -> p h e", e=65)[:, :, 0:64],
                    pB[:, 512 * (1 + t2):512 * (2 + t2)]
                    .rearrange("p (h e) -> p h e", e=64), AF.Copy),
                    reads=[("P", 5 + t2)], writes=vhk(t2),
                    tag=f"L{l}hve{fc}_{t2}")
        halo_k_prep(l)

        for hh in range(2):
            c0 = 1024 * hh
            emit_ain_half(l, hh, f"L{l}h{hh}")
            for p in range(8):
                for side in range(2):
                    par = p % 2
                    wcol = (0 if side == 0 else 1024) + 128 * p
                    wload(f"L{l}h{hh}w{side}_{p}", Lw[:, wcol:wcol + 128], par)
                    dstsl = qsl(p) if side == 0 else ksl(p)
                    dkeys = qkeys(p) if side == 0 else kkeys(p)
                    for ch in range(2):
                        fns = []
                        rds = WBH(par)
                        for k in range(8):
                            asl, akeys = ain(k)
                            rds.append(akeys[ch])
                            fns.append(lambda t, k=k, par=par, ch=ch, asl=asl: t.matmul(
                                pA[:, 512 * ch:512 * (ch + 1)],
                                s_wbuf[:, 1024 * par + 128 * k:1024 * par + 128 * (k + 1)],
                                asl[:, 512 * ch:512 * ch + 512],
                                start=(k == 0), stop=(k == 7)))
                        S.group("t", fns, reads=rds, writes=[("P", ch)],
                                tag=f"L{l}h{hh}qk{side}_{p}_{ch}")
                        S.op("s", lambda sc, dstsl=dstsl, ch=ch: sc.activation(
                            dstsl[:, 512 * ch:512 * ch + 512],
                            pA[:, 512 * ch:512 * (ch + 1)], AF.Copy),
                            reads=[("P", ch)], writes=[dkeys[ch]],
                            tag=f"L{l}h{hh}qke{side}_{p}_{ch}")
            def vw(k):
                if k < 4:
                    return s_wbuf[:, 512 * k:512 * (k + 1)], ("wb", k)
                return W[5][:, 512 * (k - 4):512 * (k - 4) + 512], ATK[k - 4]
            for fc in range(2):
                for k in range(8):
                    dst, key = vw(k)
                    kb.load("gd", dst,
                            Lw[128 * k:128 * (k + 1),
                               2048 + 512 * fc:2048 + 512 * (fc + 1)],
                            key, tag=f"L{l}h{hh}vw{fc}_{k}")
                wrd = [vw(k)[1] for k in range(8)]
                for t8 in range(8):
                    fns = []
                    rds = list(wrd)
                    for k in range(8):
                        asl, akeys = ain(k)
                        rds.append(akeys[(128 * t8) // 512])
                        fns.append(lambda t, t8=t8, k=k, asl=asl: t.matmul(
                            pA[:, 512 * (t8 % 4):512 * (t8 % 4) + 512],
                            asl[:, 128 * t8:128 * t8 + 128],
                            vw(k)[0], start=(k == 0), stop=(k == 7)))
                    S.group("t", fns, reads=rds, writes=[("P", t8 % 4)],
                            tag=f"L{l}h{hh}v{fc}_{t8}")
                    S.op("s", lambda sc, t8=t8, fc=fc: sc.activation(
                        s_v[:, 1040 * t8 + 520 * fc:1040 * t8 + 520 * fc + 520]
                        .rearrange("p (h e) -> p h e", e=65)[:, :, 0:64],
                        pA[:, 512 * (t8 % 4):512 * (t8 % 4) + 512]
                        .rearrange("p (h e) -> p h e", e=64), AF.Copy),
                        reads=[("P", t8 % 4)], writes=[("sv", t8)],
                        tag=f"L{l}h{hh}ve{fc}_{t8}")
            qk_prep(l, hh)
            for sl_ in range(8):
                g = 8 * hh + sl_
                for hgp in range(4):
                    for hi in range(4):
                        h = 4 * hgp + hi
                        p, hp = h // 2, h % 2
                        qap = qsl(p)[64 * hp:64 * hp + 64, 128 * sl_:128 * sl_ + 128]
                        pc = 4 + 2 * (sl_ % 2) + (h % 2)
                        psim = pB[:, 512 * (pc - 4):512 * (pc - 4) + 384]
                        fns = []
                        rds = list(qkeys(p))
                        for jt in range(3):
                            gj = g - 2 + jt
                            if gj < 0:
                                kap = s_khalo[64 * hp:64 * hp + 64,
                                              256 * p + 128 * (gj + 2):
                                              256 * p + 128 * (gj + 2) + 128]
                                rds.append(("khalo",))
                            elif gj < 8 * hh:
                                kap = s_kcar[64 * hp:64 * hp + 64,
                                             256 * p + 128 * (gj - 6):
                                             256 * p + 128 * (gj - 6) + 128]
                                rds.append(("khalo",))
                            else:
                                lt = gj - 8 * hh
                                kap = ksl(p)[64 * hp:64 * hp + 64,
                                             128 * lt:128 * lt + 128]
                                rds += kkeys(p)
                            fns.append(lambda t, kap=kap, qap=qap, jt=jt, psim=psim:
                                       t.matmul(psim[:, 128 * jt:128 * (jt + 1)],
                                                kap, qap, start=(jt == 0), stop=False))
                        if g == 0:
                            mk = [(0, s_m00, ("f_m00",)), (1, s_m01, ("f_m01",)),
                                  (2, s_mkl, ("f_masklo",))]
                        elif g == 1:
                            mk = [(0, s_m10, ("f_m10",)), (2, s_mkl, ("f_masklo",))]
                        else:
                            mk = [(0, s_mku, ("f_maskup",)), (2, s_mkl, ("f_masklo",))]
                        for idx, (jt, mt, mkey) in enumerate(mk):
                            last = idx == len(mk) - 1
                            fns.append(lambda t, jt=jt, mt=mt, psim=psim, last=last:
                                       t.matmul(psim[:, 128 * jt:128 * (jt + 1)],
                                                kb.s_idb[:, :], mt[:, :],
                                                start=False, stop=last))
                            rds.append(mkey)
                        rds.append("s_idb")
                        S.group("t", fns, reads=rds, writes=[("P", pc)],
                                tag=f"L{l}g{g}h{h}sim")
                        ab = h % 2
                        S.op("s", lambda sc, psim=psim, ab=ab: sc.activation(
                            s_attnT[:, 512 * ab:512 * ab + 384], psim[:, :], AF.Exp),
                            reads=[("P", pc)], writes=[ATK[ab]],
                            tag=f"L{l}g{g}h{h}exp")
                        ofs = 128 * h
                        fns2 = []
                        rds2 = [ATK[ab]]
                        for jt in range(3):
                            gj = g - 2 + jt
                            if gj < 0:
                                vap = vhalo_ap(gj + 2, h)
                                rds2 += vhk(gj + 2)
                            elif gj < 8 * hh:
                                vap = vhalo_ap(gj - 6, h)
                                rds2 += vhk(gj - 6)
                            else:
                                lt = gj - 8 * hh
                                vap = s_v[:, 1040 * lt + 65 * h:1040 * lt + 65 * h + 65]
                                rds2.append(("sv", lt))
                            st = (jt == 0) and (hi == 0)
                            sp = (jt == 2) and (hi == 3)
                            fns2.append(lambda t, vap=vap, jt=jt, ab=ab, ofs=ofs,
                                        st=st, sp=sp:
                                        t.matmul(pA[:, ofs:ofs + 65],
                                                 s_attnT[:, 512 * ab + 128 * jt:
                                                         512 * ab + 128 * (jt + 1)],
                                                 vap, start=st, stop=sp))
                        S.group("t", fns2, reads=rds2, writes=[("P", hgp)],
                                tag=f"L{l}g{g}h{h}out")
                    S.op("v", lambda v, hgp=hgp: v.reciprocal(
                        s_zinv[:, 4 * hgp:4 * hgp + 4],
                        pA[:, 512 * hgp:512 * (hgp + 1)]
                        .rearrange("p (h e) -> p h e", e=128)[:, :, 64]),
                        reads=[("P", hgp)], writes=[("zinv",)],
                        tag=f"L{l}g{g}zi{hgp}")
                    S.op("v", lambda v, hgp=hgp: v.tensor_tensor(
                        otok[:, :].bitcast(F32)[:, 256 * hgp:256 * (hgp + 1)]
                        .rearrange("p (h e) -> p h e", e=64),
                        pA[:, 512 * hgp:512 * (hgp + 1)]
                        .rearrange("p (h e) -> p h e", e=128)[:, :, 0:64],
                        s_zinv[:, 4 * hgp:4 * hgp + 4].unsqueeze(2)
                        .broadcast_to([128, 4, 64]), ALU.mult),
                        reads=[("P", hgp), ("zinv",)], writes=[OTK[hgp]],
                        tag=f"L{l}g{g}ev{hgp}")
                for j8 in range(8):
                    tb_ = 4 + j8 % 2
                    S.op("t", lambda t, j8=j8, tb_=tb_: t.transpose(
                        pB[:, 512 * (tb_ - 4):512 * (tb_ - 4) + 128],
                        otok[:, :].bitcast(F32)[:, 128 * j8:128 * (j8 + 1)],
                        kb.s_id[:, :]),
                        reads=[OTK[j8 // 2], "s_id"], writes=[("P", tb_)],
                        tag=f"L{l}g{g}tr{j8}")
                    base = s_delta if j8 < 4 else s_aT
                    jj = j8 % 4
                    dstap = base[:, :].bitcast(BF16)[:, 1024 * jj + 128 * sl_:
                                                     1024 * jj + 128 * sl_ + 128]
                    S.op("s", lambda sc, dstap=dstap, tb_=tb_: sc.activation(
                        dstap, pB[:, 512 * (tb_ - 4):512 * (tb_ - 4) + 128], AF.Copy),
                        reads=[("P", tb_)], writes=[DK(jj) if j8 < 4 else AK(jj)],
                        tag=f"L{l}g{g}tre{j8}")
            if hh == 0:
                for p in range(8):
                    S.op("v", lambda v, p=p: v.tensor_copy(
                        s_kcar[:, 256 * p:256 * (p + 1)], ksl(p)[:, 768:1024]),
                        reads=kkeys(p), writes=[("khalo",)], tag=f"L{l}kc{p}")
                for t2 in range(2):
                    S.op("v", lambda v, t2=t2: v.tensor_copy(
                        W[6 + t2][:, 0:1040],
                        s_v[:, (6 + t2) * 1040:(7 + t2) * 1040]),
                        reads=[("sv", 6 + t2)], writes=vhk(t2), tag=f"L{l}vc{t2}")
            for m8 in range(8):
                wload(f"L{l}h{hh}wo{m8}", Lo[:, 128 * m8:128 * (m8 + 1)], m8 % 2)
                for ch in range(2):
                    fns = []
                    for k in range(8):
                        bt = s_delta if k < 4 else s_aT
                        jj = k % 4
                        rhs = bt[:, :].bitcast(BF16)[:, 1024 * jj + 512 * ch:
                                                     1024 * jj + 512 * ch + 512]
                        fns.append(lambda t, k=k, m8=m8, ch=ch, rhs=rhs: t.matmul(
                            pA[:, 1024 + 512 * ch:1024 + 512 * (ch + 1)],
                            s_wbuf[:, 1024 * (m8 % 2) + 128 * k:
                                   1024 * (m8 % 2) + 128 * (k + 1)],
                            rhs, start=(k == 0), stop=(k == 7)))
                    S.group("t", fns,
                            reads=WBH(m8 % 2) + [DK(c) for c in range(4)]
                            + [AK(c) for c in range(4)],
                            writes=[("P", 2 + ch)], tag=f"L{l}h{hh}wo{m8}_{ch}")
                    S.op("v", lambda v, m8=m8, ch=ch, c0=c0: v.tensor_tensor(
                        xT[m8][:, c0 + 512 * ch:c0 + 512 * ch + 512],
                        xT[m8][:, c0 + 512 * ch:c0 + 512 * ch + 512],
                        pA[:, 1024 + 512 * ch:1024 + 512 * (ch + 1)], ALU.add),
                        reads=[("xT", m8, 2 * hh + ch), ("P", 2 + ch)],
                        writes=[("xT", m8, 2 * hh + ch)], tag=f"L{l}h{hh}wor{m8}_{ch}")

        for i, d in enumerate(range(4, 8)):
            S.op("v", lambda v, d=d, i=i: v.tensor_copy(
                s_xcol[:, i:i + 1], xT[d][:, TC - 1:TC]),
                reads=[("xT", d, 3)], writes=[("xcol",)], tag=f"L{l}x2c{i}")
            S.op("v", lambda v, d=d, i=i: v.tensor_copy(
                s_xcol[:, 4 + i:5 + i], xT[d][:, 1023:1024]),
                reads=[("xT", d, 1)], writes=[("xcol",)], tag=f"L{l}x2m{i}")
        kb.store("gd", ag_x2_in.ap()[0, :].rearrange("(p c) -> p c", p=128),
                 s_xcol[:, 0:4], reads=[("xcol",)], key=("agx2",), tag=f"L{l}agx2st")
        S.checkpoint("gd")
        S.cc(lambda g: g.collective_compute(
            "AllGather", ALU.bypass, replica_groups=groups4,
            ins=[ag_x2_in.ap()[:, :]], outs=[ag_x2_out.ap()[:, :]]),
            reads=[("agx2",)], writes=["ag_x2_out"], tag=f"L{l}agx2")
        S.op("v", lambda v: v.memset(s_xcol[:, 0:4], 0.0), writes=[("xcol",)],
             tag=f"L{l}x2z")
        for j in range(4):
            kb.load("gd", s_attnT[:, :].bitcast(F32)[:, 0:4],
                    ag_x2_out.ap()[j, :].rearrange("(p c) -> p c", p=128),
                    ATK[0], reads=["ag_x2_out"], tag=f"L{l}agx2ld{j}")
            S.op("v", lambda v, j=j: v.scalar_tensor_tensor(
                s_xcol[:, 0:4], s_attnT[:, :].bitcast(F32)[:, 0:4],
                s_ccm[:, j:j + 1], s_xcol[:, 0:4], ALU.mult, ALU.add),
                reads=[ATK[0], ("f_ccm",), ("xcol",)], writes=[("xcol",)],
                tag=f"L{l}x2sel{j}")

        build_ffn(kb, l, L1w, L2w, s_b1, s_lng, s_lnbv, s_b2v, s_wbuf,
                  emit_ain_half, ain, WBH)
    l0stack.close()


def build_ffn(kb, l, L1w, L2w, s_b1, s_lng, s_lnbv, s_b2v, s_wbuf,
              emit_ain_half, ain, WBH):
    nc, S = kb.nc, kb.S
    xT, W, P = kb.xT, kb.W, kb.P
    pA, pB = kb.pA, kb.pB
    s_cb, s_cbb, s_ones, s_stats = kb.s_cb, kb.s_cbb, kb.s_ones, kb.s_stats
    s_delta, s_aT = kb.s_delta, kb.s_aT
    STK = ("stats",)
    DK = lambda c: ("delta", c)
    AK = lambda c: ("aT", c)
    otok = W[4]
    OTK = [("W", 4, c) for c in range(4)]

    def hga(i):
        if i < 16:
            sl = W[8 + i // 2][:, 1024 * (i % 2):1024 * (i % 2) + 1024]
            keys = [("W", 8 + i // 2, 2 * (i % 2) + j) for j in range(2)]
        elif i < 20:
            jj = i - 16
            sl = s_delta[:, :].bitcast(BF16)[:, 1024 * jj:1024 * (jj + 1)]
            keys = [DK(jj)]
        else:
            jj = i - 20
            sl = s_aT[:, :].bitcast(BF16)[:, 1024 * jj:1024 * (jj + 1)]
            keys = [AK(jj)]
        return sl, keys

    for hh in range(2):
        c0 = 1024 * hh
        emit_ain_half(l, hh, f"F{l}h{hh}")
        for ch in range(2):
            cc = 512 * ch
            fns = []
            rds = ["s_cbb"]
            for d in range(8):
                asl, akeys = ain(d)
                rds.append(akeys[ch])
                fns.append(lambda t, asl=asl, cc=cc, d=d: t.matmul(
                    P[0][0:1, 0:512], kb.s_cbb[:, 0:1], asl[:, cc:cc + 512],
                    start=(d == 0), stop=(d == 7)))
            S.group("t", fns, reads=rds, writes=[("P", 0)], tag=f"F{l}h{hh}sum{ch}")
            for d in range(8):
                asl, akeys = ain(d)
                S.op("s", lambda sc, asl=asl, cc=cc: sc.activation(
                    otok[:, 0:512], asl[:, cc:cc + 512], AF.Square),
                    reads=[akeys[ch]], writes=[OTK[0]], tag=f"F{l}h{hh}sq{ch}_{d}")
                S.op("t", lambda t, d=d: t.matmul(
                    P[1][0:1, 0:512], kb.s_cbb[:, 0:1], otok[:, 0:512],
                    start=(d == 0), stop=(d == 7)),
                    reads=[OTK[0], "s_cbb"], writes=[("P", 1)],
                    tag=f"F{l}h{hh}sq2{ch}_{d}")
            S.op("s", lambda sc, cc=cc, c0=c0: sc.activation(
                s_stats[32:33, c0 + cc:c0 + cc + 512], P[0][0:1, 0:512], AF.Copy),
                reads=[("P", 0)], writes=[STK], tag=f"F{l}h{hh}se{ch}")
            S.op("s", lambda sc, cc=cc, c0=c0: sc.activation(
                s_stats[64:65, c0 + cc:c0 + cc + 512], P[1][0:1, 0:512], AF.Copy),
                reads=[("P", 1)], writes=[STK], tag=f"F{l}h{hh}se2{ch}")
        S.op("v", lambda v, c0=c0: v.tensor_scalar(s_stats[32:33, c0:c0 + 1024],
                                            s_stats[32:33, c0:c0 + 1024],
                                            1.0 / 1024.0, None, ALU.mult),
             reads=[STK], writes=[STK], tag=f"F{l}h{hh}mu")
        for ch in range(2):
            cc = c0 + 512 * ch
            S.op("s", lambda sc, cc=cc: sc.activation(
                P[0][0:1, 0:512], s_stats[32:33, cc:cc + 512], AF.Square),
                reads=[STK], writes=[("P", 0)], tag=f"F{l}h{hh}mu2_{ch}")
            S.op("v", lambda v, cc=cc: v.tensor_scalar(
                s_stats[64:65, cc:cc + 512], s_stats[64:65, cc:cc + 512],
                1.0 / 1024.0, None, ALU.mult),
                reads=[STK], writes=[STK], tag=f"F{l}h{hh}ex2_{ch}")
            S.op("v", lambda v, cc=cc: v.tensor_tensor(
                s_stats[64:65, cc:cc + 512], s_stats[64:65, cc:cc + 512],
                P[0][0:1, 0:512], ALU.subtract),
                reads=[STK, ("P", 0)], writes=[STK], tag=f"F{l}h{hh}var_{ch}")
        S.op("s", lambda sc, c0=c0: sc.activation(s_stats[64:65, c0:c0 + 1024],
                                           s_stats[64:65, c0:c0 + 1024], AF.Sqrt,
                                           bias=s_cb[0:1, CB_EPS5:CB_EPS5 + 1]),
             reads=[STK, "s_cb"], writes=[STK], tag=f"F{l}h{hh}sd")
        S.op("v", lambda v, c0=c0: v.reciprocal(s_stats[64:65, c0:c0 + 1024],
                                         s_stats[64:65, c0:c0 + 1024]),
             reads=[STK], writes=[STK], tag=f"F{l}h{hh}rs")
        for ch in range(2):
            cc = c0 + 512 * ch
            S.op("t", lambda t, cc=cc, ch=ch: t.matmul(
                pA[:, 512 * ch:512 * (ch + 1)], s_ones[32:33, :],
                s_stats[32:33, cc:cc + 512], start=True, stop=True),
                reads=[STK, "s_ones"], writes=[("P", ch)], tag=f"F{l}h{hh}bm{ch}")
            S.op("t", lambda t, cc=cc, ch=ch: t.matmul(
                pA[:, 1024 + 512 * ch:1024 + 512 * (ch + 1)], s_ones[64:65, :],
                s_stats[64:65, cc:cc + 512], start=True, stop=True),
                reads=[STK, "s_ones"], writes=[("P", 2 + ch)], tag=f"F{l}h{hh}br{ch}")
        for d in range(8):
            asl, akeys = ain(d)
            for ch in range(2):
                cc = 512 * ch
                S.op("v", lambda v, asl=asl, cc=cc, ch=ch: v.tensor_tensor(
                    asl[:, cc:cc + 512], asl[:, cc:cc + 512],
                    pA[:, 512 * ch:512 * (ch + 1)], ALU.subtract),
                    reads=[akeys[ch], ("P", ch)], writes=[akeys[ch]],
                    tag=f"F{l}h{hh}a1{d}_{ch}")
                S.op("v", lambda v, asl=asl, cc=cc, ch=ch, d=d: v.scalar_tensor_tensor(
                    asl[:, cc:cc + 512], asl[:, cc:cc + 512], s_lng[:, d:d + 1],
                    pA[:, 1024 + 512 * ch:1024 + 512 * (ch + 1)], ALU.mult, ALU.mult),
                    reads=[akeys[ch], "lv_g", ("P", 2 + ch)], writes=[akeys[ch]],
                    tag=f"F{l}h{hh}a2{d}_{ch}")
                S.op("v", lambda v, asl=asl, cc=cc, d=d: v.tensor_scalar(
                    asl[:, cc:cc + 512], asl[:, cc:cc + 512],
                    s_lnbv[:, d:d + 1], None, ALU.add),
                    reads=[akeys[ch], "lv_b"], writes=[akeys[ch]],
                    tag=f"F{l}h{hh}a3{d}_{ch}")
        for fp in range(22):
            for part, foff in [(0, fp), (1, fp + 22)]:
                kb.load("gd", s_wbuf[:, 1024 * part:1024 * (part + 1)]
                        .rearrange("p (k c) -> p k c", k=8),
                        L1w[:, 128 * foff:128 * (foff + 1)]
                        .rearrange("(k p) c -> p k c", p=128),
                        WBH(part), tag=f"F{l}h{hh}w1{fp}_{part}")
            for ch in range(2):
                for part in range(2):
                    fns = []
                    rds = WBH(part)
                    for k in range(8):
                        asl, akeys = ain(k)
                        rds.append(akeys[ch])
                        fns.append(lambda t, k=k, part=part, ch=ch, asl=asl: t.matmul(
                            pB[:, 512 * (2 * ch + part):512 * (2 * ch + part) + 512],
                            s_wbuf[:, 1024 * part + 128 * k:1024 * part + 128 * (k + 1)],
                            asl[:, 512 * ch:512 * ch + 512],
                            start=(k == 0), stop=(k == 7)))
                    S.group("t", fns, reads=rds, writes=[("P", 4 + 2 * ch + part)],
                            tag=f"F{l}h{hh}w1m{fp}_{ch}_{part}")
                S.op("s", lambda sc, fp=fp, ch=ch: sc.activation(
                    otok[:, 1536:2048], pB[:, 1024 * ch + 512:1024 * ch + 1024],
                    AF.Gelu, bias=s_b1[:, 22 + fp:23 + fp]),
                    reads=[("P", 4 + 2 * ch + 1), "lv_b1"], writes=[OTK[3]],
                    tag=f"F{l}h{hh}ge{fp}_{ch}")
                sl, keys = hga(fp)
                S.op("v", lambda v, sl=sl, ch=ch, fp=fp: v.scalar_tensor_tensor(
                    sl[:, 512 * ch:512 * ch + 512],
                    pB[:, 1024 * ch:1024 * ch + 512], s_b1[:, fp:fp + 1],
                    otok[:, 1536:2048], ALU.add, ALU.mult),
                    reads=[("P", 4 + 2 * ch), "lv_b1", OTK[3]],
                    writes=[keys[min(ch, len(keys) - 1)]],
                    tag=f"F{l}h{hh}hg{fp}_{ch}")
        for m8 in range(8):
            kb.load("gd", s_wbuf[:, 0:2816].rearrange("p (k c) -> p k c", k=22),
                    L2w[:, 128 * m8:128 * (m8 + 1)]
                    .rearrange("(k p) c -> p k c", p=128),
                    [("wb", q) for q in range(6)], tag=f"F{l}h{hh}w2{m8}")
            for ch in range(2):
                fns = []
                rds = [("wb", q) for q in range(6)]
                for k in range(22):
                    sl, keys = hga(k)
                    rds += keys
                    fns.append(lambda t, k=k, sl=sl, ch=ch: t.matmul(
                        pA[:, 512 * ch:512 * (ch + 1)],
                        s_wbuf[:, 128 * k:128 * (k + 1)],
                        sl[:, 512 * ch:512 * ch + 512],
                        start=(k == 0), stop=(k == 21)))
                S.group("t", fns, reads=rds, writes=[("P", ch)],
                        tag=f"F{l}h{hh}w2m{m8}_{ch}")
                S.op("v", lambda v, m8=m8, ch=ch, c0=c0: v.scalar_tensor_tensor(
                    xT[m8][:, c0 + 512 * ch:c0 + 512 * ch + 512],
                    pA[:, 512 * ch:512 * (ch + 1)], s_b2v[:, m8:m8 + 1],
                    xT[m8][:, c0 + 512 * ch:c0 + 512 * ch + 512], ALU.add, ALU.add),
                    reads=[("P", ch), "lv_b2", ("xT", m8, 2 * hh + ch)],
                    writes=[("xT", m8, 2 * hh + ch)], tag=f"F{l}h{hh}w2r{m8}_{ch}")


def build_l1attn(kb, groups4):
    nc, S = kb.nc, kb.S
    xT, W, P = kb.xT, kb.W, kb.P
    pA, pB = kb.pA, kb.pB
    s_cb, s_ones, s_stats = kb.s_cb, kb.s_ones, kb.s_stats
    s_delta, s_aT = kb.s_delta, kb.s_aT
    s_v, s_wbuf, s_zinv = kb.s_v, kb.s_wbuf, kb.s_zinv
    sb = kb.sb
    STK = ("stats",)
    DK = lambda c: ("delta", c)
    AK = lambda c: ("aT", c)
    FENCE = [("xT", d, c) for d in range(8) for c in range(4)]

    s_l1b = sb("s_l1b", [128, 2048], BF16)
    s_bq = sb("s_bq", [128, 24])
    s_bp = sb("s_bp", [128, 8])
    s_ccn = sb("s_ccn", [128, 4])
    s_ccpw = sb("s_ccpw", [128, 4])
    s_r4 = sb("s_r4", [128, 32])
    s_r4b = sb("s_r4b", [128, 32])
    ag_r_in = nc.dram_tensor("ag_r_in", [1, 128 * 32], F32)
    ag_r_out = nc.dram_tensor("ag_r_out", [4, 128 * 32], F32)
    ag_u_in = nc.dram_tensor("ag_u_in", [1, 128 * 32], BF16)
    ag_u_out = nc.dram_tensor("ag_u_out", [4, 128 * 32], BF16)

    for nm, dst, key in [("f_l1bias", s_l1b, ("l1b",)), ("f_l1bq", s_bq, ("l1bq",)),
                         ("f_l1bp", s_bp, ("l1bp",)), ("f_ccn", s_ccn, ("ccn",)),
                         ("f_ccpw", s_ccpw, ("ccpw",))]:
        kb.load("gd", dst[:, :], kb.ap[nm], key, reads=FENCE)
    S.checkpoint("gd")

    Lw = kb.ap["f_l1w"]
    Lp = kb.ap["f_l1proj"]
    WBH = lambda par: [("wb", 2 * par), ("wb", 2 * par + 1)]

    def wload(tag, dram2d, par):
        kb.load("gd", s_wbuf[:, 1024 * par:1024 * (par + 1)]
                .rearrange("p (k c) -> p k c", k=8),
                dram2d.rearrange("(k p) c -> p k c", p=128),
                WBH(par), tag=tag)

    def ain(d):
        sl = W[d % 4][:, 1024 * (d // 4):1024 * (d // 4) + 1024]
        return sl, [("W", d % 4, 2 * (d // 4) + i) for i in range(2)]

    def qsl(p):
        return W[8 + p // 2][:, 1024 * (p % 2):1024 * (p % 2) + 1024]

    def qkeys(p):
        return [("W", 8 + p // 2, 2 * (p % 2) + i) for i in range(2)]

    def ksl(p):
        return W[12 + p // 2][:, 1024 * (p % 2):1024 * (p % 2) + 1024]

    def kkeys(p):
        return [("W", 12 + p // 2, 2 * (p % 2) + i) for i in range(2)]

    def orl(d):
        sl = W[12 + d // 2][:, 1024 * (d % 2):1024 * (d % 2) + 1024]
        return sl, [("W", 12 + d // 2, 2 * (d % 2) + i) for i in range(2)]

    otok = W[4]
    OTK = [("W", 4, c) for c in range(4)]
    s_attnT = W[5]
    ATK = [("W", 5, c) for c in range(4)]

    def vw(k):
        if k < 4:
            return s_wbuf[:, 512 * k:512 * (k + 1)], ("wb", k)
        return W[5][:, 512 * (k - 4):512 * (k - 4) + 512], ATK[k - 4]

    for d in range(8):
        kb.store("gd", ag_r_in.ap()[0, :].rearrange("(p c) -> p c", p=128)
                 [:, 4 * d:4 * d + 4],
                 xT[d][:, 0:4], reads=[("xT", d, 0)], key=("agr",), tag=f"agr{d}")
    S.checkpoint("gd")
    S.cc(lambda g: g.collective_compute(
        "AllGather", ALU.bypass, replica_groups=groups4,
        ins=[ag_r_in.ap()[:, :]], outs=[ag_r_out.ap()[:, :]]),
        reads=[("agr",)], writes=["ag_r_out"], tag="agroll")
    S.op("v", lambda v: v.memset(s_r4[:, :], 0.0), reads=FENCE, writes=[("r4",)],
         tag="r4z")
    for j in range(4):
        kb.load("gd", s_stats[:, 0:32],
                ag_r_out.ap()[j, :].rearrange("(p c) -> p c", p=128),
                STK, reads=["ag_r_out"], tag=f"agrl{j}")
        S.op("v", lambda v, j=j: v.scalar_tensor_tensor(
            s_r4[:, :], s_stats[:, 0:32], s_ccn[:, j:j + 1], s_r4[:, :],
            ALU.mult, ALU.add),
            reads=[STK, ("ccn",), ("r4",)], writes=[("r4",)], tag=f"agrs{j}")

    for hh in range(2):
        for d in range(8):
            sl, keys = ain(d)
            if hh == 0:
                S.op("s", lambda sc, sl=sl, d=d: sc.activation(
                    sl[:, :], xT[d][:, 4:1028], AF.Copy),
                    reads=[("xT", d, 0), ("xT", d, 1), ("xT", d, 2)],
                    writes=keys, tag=f"A1h0ain{d}")
            else:
                S.op("s", lambda sc, sl=sl, d=d: sc.activation(
                    sl[:, 0:1020], xT[d][:, 1028:2048], AF.Copy),
                    reads=[("xT", d, 2), ("xT", d, 3)],
                    writes=keys, tag=f"A1h1ain{d}")
                S.op("v", lambda v, sl=sl, d=d: v.tensor_copy(
                    sl[:, 1020:1024], s_r4[:, 4 * d:4 * d + 4]),
                    reads=[("r4",)], writes=[keys[1]], tag=f"A1h1t{d}")
        for p in range(8):
            for side in range(2):
                par = p % 2
                wcol = (0 if side == 0 else 1024) + 128 * p
                wload(f"A1h{hh}w{side}_{p}", Lw[:, wcol:wcol + 128], par)
                dstsl = qsl(p) if side == 0 else ksl(p)
                dkeys = qkeys(p) if side == 0 else kkeys(p)
                for ch in range(2):
                    fns = []
                    rds = WBH(par)
                    for k in range(8):
                        asl, akeys = ain(k)
                        rds.append(akeys[ch])
                        fns.append(lambda t, k=k, par=par, ch=ch, asl=asl: t.matmul(
                            pA[:, 512 * ch:512 * (ch + 1)],
                            s_wbuf[:, 1024 * par + 128 * k:1024 * par + 128 * (k + 1)],
                            asl[:, 512 * ch:512 * ch + 512],
                            start=(k == 0), stop=(k == 7)))
                    S.group("t", fns, reads=rds, writes=[("P", ch)],
                            tag=f"A1h{hh}qk{side}_{p}_{ch}")
                    if side == 0:
                        S.op("v", lambda v, dstsl=dstsl, ch=ch, p=p: v.tensor_scalar(
                            dstsl[:, 512 * ch:512 * ch + 512],
                            pA[:, 512 * ch:512 * (ch + 1)],
                            s_bq[:, p:p + 1], None, ALU.add),
                            reads=[("P", ch), ("l1bq",)], writes=[dkeys[ch]],
                            tag=f"A1h{hh}qe{p}_{ch}")
                    else:
                        S.op("v", lambda v, dstsl=dstsl, ch=ch, p=p: v.tensor_scalar(
                            dstsl[:, 512 * ch:512 * ch + 512],
                            pA[:, 512 * ch:512 * (ch + 1)],
                            s_bq[:, 8 + p:9 + p], 0.125, ALU.add, ALU.mult),
                            reads=[("P", ch), ("l1bq",)], writes=[dkeys[ch]],
                            tag=f"A1h{hh}ke{p}_{ch}")
        for fc in range(2):
            for k in range(8):
                dst, key = vw(k)
                kb.load("gd", dst,
                        Lw[128 * k:128 * (k + 1),
                           2048 + 512 * fc:2048 + 512 * (fc + 1)],
                        key, tag=f"A1h{hh}vw{fc}_{k}")
            wrd = [vw(k)[1] for k in range(8)]
            for t8 in range(8):
                fns = []
                rds = list(wrd)
                for k in range(8):
                    asl, akeys = ain(k)
                    rds.append(akeys[(128 * t8) // 512])
                    fns.append(lambda t, t8=t8, k=k, asl=asl: t.matmul(
                        pA[:, 512 * (t8 % 4):512 * (t8 % 4) + 512],
                        asl[:, 128 * t8:128 * t8 + 128],
                        vw(k)[0], start=(k == 0), stop=(k == 7)))
                S.group("t", fns, reads=rds, writes=[("P", t8 % 4)],
                        tag=f"A1h{hh}v{fc}_{t8}")
                S.op("s", lambda sc, t8=t8, fc=fc: sc.activation(
                    s_v[:, 1040 * t8 + 520 * fc:1040 * t8 + 520 * fc + 520]
                    .rearrange("p (h e) -> p h e", e=65)[:, :, 0:64],
                    pA[:, 512 * (t8 % 4):512 * (t8 % 4) + 512]
                    .rearrange("p (h e) -> p h e", e=64), AF.Copy),
                    reads=[("P", t8 % 4)], writes=[("sv", t8)],
                    tag=f"A1h{hh}ve{fc}_{t8}")
        for sl_ in range(8):
            for b in range(4):
                fns = []
                rds = ["s_idb", ("l1b",)]
                for hi in range(4):
                    h = 4 * b + hi
                    p, hp = h // 2, h % 2
                    qap = qsl(p)[64 * hp:64 * hp + 64, 128 * sl_:128 * sl_ + 128]
                    kap = ksl(p)[64 * hp:64 * hp + 64, 128 * sl_:128 * sl_ + 128]
                    rds += qkeys(p) + kkeys(p)
                    dst = pB[:, 512 * b + 128 * hi:512 * b + 128 * hi + 128]
                    fns.append(lambda t, kap=kap, qap=qap, dst=dst, hi=hi: t.matmul(
                        dst, kap, qap, start=(hi == 0), stop=False))
                    fns.append(lambda t, dst=dst, h=h, hi=hi: t.matmul(
                        dst, kb.s_idb[:, :], s_l1b[:, 128 * h:128 * (h + 1)],
                        start=False, stop=(hi == 3)))
                S.group("t", fns, reads=rds, writes=[("P", 4 + b)],
                        tag=f"A1h{hh}s{sl_}_{b}")
                S.op("s", lambda sc, b=b: sc.activation(
                    s_attnT[:, 512 * b:512 * (b + 1)],
                    pB[:, 512 * b:512 * (b + 1)], AF.Exp),
                    reads=[("P", 4 + b)], writes=[ATK[b]],
                    tag=f"A1h{hh}e{sl_}_{b}")
                fns2 = []
                rds2 = [ATK[b]]
                for hi in range(4):
                    h = 4 * b + hi
                    vap = s_v[:, 1040 * sl_ + 65 * h:1040 * sl_ + 65 * h + 65]
                    rds2.append(("sv", sl_))
                    fns2.append(lambda t, h=h, hi=hi, vap=vap: t.matmul(
                        pA[:, 128 * h:128 * h + 65],
                        s_attnT[:, 128 * h:128 * (h + 1)], vap,
                        start=(hi == 0), stop=(hi == 3)))
                S.group("t", fns2, reads=rds2, writes=[("P", b)],
                        tag=f"A1h{hh}o{sl_}_{b}")
                S.op("v", lambda v, b=b: v.reciprocal(
                    s_zinv[:, 4 * b:4 * b + 4],
                    pA[:, 512 * b:512 * (b + 1)]
                    .rearrange("p (h e) -> p h e", e=128)[:, :, 64]),
                    reads=[("P", b)], writes=[("zinv",)], tag=f"A1h{hh}z{sl_}_{b}")
                S.op("v", lambda v, b=b: v.tensor_tensor(
                    otok[:, :].bitcast(F32)[:, 256 * b:256 * (b + 1)]
                    .rearrange("p (h e) -> p h e", e=64),
                    pA[:, 512 * b:512 * (b + 1)]
                    .rearrange("p (h e) -> p h e", e=128)[:, :, 0:64],
                    s_zinv[:, 4 * b:4 * b + 4].unsqueeze(2)
                    .broadcast_to([128, 4, 64]), ALU.mult),
                    reads=[("P", b), ("zinv",)], writes=[OTK[b]],
                    tag=f"A1h{hh}n{sl_}_{b}")
            for j8 in range(8):
                tb_ = 4 + j8 % 2
                S.op("t", lambda t, j8=j8, tb_=tb_: t.transpose(
                    pB[:, 512 * (tb_ - 4):512 * (tb_ - 4) + 128],
                    otok[:, :].bitcast(F32)[:, 128 * j8:128 * (j8 + 1)],
                    kb.s_id[:, :]),
                    reads=[OTK[j8 // 2], "s_id"], writes=[("P", tb_)],
                    tag=f"A1h{hh}tr{sl_}_{j8}")
                bt = s_delta if j8 < 4 else s_aT
                jj = j8 % 4
                dstap = bt[:, :].bitcast(BF16)[:, 1024 * jj + 128 * sl_:
                                               1024 * jj + 128 * sl_ + 128]
                S.op("v", lambda v, dstap=dstap, tb_=tb_, j8=j8: v.tensor_scalar(
                    dstap, pB[:, 512 * (tb_ - 4):512 * (tb_ - 4) + 128],
                    s_bq[:, 16 + j8:17 + j8], None, ALU.add),
                    reads=[("P", tb_), ("l1bq",)],
                    writes=[DK(jj) if j8 < 4 else AK(jj)],
                    tag=f"A1h{hh}te{sl_}_{j8}")
        for m8 in range(8):
            wload(f"A1h{hh}pw{m8}", Lp[:, 128 * m8:128 * (m8 + 1)], m8 % 2)
            osl, okeys = orl(m8)
            for ch in range(2):
                fns = []
                for k in range(8):
                    bt = s_delta if k < 4 else s_aT
                    jj = k % 4
                    rhs = bt[:, :].bitcast(BF16)[:, 1024 * jj + 512 * ch:
                                                 1024 * jj + 512 * ch + 512]
                    fns.append(lambda t, k=k, m8=m8, ch=ch, rhs=rhs: t.matmul(
                        pA[:, 1024 + 512 * ch:1024 + 512 * (ch + 1)],
                        s_wbuf[:, 1024 * (m8 % 2) + 128 * k:
                               1024 * (m8 % 2) + 128 * (k + 1)],
                        rhs, start=(k == 0), stop=(k == 7)))
                S.group("t", fns,
                        reads=WBH(m8 % 2) + [DK(c) for c in range(4)]
                        + [AK(c) for c in range(4)],
                        writes=[("P", 2 + ch)], tag=f"A1h{hh}pj{m8}_{ch}")
                S.op("s", lambda sc, osl=osl, ch=ch: sc.activation(
                    osl[:, 512 * ch:512 * ch + 512],
                    pA[:, 1024 + 512 * ch:1024 + 512 * (ch + 1)], AF.Copy),
                    reads=[("P", 2 + ch)], writes=[okeys[ch]],
                    tag=f"A1h{hh}pe{m8}_{ch}")
        for d in range(8):
            osl, okeys = orl(d)
            if hh == 0:
                S.op("v", lambda v, d=d, osl=osl: v.scalar_tensor_tensor(
                    xT[d][:, 4:1028], osl[:, 0:1024], s_bp[:, d:d + 1],
                    xT[d][:, 4:1028], ALU.add, ALU.add),
                    reads=okeys + [("l1bp",)] + [("xT", d, c) for c in range(3)],
                    writes=[("xT", d, c) for c in range(3)], tag=f"A1r0_{d}")
            else:
                S.op("v", lambda v, d=d, osl=osl: v.scalar_tensor_tensor(
                    xT[d][:, 1028:2048], osl[:, 0:1020], s_bp[:, d:d + 1],
                    xT[d][:, 1028:2048], ALU.add, ALU.add),
                    reads=okeys + [("l1bp",)] + [("xT", d, 2), ("xT", d, 3)],
                    writes=[("xT", d, 2), ("xT", d, 3)], tag=f"A1r1_{d}")

    for d in range(8):
        osl, okeys = orl(d)
        kb.store("gd", ag_u_in.ap()[0, :].rearrange("(p c) -> p c", p=128)
                 [:, 4 * d:4 * d + 4],
                 osl[:, 1020:1024], reads=okeys, key=("agu",), tag=f"agu{d}")
    S.checkpoint("gd")
    S.cc(lambda g: g.collective_compute(
        "AllGather", ALU.bypass, replica_groups=groups4,
        ins=[ag_u_in.ap()[:, :]], outs=[ag_u_out.ap()[:, :]]),
        reads=[("agu",)], writes=["ag_u_out"], tag="agunroll")
    S.op("v", lambda v: v.memset(s_r4b[:, :], 0.0), writes=[("r4b",)], tag="r4bz")
    for j in range(4):
        kb.load("gd", s_attnT[:, 0:32],
                ag_u_out.ap()[j, :].rearrange("(p c) -> p c", p=128),
                ATK[0], reads=["ag_u_out"], tag=f"agul{j}")
        S.op("v", lambda v, j=j: v.scalar_tensor_tensor(
            s_r4b[:, :], s_attnT[:, 0:32], s_ccpw[:, j:j + 1], s_r4b[:, :],
            ALU.mult, ALU.add),
            reads=[ATK[0], ("ccpw",), ("r4b",)], writes=[("r4b",)], tag=f"agus{j}")
    for d in range(8):
        S.op("v", lambda v, d=d: v.scalar_tensor_tensor(
            xT[d][:, 0:4], s_r4b[:, 4 * d:4 * d + 4], s_bp[:, d:d + 1],
            xT[d][:, 0:4], ALU.add, ALU.add),
            reads=[("r4b",), ("l1bp",), ("xT", d, 0)],
            writes=[("xT", d, 0)], tag=f"A1patch{d}")

    build_topln(kb, tag="ln3")


def build_l1ffn(kb):
    nc, S = kb.nc, kb.S
    xT, W, P = kb.xT, kb.W, kb.P
    pA, pB = kb.pA, kb.pB
    s_cb, s_stats = kb.s_cb, kb.s_stats
    s_delta, s_aT = kb.s_delta, kb.s_aT
    s_wbuf = kb.s_wbuf
    sb = kb.sb
    STK = ("stats",)
    DK = lambda c: ("delta", c)
    AK = lambda c: ("aT", c)
    FENCE = [("xT", d, c) for d in range(8) for c in range(4)]
    s_wd2 = sb("s_wd2", [128, 256], BF16)
    Wg = kb.ap["f_wgT"]
    Wu = kb.ap["f_wuT"]
    Wd = kb.ap["f_wdT"]

    def hid(i):
        if i < 16:
            sl = W[8 + i // 2][:, 1024 * (i % 2):1024 * (i % 2) + 1024]
            keys = [("W", 8 + i // 2, 2 * (i % 2) + j) for j in range(2)]
        elif i < 20:
            jj = i - 16
            sl = s_delta[:, :].bitcast(BF16)[:, 1024 * jj:1024 * (jj + 1)]
            keys = [DK(jj)]
        else:
            jj = i - 20
            sl = s_aT[:, :].bitcast(BF16)[:, 1024 * jj:1024 * (jj + 1)]
            keys = [AK(jj)]
        return sl, keys

    for d in range(8):
        S.op("s", lambda sc, d=d: sc.activation(W[d][:, :], xT[d][:, :], AF.Copy),
             reads=[("xT", d, c) for c in range(4)],
             writes=[("W", d, c) for c in range(4)], tag=f"Fx3c{d}")

    for hh in range(2):
        c0 = 1024 * hh
        for hi in range(24):
            kb.load("gd", s_wbuf[:, 0:1024].rearrange("p (k c) -> p k c", k=8),
                    Wg[:, 128 * hi:128 * (hi + 1)]
                    .rearrange("(k p) c -> p k c", p=128),
                    [("wb", 0), ("wb", 1)], tag=f"Fg{hh}_{hi}")
            kb.load("gd", s_wbuf[:, 1024:2048].rearrange("p (k c) -> p k c", k=8),
                    Wu[:, 128 * hi:128 * (hi + 1)]
                    .rearrange("(k p) c -> p k c", p=128),
                    [("wb", 2), ("wb", 3)], tag=f"Fu{hh}_{hi}")
            hsl, hkeys = hid(hi)
            for ch in range(2):
                cc = c0 + 512 * ch
                for part in range(2):
                    fns = []
                    rds = [("wb", 2 * part), ("wb", 2 * part + 1)]
                    for k in range(8):
                        rds.append(("W", k, cc // 512))
                        fns.append(lambda t, k=k, part=part, ch=ch, cc=cc: t.matmul(
                            pA[:, 1024 * part + 512 * ch:
                               1024 * part + 512 * ch + 512],
                            s_wbuf[:, 1024 * part + 128 * k:
                                   1024 * part + 128 * (k + 1)],
                            W[k][:, cc:cc + 512], start=(k == 0), stop=(k == 7)))
                    S.group("t", fns, reads=rds, writes=[("P", 2 * part + ch)],
                            tag=f"F1{hh}m{hi}_{ch}_{part}")
                S.op("s", lambda sc, ch=ch: sc.activation(
                    s_stats[:, 512 * ch:512 * ch + 512],
                    pA[:, 512 * ch:512 * (ch + 1)], AF.Silu,
                    scale=s_cb[:, CB_I127:CB_I127 + 1]),
                    reads=[("P", ch), "s_cb"], writes=[STK],
                    tag=f"F1{hh}sig{hi}_{ch}")
                S.op("v", lambda v, hsl=hsl, ch=ch: v.scalar_tensor_tensor(
                    hsl[:, 512 * ch:512 * ch + 512],
                    pA[:, 1024 + 512 * ch:1024 + 512 * (ch + 1)],
                    s_cb[:, CB_I127:CB_I127 + 1],
                    s_stats[:, 512 * ch:512 * ch + 512], ALU.mult, ALU.mult),
                    reads=[("P", 2 + ch), "s_cb", STK],
                    writes=[hkeys[min(ch, len(hkeys) - 1)]],
                    tag=f"F1{hh}h{hi}_{ch}")
        for m8 in range(8):
            kb.load("gd", s_wbuf[:, 0:2816].rearrange("p (k c) -> p k c", k=22),
                    Wd[0:2816, 128 * m8:128 * (m8 + 1)]
                    .rearrange("(k p) c -> p k c", p=128),
                    [("wb", q) for q in range(6)],
                    reads=FENCE if (hh == 0 and m8 == 0) else [],
                    tag=f"Fd{hh}_{m8}")
            kb.load("gd", s_wd2[:, :].rearrange("p (k c) -> p k c", k=2),
                    Wd[2816:3072, 128 * m8:128 * (m8 + 1)]
                    .rearrange("(k p) c -> p k c", p=128),
                    ("wd2",), reads=FENCE if (hh == 0 and m8 == 0) else [],
                    tag=f"Fd2{hh}_{m8}")
            for ch in range(2):
                fns = []
                rds = [("wb", q) for q in range(6)] + [("wd2",)]
                for k in range(24):
                    hsl, hkeys = hid(k)
                    rds += hkeys
                    lhs = (s_wbuf[:, 128 * k:128 * (k + 1)] if k < 22
                           else s_wd2[:, 128 * (k - 22):128 * (k - 21)])
                    fns.append(lambda t, k=k, ch=ch, hsl=hsl, lhs=lhs: t.matmul(
                        pB[:, 512 * ch:512 * (ch + 1)], lhs,
                        hsl[:, 512 * ch:512 * ch + 512],
                        start=(k == 0), stop=(k == 23)))
                S.group("t", fns, reads=rds, writes=[("P", 4 + ch)],
                        tag=f"F1{hh}o{m8}_{ch}")
                S.op("v", lambda v, m8=m8, ch=ch, c0=c0: v.scalar_tensor_tensor(
                    xT[m8][:, c0 + 512 * ch:c0 + 512 * ch + 512],
                    pB[:, 512 * ch:512 * (ch + 1)],
                    s_cb[:, CB_I127:CB_I127 + 1],
                    xT[m8][:, c0 + 512 * ch:c0 + 512 * ch + 512],
                    ALU.mult, ALU.add),
                    reads=[("P", 4 + ch), "s_cb", ("xT", m8, 2 * hh + ch)],
                    writes=[("xT", m8, 2 * hh + ch)], tag=f"F1{hh}r{m8}_{ch}")

    build_topln(kb, tag="ln4")


# version 02
# speedup vs baseline: 1.1936x; 1.1936x over previous
import numpy as np
import ml_dtypes
from collections import defaultdict
from contextlib import ExitStack

import concourse.bass as bass
import concourse.mybir as mybir
from concourse.bass_utils import run_bass_kernel_spmd

F32 = mybir.dt.float32
BF16 = mybir.dt.bfloat16
I32 = mybir.dt.int32
I32 = mybir.dt.int32
AF = mybir.ActivationFunctionType
ALU = mybir.AluOpType
AX = mybir.AxisListType
bf16 = ml_dtypes.bfloat16

COMPUTE = ("t", "v", "s", "g", "y")
ASYNC = ("yd", "gd", "sd", "gc")
ISSUER = {"yd": "y", "gd": "g", "sd": "s", "gc": "g"}


class _Op:
    __slots__ = ("eng", "fns", "reads", "writes", "seq", "amount", "tag",
                 "dep_ops", "waits", "inc_needed", "val", "is_ckpt",
                 "ckpt_queue", "ckpt_cover")

    def __init__(self, eng, fns, reads, writes, seq, amount, tag):
        self.eng = eng; self.fns = fns; self.reads = reads; self.writes = writes
        self.seq = seq; self.amount = amount; self.tag = tag
        self.dep_ops = []; self.waits = {}
        self.inc_needed = False; self.val = None
        self.is_ckpt = False; self.ckpt_queue = None; self.ckpt_cover = 0


class Seq:
    def __init__(self):
        self.ops = []
        self.per_eng = defaultdict(list)
        self.writer = {}
        self.readers = defaultdict(dict)
        self.ckpts = defaultdict(list)

    def _mkop(self, eng, fns, reads, writes, amount, tag):
        op = _Op(eng, fns, list(reads), list(writes),
                 len(self.per_eng[eng]) + 1, amount, tag)
        self.ops.append(op)
        self.per_eng[eng].append(op)
        return op

    def checkpoint(self, queue, tag=None):
        assert queue in ASYNC
        cover = len(self.per_eng[queue])
        if cover == 0:
            return None
        if self.ckpts[queue] and self.ckpts[queue][-1].ckpt_cover >= cover:
            return self.ckpts[queue][-1]
        op = self._mkop(ISSUER[queue], [lambda e: e.nop()], [], [], 1,
                        tag or f"ckpt_{queue}_{cover}")
        op.is_ckpt = True; op.ckpt_queue = queue; op.ckpt_cover = cover
        self.ckpts[queue].append(op)
        return op

    def _dep_for(self, dep_op):
        if dep_op.eng in ASYNC:
            for cp in self.ckpts[dep_op.eng]:
                if cp.ckpt_cover >= dep_op.seq:
                    return cp
            return self.checkpoint(dep_op.eng)
        return dep_op

    def _record(self, eng, fns, reads, writes, amount, tag):
        deps = {}

        def need(d):
            if d is None:
                return
            prev = deps.get(d.eng)
            if prev is None or d.seq > prev.seq:
                deps[d.eng] = d

        for b in reads:
            w = self.writer.get(b)
            if w is not None:
                need(self._dep_for(w))
        for b in writes:
            w = self.writer.get(b)
            if w is not None:
                need(self._dep_for(w))
            for rop in self.readers.get(b, {}).values():
                need(self._dep_for(rop))
        op = self._mkop(eng, fns, reads, writes, amount, tag)
        op.dep_ops = [d for d in deps.values()
                      if not (d.eng == eng and d.seq >= op.seq)]
        for b in op.writes:
            self.writer[b] = op
            self.readers[b] = {}
        for b in op.reads:
            self.readers[b][eng] = op
        return op

    def op(self, eng, fn, reads=(), writes=(), tag=None):
        assert eng in COMPUTE
        return self._record(eng, [fn], reads, writes, 1, tag)

    def group(self, eng, fns, reads=(), writes=(), tag=None):
        assert eng in COMPUTE
        return self._record(eng, list(fns), reads, writes, 1, tag)

    def dma(self, queue, fn, reads=(), writes=(), tag=None):
        assert queue in ("yd", "gd", "sd")
        return self._record(queue, [fn], reads, writes, 16, tag)

    def cc(self, fn, reads=(), writes=(), tag=None):
        return self._record("gc", [fn], reads, writes, 1, tag)

    def finalize(self, nc, block):
        targets = defaultdict(set)
        for op in self.ops:
            for d in op.dep_ops:
                targets[d.eng].add(d.seq)
        for e, ops in self.per_eng.items():
            v = 0
            always = e in ASYNC
            for o in ops:
                if always or o.seq in targets[e] or o.is_ckpt:
                    o.inc_needed = True
                    v += o.amount
                o.val = v
        for op in self.ops:
            rw = {}
            for d in op.dep_ops:
                assert d.inc_needed
                if d.val > 0 and rw.get(d.eng, 0) < d.val:
                    rw[d.eng] = d.val
            op.waits = rw
            if op.is_ckpt:
                q = op.ckpt_queue
                op.waits[q] = self.per_eng[q][op.ckpt_cover - 1].val

        stack = ExitStack()
        sems = {e: stack.enter_context(nc.semaphore(f"sem_{e}"))
                for e in list(self.per_eng)}
        self._semstack = stack
        phys_of = {"t": "t", "v": "v", "s": "s", "g": "g", "y": "y",
                   "yd": "y", "gd": "g", "sd": "s", "gc": "g"}
        by_phys = defaultdict(list)
        for op in self.ops:
            by_phys[phys_of[op.eng]].append(op)
        decorators = {"t": block.tensor, "v": block.vector, "s": block.scalar,
                      "g": block.gpsimd, "y": block.sync}
        stats = {"n_ops": len(self.ops), "n_incs": 0, "n_waits": 0}
        for pe, ops_list in by_phys.items():
            tail = [(q, self.per_eng[q][-1].val) for q in ASYNC
                    if self.per_eng[q] and ISSUER[q] == pe]

            def body(eng, _ops=ops_list, _tail=tail):
                obs = defaultdict(int)
                for o in _ops:
                    for e2 in sorted(o.waits, key=str):
                        v = o.waits[e2]
                        if obs[e2] < v:
                            eng.wait_ge(sems[e2], v)
                            obs[e2] = v
                            stats["n_waits"] += 1
                    inst = None
                    for fn in o.fns:
                        inst = fn(eng)
                    if o.inc_needed:
                        inst.then_inc(sems[o.eng], o.amount)
                        stats["n_incs"] += 1
                for q, v in _tail:
                    if obs[q] < v:
                        eng.wait_ge(sems[q], v)
            decorators[pe](body)
        self.stats = stats
        return sems


NCORES = 8
B, L, D = 2, 8192, 1024
TC = 2048
DT = 8
NEG = -1e30
PHASE_LIMIT = 99

CB_ONE = 0
CB_EPS5 = 1
CB_I127 = 3


class KB:

    def __init__(self, nc, S, ctx):
        self.nc = nc
        self.S = S
        self.ctx = ctx
        self.ap = {}

    def inp(self, name, shape, dtype=F32):
        self.ap[name] = self.nc.dram_tensor(name, list(shape), dtype,
                                            kind="ExternalInput").ap()
        return self.ap[name]

    def sb(self, name, shape, dtype=F32):
        return self.ctx.enter_context(self.nc.sbuf_tensor(name, shape, dtype))

    def psum(self, name, shape, dtype=F32):
        return self.ctx.enter_context(self.nc.psum_tensor(name, shape, dtype))

    def load(self, queue, dst_ap, src_ap, key, reads=(), tag=None):
        wk = key if isinstance(key, list) else [key]
        self.S.dma(queue, lambda g, d=dst_ap, s=src_ap: g.dma_start(out=d, in_=s),
                   writes=wk, reads=list(reads), tag=tag)

    def store(self, queue, dst_ap, src_ap, reads, key=None, tag=None):
        wk = key if isinstance(key, list) else ([key] if key else [])
        self.S.dma(queue, lambda g, d=dst_ap, s=src_ap: g.dma_start(out=d, in_=s),
                   reads=list(reads), writes=wk, tag=tag)


def wkeys(i, c0=0, c1=4):
    return [("W", i, c) for c in range(c0, c1)]


def xkeys(d, c0=0, c1=4):
    return [("xT", d, c) for c in range(c0, c1)]


def build_nc(phase_limit=99):
    nc = bass.Bass("TRN2", target_bir_lowering=False, dynamic_dma_scratch_size=16384)
    S = Seq()
    with ExitStack() as ctx:
        kb = KB(nc, S, ctx)
        inp, sb = kb.inp, kb.sb

        inp("x_tok", [TC, D])
        inp("m_xprojT", [D, 96])
        inp("m_dtwT", [64, D])
        inp("m_dtb", [128, 8])
        inp("m_Alog", [128, 128])
        inp("m_D", [128, 8])
        inp("m_lna", [128, 8])
        inp("m_lnb", [128, 8])
        inp("identity", [128, 128])
        inp("ones_row", [128, 128])
        inp("cbank", [128, 8])
        inp("selBC", [32, 32 * 128])
        inp("ccpre", [128, 4])
        inp("f_wqkv", [6, D, 3072])
        inp("f_wo", [6, D, D])
        inp("f_w1", [6, D, 5632])
        inp("f_w2", [6, 2816, D])
        inp("f_b1", [6, 128, 44])
        inp("f_lng", [6, 128, 8])
        inp("f_lnb2", [6, 128, 8])
        inp("f_b2", [6, 128, 8])
        inp("f_qs8", [6, 128, 1])
        inp("f_ks", [6, 128, 1])
        inp("f_ropeqA", [128, TC])
        inp("f_ropeqB", [128, TC])
        inp("f_ropekA", [128, TC])
        inp("f_ropekB", [128, TC])
        inp("f_ropekAh", [128, 256])
        inp("f_ropekBh", [128, 256])
        inp("f_maskup", [128, 128])
        inp("f_masklo", [128, 128])
        inp("f_m00", [128, 128])
        inp("f_m01", [128, 128])
        inp("f_m10", [128, 128])
        inp("f_selE2", [128, 16 * 64])
        inp("f_selN", [64, 16 * 128])
        inp("f_ccm", [128, 4])
        inp("f_l1w", [D, 3072])
        inp("f_l1proj", [D, D])
        inp("f_l1bq", [128, 24])
        inp("f_l1bp", [128, 8])
        inp("f_l1bias", [128, 2048])
        inp("f_ccn", [128, 4])
        inp("f_ccpw", [128, 4])
        inp("f_wgT", [D, 3072], I32)
        inp("f_wuT", [D, 3072], I32)
        inp("f_wdT", [3072, D], I32)
        y_tok = nc.dram_tensor("y_tok", [TC, D], F32, kind="ExternalOutput").ap()

        ag_scan_in = nc.dram_tensor("ag_scan_in", [1, 128 * 136], F32)
        kb.y_scr = nc.dram_tensor("y_scr", [8, 128, 2048], F32, kind="ExternalOutput")
        ag_scan_out = nc.dram_tensor("ag_scan_out", [4, 128 * 136], F32)
        groups4 = [[0, 1, 2, 3], [4, 5, 6, 7]]

        kb.xT = [sb(f"xT{d}", [128, TC]) for d in range(DT)]
        kb.s_id = sb("s_id", [128, 128])
        kb.s_idb = sb("s_idb", [128, 128], BF16)
        kb.s_cb = sb("s_cb", [128, 8])
        kb.s_cbb = sb("s_cbb", [128, 8], BF16)
        kb.s_ones = sb("s_ones", [128, 128])
        kb.W = [sb(f"W{i}", [128, TC], BF16) for i in range(16)]
        kb.s_stats = sb("s_stats", [128, TC])
        kb.s_delta = sb("s_delta", [128, TC])
        kb.s_aT = sb("s_aT", [128, TC])
        kb.pA = kb.psum("pA", [128, 2048])
        kb.pB = kb.psum("pB", [128, 2048])
        kb.P = ([kb.pA[:, 512 * i:512 * (i + 1)] for i in range(4)]
                + [kb.pB[:, 512 * i:512 * (i + 1)] for i in range(4)])

        block = ctx.enter_context(nc.Block())
        xT, W, P = kb.xT, kb.W, kb.P
        s_id, s_cb = kb.s_id, kb.s_cb

        kb.load("yd", s_id[:, :], kb.ap["identity"], "s_id")
        kb.load("yd", kb.s_ones[:, :], kb.ap["ones_row"], "s_ones")
        kb.load("yd", s_cb[:, :], kb.ap["cbank"], "s_cb")
        S.checkpoint("yd")
        S.op("s", lambda sc: sc.activation(kb.s_idb[:, :], s_id[:, :], AF.Copy),
             reads=["s_id"], writes=["s_idb"])
        S.op("s", lambda sc: sc.activation(kb.s_cbb[:, :], s_cb[:, :], AF.Copy),
             reads=["s_cb"], writes=["s_cbb"])

        for tt in range(16):
            stg = kb.W[tt % 2][:, :].bitcast(F32)
            kb.load("yd", stg[:, :], kb.ap["x_tok"][128 * tt:128 * (tt + 1), :],
                    ("W", tt % 2, 0), tag=f"ldx{tt}")
            for d in range(DT):
                bk = (tt * 8 + d) % 4
                S.op("t", lambda t, stg=stg, d=d, bk=bk:
                     t.transpose(P[bk][:, 0:128], stg[:, 128 * d:128 * (d + 1)], s_id[:, :]),
                     reads=[("W", tt % 2, 0), "s_id"], writes=[("P", bk)],
                     tag=f"xtr{tt}_{d}")
                S.op("s", lambda sc, d=d, tt=tt, bk=bk:
                     sc.activation(xT[d][:, 128 * tt:128 * (tt + 1)], P[bk][:, 0:128],
                                   AF.Copy),
                     reads=[("P", bk)], writes=[("xT", d, tt // 4)],
                     tag=f"xev{tt}_{d}")

        if phase_limit >= 1:
            build_mamba(kb, groups4, ag_scan_in, ag_scan_out)

        n_l0 = max(0, min(6, phase_limit - 1))
        if n_l0 > 0:
            for d in range(DT):
                kb.store("gd", kb.y_scr.ap()[d, :, 0:1024], xT[d][:, 0:1024],
                         reads=xkeys(d), key=("yscr", d, 0), tag=f"x1sp{d}a")
                kb.store("gd", kb.y_scr.ap()[d, :, 1024:2048], xT[d][:, 1024:2048],
                         reads=xkeys(d), key=("yscr", d, 1), tag=f"x1sp{d}b")
            build_l0ffn(kb, groups4, n_l0)
        if phase_limit >= 8:
            for d in range(DT):
                w0, w1 = W[2 * (d % 2)], W[2 * (d % 2) + 1]
                k0 = ("W", 2 * (d % 2), 0)
                k1 = ("W", 2 * (d % 2) + 1, 0)
                kb.load("gd", w0[:, :].bitcast(F32), kb.y_scr.ap()[d, :, 0:1024],
                        k0, reads=[("yscr", d, 0)], tag=f"x1ld{d}a")
                kb.load("gd", w1[:, :].bitcast(F32), kb.y_scr.ap()[d, :, 1024:2048],
                        k1, reads=[("yscr", d, 1)], tag=f"x1ld{d}b")
                S.op("v", lambda v, d=d, w0=w0: v.tensor_tensor(
                    xT[d][:, 0:1024], xT[d][:, 0:1024], w0[:, :].bitcast(F32),
                    ALU.add),
                    reads=[("xT", d, 0), ("xT", d, 1), k0],
                    writes=[("xT", d, 0), ("xT", d, 1)], tag=f"x1ad{d}a")
                S.op("v", lambda v, d=d, w1=w1: v.tensor_tensor(
                    xT[d][:, 1024:2048], xT[d][:, 1024:2048], w1[:, :].bitcast(F32),
                    ALU.add),
                    reads=[("xT", d, 2), ("xT", d, 3), k1],
                    writes=[("xT", d, 2), ("xT", d, 3)], tag=f"x1ad{d}b")
            build_topln(kb, tag="ln2")
        if phase_limit >= 9:
            build_l1attn(kb, groups4)
        if phase_limit >= 10:
            build_l1ffn(kb)

        for tt in range(16):
            stg = kb.W[tt % 2][:, :].bitcast(F32)
            for d in range(DT):
                bk = 4 + (tt * 8 + d) % 4
                S.op("t", lambda t, d=d, tt=tt, bk=bk:
                     t.transpose(P[bk][:, 0:128], xT[d][:, 128 * tt:128 * (tt + 1)],
                                 s_id[:, :]),
                     reads=[("xT", d, tt // 4), "s_id"], writes=[("P", bk)],
                     tag=f"otr{tt}_{d}")
                S.op("s", lambda sc, d=d, bk=bk, stg=stg, tt=tt:
                     sc.activation(stg[:, 128 * d:128 * (d + 1)], P[bk][:, 0:128],
                                   AF.Copy),
                     reads=[("P", bk)], writes=[("W", tt % 2, 0)],
                     tag=f"oev{tt}_{d}")
            kb.store("yd", y_tok[128 * tt:128 * (tt + 1), :], stg[:, :],
                     reads=[("W", tt % 2, 0)],
                     key=("ytok", tt), tag=f"sty{tt}")

        S.finalize(nc, block)
    return nc, S.stats


def build_mamba(kb, groups4, ag_scan_in, ag_scan_out):
    nc, S = kb.nc, kb.S
    xT, W, P = kb.xT, kb.W, kb.P
    s_cb, s_cbb, s_ones, s_stats = kb.s_cb, kb.s_cbb, kb.s_ones, kb.s_stats
    mstack = ExitStack()
    sb = lambda name, shape, dtype=F32: mstack.enter_context(
        nc.sbuf_tensor(name, shape, dtype))
    kb._mamba_stack = mstack

    s_xproj = sb("s_xproj", [128, 8 * 96], BF16)
    s_dtw = sb("s_dtw", [64, D], BF16)
    s_dtb = sb("s_dtb", [128, 8])
    s_Aneg = sb("s_Aneg", [128, 128])
    s_mD = sb("s_mD", [128, 8])
    s_lna = sb("s_lna", [128, 8])
    s_lnb = sb("s_lnb", [128, 8])
    s_selBC = sb("s_selBC", [32, 32 * 128], BF16)
    s_bc = sb("s_bc", [32, TC], BF16)
    s_ccp = sb("s_ccp", [128, 4])
    s_om = sb("s_om", [128, 4])
    s_hlast = sb("s_hlast", [128, 128])
    s_sumd = sb("s_sumd", [128, 8])
    s_hin = sb("s_hin", [128, 128])
    s_agscan = sb("s_agscan", [128, 4 * 136])
    s_dbcd = sb("s_dbcd", [64, TC], BF16)
    s_delta = kb.s_delta
    s_aT = kb.s_aT
    s_h32 = sb("s_h32", [128, TC])
    s_Ptmp = sb("s_Ptmp", [128, 128])

    for k in range(8):
        kb.load("gd", s_xproj[:, 96 * k:96 * (k + 1)],
                kb.ap["m_xprojT"][128 * k:128 * (k + 1), :], ("xproj", k))
    kb.load("gd", s_dtw[:, :], kb.ap["m_dtwT"], "dtw")
    kb.load("gd", s_dtb[:, :], kb.ap["m_dtb"], "dtb")
    kb.load("gd", s_Aneg[:, :], kb.ap["m_Alog"], "Alog")
    kb.load("gd", s_mD[:, :], kb.ap["m_D"], "mD")
    kb.load("gd", s_lna[:, :], kb.ap["m_lna"], "lna")
    kb.load("gd", s_lnb[:, :], kb.ap["m_lnb"], "lnb")
    kb.load("gd", s_selBC[:, :], kb.ap["selBC"], "selBC")
    kb.load("gd", s_ccp[:, :], kb.ap["ccpre"], "ccp")
    S.checkpoint("gd")

    S.op("s", lambda sc: sc.activation(s_Aneg[:, :], s_Aneg[:, :], AF.Exp),
         reads=["Alog"], writes=["Aneg_t"])
    S.op("v", lambda v: v.tensor_scalar(s_Aneg[:, :], s_Aneg[:, :], -1.0, None,
                                        ALU.mult),
         reads=["Aneg_t"], writes=["Aneg"])
    S.op("v", lambda v: v.tensor_scalar(s_om[:, :], s_ccp[:, :], -1.0, 1.0,
                                        ALU.mult, ALU.add),
         reads=["ccp"], writes=["s_om"])

    for ch in range(4):
        for k in range(8):
            wslot = W[0 + k % 2]
            S.op("s", lambda sc, k=k, ch=ch, wslot=wslot:
                 sc.activation(wslot[:, 0:512], xT[k][:, 512 * ch:512 * (ch + 1)],
                               AF.Copy),
                 reads=[("xT", k, ch)], writes=[("W", 0 + k % 2, 0)],
                 tag=f"xb{ch}_{k}")
            S.group("t",
                    [lambda t, k=k, ch=ch, wslot=wslot:
                     t.matmul(P[ch][0:64, 0:512], s_xproj[:, 96 * k:96 * k + 64],
                              wslot[:, 0:512], start=(k == 0), stop=(k == 7)),
                     lambda t, k=k, ch=ch, wslot=wslot:
                     t.matmul(P[4 + ch][0:32, 0:512],
                              s_xproj[:, 96 * k + 64:96 * (k + 1)],
                              wslot[:, 0:512], start=(k == 0), stop=(k == 7))],
                    reads=[("xproj", k), ("W", 0 + k % 2, 0)],
                    writes=[("P", ch), ("P", 4 + ch)], tag=f"dbc{ch}_{k}")
        S.op("s", lambda sc, ch=ch: sc.activation(
            s_dbcd[:, 512 * ch:512 * (ch + 1)], P[ch][0:64, 0:512], AF.Copy),
            reads=[("P", ch)], writes=[("dbcd", ch)], tag=f"dbcev{ch}")
        S.op("v", lambda v, ch=ch: v.tensor_copy(
            s_bc[:, 512 * ch:512 * (ch + 1)], P[4 + ch][0:32, 0:512]),
            reads=[("P", 4 + ch)], writes=[("stats",)], tag=f"bcev{ch}")

    dX = [W[8 + i] for i in range(8)]

    def emit_delta(d, p):
        for ch in range(4):
            bk = ch % 2
            S.op("t", lambda t, d=d, ch=ch, bk=bk:
                 t.matmul(P[bk][:, 0:512], s_dtw[:, 128 * d:128 * (d + 1)],
                          s_dbcd[:, 512 * ch:512 * (ch + 1)], start=True, stop=True),
                 reads=["dtw", ("dbcd", ch)], writes=[("P", bk)], tag=f"dt{p}{d}_{ch}")
            S.op("s", lambda sc, d=d, bk=bk:
                 sc.activation(P[bk][:, 0:512], P[bk][:, 0:512], AF.Exp,
                               bias=s_dtb[:, d:d + 1]),
                 reads=[("P", bk), "dtb"], writes=[("P", bk)], tag=f"dte{p}{d}_{ch}")
            S.op("s", lambda sc, d=d, ch=ch, bk=bk:
                 sc.activation(s_delta[:, 512 * ch:512 * (ch + 1)], P[bk][:, 0:512],
                               AF.Ln, bias=s_cb[:, CB_ONE:CB_ONE + 1]),
                 reads=[("P", bk), "s_cb"], writes=[("delta", ch)], tag=f"dtl{p}{d}_{ch}")
        if p == 1:
            S.op("v", lambda v, d=d: v.tensor_tensor(dX[d][:, :], s_delta[:, :],
                                                     xT[d][:, :], ALU.mult),
                 reads=[("delta", c) for c in range(4)] + xkeys(d),
                 writes=wkeys(8 + d), tag=f"dx{d}")
            S.op("v", lambda v, d=d: v.tensor_reduce(s_sumd[:, d:d + 1],
                                                     s_delta[:, :], AX.X, ALU.add),
                 reads=[("delta", c) for c in range(4)], writes=[("sumd", d)],
                 tag=f"sumd{d}")

    aT = s_aT
    hT, bT = W[5], W[6]
    for d in range(8):
        emit_delta(d, 1)
        for n in range(16):
            S.op("s", lambda sc, d=d, n=n:
                 sc.activation(aT[:, :], s_delta[:, :], AF.Exp,
                               scale=s_Aneg[:, 16 * d + n:16 * d + n + 1]),
                 reads=[("delta", c) for c in range(4)] + ["Aneg"],
                 writes=[("aT", c) for c in range(4)], tag=f"p1a{d}_{n}")
            for ch in range(4):
                S.op("t", lambda t, n=n, ch=ch:
                     t.matmul(P[4 + ch][:, 0:512], s_selBC[:, 128 * n:128 * (n + 1)],
                              s_bc[:, 512 * ch:512 * (ch + 1)],
                              start=True, stop=True),
                     reads=["selBC"] + [("stats",)],
                     writes=[("P", 4 + ch)], tag=f"p1B{d}_{n}_{ch}")
                S.op("v", lambda v, d=d, ch=ch:
                     v.tensor_tensor(bT[:, 512 * ch:512 * (ch + 1)],
                                     dX[d][:, 512 * ch:512 * (ch + 1)],
                                     P[4 + ch][:, 0:512], ALU.mult),
                     reads=[("W", 8 + d, ch)] + [("P", 4 + ch)],
                     writes=[("W", 6, ch)], tag=f"p1b{d}_{n}_{ch}")
            S.op("v", lambda v:
                 v.tensor_tensor_scan(s_h32[:, :], aT[:, :], bT[:, :], 0.0,
                                      ALU.mult, ALU.add),
                 reads=[("aT", c) for c in range(4)] + wkeys(6), writes=[("h32",)],
                 tag=f"p1s{d}_{n}")
            S.op("v", lambda v, d=d, n=n:
                 v.tensor_copy(s_hlast[:, 16 * d + n:16 * d + n + 1],
                               s_h32[:, TC - 1:TC]),
                 reads=[("h32",)], writes=[("hlast", d, n)], tag=f"p1h{d}_{n}")

    kb.store("gd", ag_scan_in.ap()[0, 0:128 * 128].rearrange("(p c) -> p c", p=128),
             s_hlast[:, :],
             reads=[("hlast", d, n) for d in range(8) for n in range(16)],
             key="agin_h")
    kb.store("gd", ag_scan_in.ap()[0, 128 * 128:128 * 136].rearrange("(p c) -> p c", p=128),
             s_sumd[:, :], reads=[("sumd", d) for d in range(8)], key="agin_s")
    S.checkpoint("gd")
    S.cc(lambda g: g.collective_compute(
        "AllGather", ALU.bypass, replica_groups=groups4,
        ins=[ag_scan_in.ap()[:, :]], outs=[ag_scan_out.ap()[:, :]]),
        reads=["agin_h", "agin_s"], writes=["ag_scan_out"], tag="ag_scan")
    for j in range(4):
        kb.load("gd", s_agscan[:, 136 * j:136 * j + 128],
                ag_scan_out.ap()[j, 0:128 * 128].rearrange("(p c) -> p c", p=128),
                ("agscan", j), reads=["ag_scan_out"])
        kb.load("gd", s_agscan[:, 136 * j + 128:136 * (j + 1)],
                ag_scan_out.ap()[j, 128 * 128:128 * 136].rearrange("(p c) -> p c", p=128),
                ("agscan", j, "s"), reads=["ag_scan_out"])

    S.op("v", lambda v: v.memset(s_hin[:, :], 0.0), writes=[("hin",)], tag="hin0")
    for j in range(4):
        for d in range(8):
            S.op("s", lambda sc, j=j, d=d:
                 sc.activation(s_Ptmp[:, 16 * d:16 * (d + 1)],
                               s_Aneg[:, 16 * d:16 * (d + 1)], AF.Exp,
                               scale=s_agscan[:, 136 * j + 128 + d:136 * j + 129 + d]),
                 reads=[("agscan", j), ("agscan", j, "s"), "Aneg"], writes=[("Ptmp", d)], tag=f"cmP{j}_{d}")
        PK = [("Ptmp", d) for d in range(8)]
        S.op("v", lambda v, j=j: v.tensor_scalar(
            s_Ptmp[:, :], s_Ptmp[:, :], s_ccp[:, j:j + 1], None, ALU.mult),
            reads=PK + ["ccp"], writes=PK, tag=f"cmb{j}")
        S.op("v", lambda v, j=j: v.tensor_scalar(
            s_Ptmp[:, :], s_Ptmp[:, :], s_om[:, j:j + 1], None, ALU.add),
            reads=PK + ["s_om"], writes=PK, tag=f"cmb2{j}")
        S.op("v", lambda v: v.tensor_tensor(s_hin[:, :], s_hin[:, :],
                                            s_Ptmp[:, :], ALU.mult),
             reads=[("hin",)] + PK, writes=[("hin",)], tag=f"cmm{j}")
        S.op("v", lambda v, j=j: v.scalar_tensor_tensor(
            s_hin[:, :], s_agscan[:, 136 * j:136 * j + 128], s_ccp[:, j:j + 1],
            s_hin[:, :], ALU.mult, ALU.add),
            reads=[("agscan", j), "ccp", ("hin",)], writes=[("hin",)], tag=f"cma{j}")


    hC = W[7]
    for d in range(8):
        emit_delta(d, 2)
        for n in range(16):
            S.op("s", lambda sc, d=d, n=n:
                 sc.activation(aT[:, :], s_delta[:, :], AF.Exp,
                               scale=s_Aneg[:, 16 * d + n:16 * d + n + 1]),
                 reads=[("delta", c) for c in range(4)] + ["Aneg"],
                 writes=[("aT", c) for c in range(4)], tag=f"p2a{d}_{n}")
            for ch in range(4):
                S.op("t", lambda t, n=n, ch=ch:
                     t.matmul(P[4 + ch][:, 0:512], s_selBC[:, 128 * n:128 * (n + 1)],
                              s_bc[:, 512 * ch:512 * (ch + 1)],
                              start=True, stop=True),
                     reads=["selBC"] + [("stats",)],
                     writes=[("P", 4 + ch)], tag=f"p2B{d}_{n}_{ch}")
                S.op("v", lambda v, d=d, ch=ch:
                     v.tensor_tensor(bT[:, 512 * ch:512 * (ch + 1)],
                                     dX[d][:, 512 * ch:512 * (ch + 1)],
                                     P[4 + ch][:, 0:512], ALU.mult),
                     reads=[("W", 8 + d, ch)] + [("P", 4 + ch)],
                     writes=[("W", 6, ch)], tag=f"p2b{d}_{n}_{ch}")
            S.op("v", lambda v, d=d, n=n:
                 v.tensor_tensor_scan(hT[:, :], aT[:, :], bT[:, :],
                                      s_hin[:, 16 * d + n:16 * d + n + 1],
                                      ALU.mult, ALU.add),
                 reads=[("aT", c) for c in range(4)] + wkeys(6) + [("hin",)],
                 writes=wkeys(5), tag=f"p2s{d}_{n}")
            for ch in range(4):
                S.op("t", lambda t, n=n, ch=ch:
                     t.matmul(P[4 + ch][:, 0:512],
                              s_selBC[:, 128 * (16 + n):128 * (17 + n)],
                              s_bc[:, 512 * ch:512 * (ch + 1)],
                              start=True, stop=True),
                     reads=["selBC"] + [("stats",)],
                     writes=[("P", 4 + ch)], tag=f"p2C{d}_{n}_{ch}")
                S.op("v", lambda v, ch=ch:
                     v.tensor_tensor(hC[:, 512 * ch:512 * (ch + 1)],
                                     hT[:, 512 * ch:512 * (ch + 1)],
                                     P[4 + ch][:, 0:512], ALU.mult),
                     reads=wkeys(5, ch, ch + 1) + [("P", 4 + ch)],
                     writes=[("W", 7, ch)], tag=f"p2hc{d}_{n}_{ch}")
            yv0 = W[2].bitcast(F32)
            yv1 = W[3].bitcast(F32)
            if n == 0:
                S.op("v", lambda v, yv0=yv0: v.tensor_copy(yv0[:, :], hC[:, 0:1024]),
                     reads=[("W", 7, 0), ("W", 7, 1)],
                     writes=[("W", 2, c) for c in range(4)], tag=f"p2y0{d}")
                S.op("v", lambda v, yv1=yv1: v.tensor_copy(yv1[:, :], hC[:, 1024:2048]),
                     reads=[("W", 7, 2), ("W", 7, 3)],
                     writes=[("W", 3, c) for c in range(4)], tag=f"p2y1{d}")
            else:
                S.op("v", lambda v, yv0=yv0: v.tensor_tensor(
                    yv0[:, :], yv0[:, :], hC[:, 0:1024], ALU.add),
                    reads=[("W", 7, 0), ("W", 7, 1)] + [("W", 2, c) for c in range(4)],
                    writes=[("W", 2, c) for c in range(4)], tag=f"p2ya{d}_{n}")
                S.op("v", lambda v, yv1=yv1: v.tensor_tensor(
                    yv1[:, :], yv1[:, :], hC[:, 1024:2048], ALU.add),
                    reads=[("W", 7, 2), ("W", 7, 3)] + [("W", 3, c) for c in range(4)],
                    writes=[("W", 3, c) for c in range(4)], tag=f"p2yb{d}_{n}")
        S.op("v", lambda v, d=d: v.scalar_tensor_tensor(
            W[2].bitcast(F32)[:, :], xT[d][:, 0:1024], s_mD[:, d:d + 1],
            W[2].bitcast(F32)[:, :], ALU.mult, ALU.add),
            reads=xkeys(d, 0, 2) + ["mD"] + [("W", 2, c) for c in range(4)],
            writes=[("W", 2, c) for c in range(4)], tag=f"yD{d}a")
        S.op("v", lambda v, d=d: v.scalar_tensor_tensor(
            W[3].bitcast(F32)[:, :], xT[d][:, 1024:2048], s_mD[:, d:d + 1],
            W[3].bitcast(F32)[:, :], ALU.mult, ALU.add),
            reads=xkeys(d, 2, 4) + ["mD"] + [("W", 3, c) for c in range(4)],
            writes=[("W", 3, c) for c in range(4)], tag=f"yD{d}b")
        for ch in range(4):
            yv = W[2].bitcast(F32) if ch < 2 else W[3].bitcast(F32)
            sl = slice(512 * (ch % 2), 512 * (ch % 2 + 1))
            S.op("t", lambda t, yv=yv, sl=sl: t.matmul(
                P[2][0:1, 0:512], s_cb[:, CB_ONE:CB_ONE + 1], yv[:, sl],
                start=True, stop=True),
                reads=[("W", 2 + ch // 2, c) for c in range(4)] + ["s_cb"],
                writes=[("P", 2)], tag=f"yst{d}_{ch}")
            if d == 0:
                S.op("s", lambda sc, ch=ch: sc.activation(
                    s_stats[32:33, 512 * ch:512 * (ch + 1)], P[2][0:1, 0:512],
                    AF.Copy),
                    reads=[("P", 2)], writes=[("stats",)], tag=f"ysc{d}_{ch}")
            else:
                S.op("v", lambda v, ch=ch: v.tensor_tensor(
                    s_stats[32:33, 512 * ch:512 * (ch + 1)],
                    s_stats[32:33, 512 * ch:512 * (ch + 1)], P[2][0:1, 0:512],
                    ALU.add),
                    reads=[("P", 2), ("stats",)], writes=[("stats",)],
                    tag=f"ysa{d}_{ch}")
            S.op("s", lambda sc, yv=yv, sl=sl, ch=ch: sc.activation(
                s_aT[:, 512 * ch:512 * (ch + 1)], yv[:, sl], AF.Square),
                reads=[("W", 2 + ch // 2, c) for c in range(4)],
                writes=[("aT", ch)], tag=f"ysq{d}_{ch}")
            S.op("t", lambda t, ch=ch: t.matmul(
                P[3][0:1, 0:512], s_cb[:, CB_ONE:CB_ONE + 1],
                s_aT[:, 512 * ch:512 * (ch + 1)], start=True, stop=True),
                reads=[("aT", ch), "s_cb"], writes=[("P", 3)], tag=f"ys2{d}_{ch}")
            if d == 0:
                S.op("s", lambda sc, ch=ch: sc.activation(
                    s_stats[64:65, 512 * ch:512 * (ch + 1)], P[3][0:1, 0:512],
                    AF.Copy),
                    reads=[("P", 3)], writes=[("stats",)], tag=f"y2c{d}_{ch}")
            else:
                S.op("v", lambda v, ch=ch: v.tensor_tensor(
                    s_stats[64:65, 512 * ch:512 * (ch + 1)],
                    s_stats[64:65, 512 * ch:512 * (ch + 1)], P[3][0:1, 0:512],
                    ALU.add),
                    reads=[("P", 3), ("stats",)], writes=[("stats",)],
                    tag=f"y2a{d}_{ch}")
        kb.store("gd", kb.y_scr.ap()[d, :, 0:1024], W[2].bitcast(F32)[:, :],
                 reads=[("W", 2, c) for c in range(4)], key=("yscr", d, 0),
                 tag=f"yspill{d}a")
        kb.store("gd", kb.y_scr.ap()[d, :, 1024:2048], W[3].bitcast(F32)[:, :],
                 reads=[("W", 3, c) for c in range(4)], key=("yscr", d, 1),
                 tag=f"yspill{d}b")

    S.op("v", lambda v: v.tensor_scalar(s_stats[32:33, :], s_stats[32:33, :],
                                        1.0 / 1024.0, None, ALU.mult),
         reads=[("stats",) for c in range(4)], writes=[("stats",) for c in range(4)],
         tag="mu")
    for ch in range(4):
        S.op("s", lambda sc, ch=ch: sc.activation(
            P[ch][0:1, 0:512], s_stats[32:33, 512 * ch:512 * (ch + 1)], AF.Square,
            scale=s_cb[0:1, 2:3]),
            reads=[("stats",), "s_cb"], writes=[("P", ch)], tag=f"mu2_{ch}")
    S.op("v", lambda v: v.tensor_scalar(s_stats[64:65, :], s_stats[64:65, :],
                                        1.0 / 1023.0, None, ALU.mult),
         reads=[("stats",) for c in range(4)], writes=[("stats",) for c in range(4)],
         tag="sqs")
    for ch in range(4):
        S.op("v", lambda v, ch=ch: v.tensor_tensor(
            s_stats[64:65, 512 * ch:512 * (ch + 1)],
            s_stats[64:65, 512 * ch:512 * (ch + 1)], P[ch][0:1, 0:512], ALU.subtract),
            reads=[("stats",), ("P", ch)], writes=[("stats",)],
            tag=f"var{ch}")
    S.op("s", lambda sc: sc.activation(s_stats[64:65, :], s_stats[64:65, :], AF.Sqrt),
         reads=[("stats",) for c in range(4)],
         writes=[("stats",) for c in range(4)], tag="std")
    S.op("v", lambda v: v.tensor_scalar(s_stats[64:65, :], s_stats[64:65, :],
                                        1e-12, None, ALU.add),
         reads=[("stats",) for c in range(4)],
         writes=[("stats",) for c in range(4)], tag="stde")
    S.op("v", lambda v: v.reciprocal(s_stats[64:65, :], s_stats[64:65, :]),
         reads=[("stats",) for c in range(4)],
         writes=[("stats",) for c in range(4)], tag="dinv")
    for ch in range(4):
        S.op("t", lambda t, ch=ch: t.matmul(
            P[ch][:, 0:512], s_ones[32:33, :], s_stats[32:33, 512 * ch:512 * (ch + 1)],
            start=True, stop=True),
            reads=[("stats",), "s_ones"], writes=[("P", ch)], tag=f"bmu{ch}")
        S.op("t", lambda t, ch=ch: t.matmul(
            P[4 + ch][:, 0:512], s_ones[64:65, :], s_stats[64:65, 512 * ch:512 * (ch + 1)],
            start=True, stop=True),
            reads=[("stats",), "s_ones"], writes=[("P", 4 + ch)], tag=f"bdi{ch}")
    for d in range(8):
        kb.load("gd", s_h32[:, 0:1024], kb.y_scr.ap()[d, :, 0:1024],
                ("h32",), reads=[("yscr", d, 0)], tag=f"yld{d}a")
        kb.load("gd", s_h32[:, 1024:2048], kb.y_scr.ap()[d, :, 1024:2048],
                ("h32",), reads=[("yscr", d, 1)], tag=f"yld{d}b")
        for ch in range(4):
            S.op("v", lambda v, ch=ch: v.tensor_tensor(
                s_delta[:, 512 * ch:512 * (ch + 1)],
                s_h32[:, 512 * ch:512 * (ch + 1)],
                P[ch][:, 0:512], ALU.subtract),
                reads=[("h32",), ("P", ch)], writes=[("delta", ch)],
                tag=f"ap1{d}_{ch}")
            S.op("v", lambda v, ch=ch: v.tensor_tensor(
                s_aT[:, 512 * ch:512 * (ch + 1)], s_delta[:, 512 * ch:512 * (ch + 1)],
                P[4 + ch][:, 0:512], ALU.mult),
                reads=[("delta", ch), ("P", 4 + ch)], writes=[("aT", ch)],
                tag=f"ap2{d}_{ch}")
        S.op("v", lambda v, d=d: v.scalar_tensor_tensor(
            xT[d][:, :], s_aT[:, :], s_lna[:, d:d + 1], xT[d][:, :],
            ALU.mult, ALU.add),
            reads=[("aT", c) for c in range(4)] + ["lna"] + xkeys(d),
            writes=xkeys(d), tag=f"ap3{d}")
        S.op("v", lambda v, d=d: v.tensor_scalar(
            xT[d][:, :], xT[d][:, :], s_lnb[:, d:d + 1], None, ALU.add),
            reads=["lnb"] + xkeys(d), writes=xkeys(d), tag=f"ap4{d}")

    build_topln(kb, tag="ln1")
    mstack.close()


def build_topln(kb, tag):
    S = kb.S
    xT, W, P = kb.xT, kb.W, kb.P
    s_cb, s_cbb, s_ones, s_stats = kb.s_cb, kb.s_cbb, kb.s_ones, kb.s_stats
    for ch in range(4):
        for d in range(8):
            S.op("s", lambda sc, d=d, ch=ch: sc.activation(
                W[4][:, 0:512], xT[d][:, 512 * ch:512 * (ch + 1)], AF.Copy),
                reads=[("xT", d, ch)], writes=[("W", 4, 0)], tag=f"{tag}c{ch}_{d}")
            S.op("s", lambda sc, d=d, ch=ch: sc.activation(
                W[5][:, 0:512], xT[d][:, 512 * ch:512 * (ch + 1)], AF.Square),
                reads=[("xT", d, ch)], writes=[("W", 5, 0)], tag=f"{tag}q{ch}_{d}")
            S.op("t", lambda t, d=d, ch=ch: t.matmul(
                P[ch][0:1, 0:512], s_cbb[:, 0:1], W[4][:, 0:512],
                start=(d == 0), stop=(d == 7)),
                reads=[("W", 4, 0), "s_cbb"], writes=[("P", ch)], tag=f"{tag}s{ch}_{d}")
            S.op("t", lambda t, d=d, ch=ch: t.matmul(
                P[4 + ch][0:1, 0:512], s_cbb[:, 0:1], W[5][:, 0:512],
                start=(d == 0), stop=(d == 7)),
                reads=[("W", 5, 0), "s_cbb"], writes=[("P", 4 + ch)],
                tag=f"{tag}s2{ch}_{d}")
    for ch in range(4):
        S.op("s", lambda sc, ch=ch: sc.activation(
            s_stats[32:33, 512 * ch:512 * (ch + 1)], P[ch][0:1, 0:512], AF.Copy),
            reads=[("P", ch)], writes=[("stats",)], tag=f"{tag}e{ch}")
        S.op("s", lambda sc, ch=ch: sc.activation(
            s_stats[64:65, 512 * ch:512 * (ch + 1)], P[4 + ch][0:1, 0:512], AF.Copy),
            reads=[("P", 4 + ch)], writes=[("stats",)], tag=f"{tag}e2{ch}")
    S.op("v", lambda v: v.tensor_scalar(s_stats[32:33, :], s_stats[32:33, :],
                                        1.0 / 1024.0, None, ALU.mult),
         reads=[("stats",) for c in range(4)],
         writes=[("stats",) for c in range(4)], tag=f"{tag}mu")
    for ch in range(4):
        S.op("s", lambda sc, ch=ch: sc.activation(
            P[ch][0:1, 0:512], s_stats[32:33, 512 * ch:512 * (ch + 1)], AF.Square),
            reads=[("stats",)], writes=[("P", ch)], tag=f"{tag}mu2_{ch}")
    S.op("v", lambda v: v.tensor_scalar(s_stats[64:65, :], s_stats[64:65, :],
                                        1.0 / 1024.0, None, ALU.mult),
         reads=[("stats",) for c in range(4)],
         writes=[("stats",) for c in range(4)], tag=f"{tag}ex2")
    for ch in range(4):
        S.op("v", lambda v, ch=ch: v.tensor_tensor(
            s_stats[64:65, 512 * ch:512 * (ch + 1)],
            s_stats[64:65, 512 * ch:512 * (ch + 1)], P[ch][0:1, 0:512], ALU.subtract),
            reads=[("stats",), ("P", ch)], writes=[("stats",)],
            tag=f"{tag}var{ch}")
    S.op("s", lambda sc: sc.activation(s_stats[64:65, :], s_stats[64:65, :],
                                       AF.Sqrt, bias=s_cb[0:1, CB_EPS5:CB_EPS5 + 1]),
         reads=[("stats",) for c in range(4)] + ["s_cb"],
         writes=[("stats",) for c in range(4)], tag=f"{tag}sd")
    S.op("v", lambda v: v.reciprocal(s_stats[64:65, :], s_stats[64:65, :]),
         reads=[("stats",) for c in range(4)],
         writes=[("stats",) for c in range(4)], tag=f"{tag}rs")
    for ch in range(4):
        S.op("t", lambda t, ch=ch: t.matmul(
            P[ch][:, 0:512], s_ones[32:33, :], s_stats[32:33, 512 * ch:512 * (ch + 1)],
            start=True, stop=True),
            reads=[("stats",), "s_ones"], writes=[("P", ch)], tag=f"{tag}bm{ch}")
        S.op("t", lambda t, ch=ch: t.matmul(
            P[4 + ch][:, 0:512], s_ones[64:65, :], s_stats[64:65, 512 * ch:512 * (ch + 1)],
            start=True, stop=True),
            reads=[("stats",), "s_ones"], writes=[("P", 4 + ch)],
            tag=f"{tag}br{ch}")
    for d in range(8):
        for ch in range(4):
            S.op("v", lambda v, d=d, ch=ch: v.tensor_tensor(
                xT[d][:, 512 * ch:512 * (ch + 1)], xT[d][:, 512 * ch:512 * (ch + 1)],
                P[ch][:, 0:512], ALU.subtract),
                reads=[("xT", d, ch), ("P", ch)], writes=[("xT", d, ch)],
                tag=f"{tag}a{d}_{ch}")
            S.op("v", lambda v, d=d, ch=ch: v.tensor_tensor(
                xT[d][:, 512 * ch:512 * (ch + 1)], xT[d][:, 512 * ch:512 * (ch + 1)],
                P[4 + ch][:, 0:512], ALU.mult),
                reads=[("xT", d, ch), ("P", 4 + ch)], writes=[("xT", d, ch)],
                tag=f"{tag}b{d}_{ch}")


def _host_prep(inputs):
    x = np.ascontiguousarray(inputs["L0_attn_x"])
    ident = np.eye(128, dtype=np.float32)
    ones_row = np.ones((128, 128), np.float32)
    cbank = np.zeros((128, 8), np.float32)
    cbank[:, CB_ONE] = 1.0
    cbank[:, CB_EPS5] = 1e-5
    cbank[:, 2] = np.sqrt(1024.0 / 1023.0)
    cbank[:, CB_I127] = 1.0 / 127.0
    selBC = np.zeros((32, 32 * 128), np.float32)
    for n in range(32):
        selBC[n, 128 * n:128 * (n + 1)] = 1.0

    base = {
        "m_xprojT": np.ascontiguousarray(inputs["L0_attn_x_proj_w"].T),
        "m_dtwT": np.ascontiguousarray(inputs["L0_attn_dt_proj_w"].T),
        "m_dtb": np.ascontiguousarray(inputs["L0_attn_dt_proj_b"].reshape(8, 128).T),
        "m_Alog": np.ascontiguousarray(
            inputs["L0_attn_A_log"].reshape(8, 128, 16).transpose(1, 0, 2).reshape(128, 128)),
        "m_D": np.ascontiguousarray(inputs["L0_attn_D_param"].reshape(8, 128).T),
        "m_lna": np.ascontiguousarray(inputs["L0_attn_ln_a"].reshape(8, 128).T),
        "m_lnb": np.ascontiguousarray(inputs["L0_attn_ln_b"].reshape(8, 128).T),
        "identity": ident,
        "ones_row": ones_row,
        "cbank": cbank,
        "selBC": selBC,
    }
    NEGC = np.float32(-1e30)
    W1 = inputs["L0_ffn_W1"]
    W2 = inputs["L0_ffn_W2"]
    b1 = inputs["L0_ffn_b1"]
    w1p = np.zeros((6, D, 5632), np.float32)
    w1p[:, :, 0:2730] = W1[:, :, 0:2730]
    w1p[:, :, 2816:5546] = W1[:, :, 2730:5460]
    w2p = np.zeros((6, 2816, D), np.float32)
    w2p[:, 0:2730, :] = W2
    b1p = np.zeros((6, 5632), np.float32)
    b1p[:, 0:2730] = b1[:, 0:2730]
    b1p[:, 2816:5546] = b1[:, 2730:5460]
    qs = inputs["L0_ffn_q_scale"]
    ks = inputs["L0_ffn_k_scale"]
    base.update({
        "f_wqkv": np.ascontiguousarray(inputs["L0_ffn_Wqkv"]),
        "f_wo": np.ascontiguousarray(inputs["L0_ffn_Wo"]),
        "f_w1": w1p,
        "f_w2": w2p,
        "f_b1": np.ascontiguousarray(b1p.reshape(6, 44, 128).transpose(0, 2, 1)),
        "f_lng": np.ascontiguousarray(inputs["L0_ffn_ln_g"].reshape(6, 8, 128).transpose(0, 2, 1)),
        "f_lnb2": np.ascontiguousarray(inputs["L0_ffn_ln_b"].reshape(6, 8, 128).transpose(0, 2, 1)),
        "f_b2": np.ascontiguousarray(inputs["L0_ffn_b2"].reshape(6, 8, 128).transpose(0, 2, 1)),
        "f_qs8": np.ascontiguousarray((8.0 * np.concatenate([qs, qs], 1))[:, :, None]),
        "f_ks": np.ascontiguousarray(np.concatenate([ks, ks], 1)[:, :, None]),
    })
    selE2 = np.zeros((128, 16 * 64), np.float32)
    selN = np.zeros((64, 16 * 128), np.float32)
    for p in range(8):
        for side in range(2):
            bidx = p + 8 * side
            for k in range(128):
                selE2[k, 64 * bidx + 32 * side + 2 * p + (k >= 64)] = 1.0
            for m in range(128):
                selN[32 * side + 2 * p + (m >= 64), 128 * bidx + m] = 1.0
    base["f_selE2"] = selE2
    base["f_selN"] = selN
    base["f_l1w"] = np.ascontiguousarray(inputs["L1_attn_w_qkv"])
    base["f_l1proj"] = np.ascontiguousarray(inputs["L1_attn_w_proj"])
    base["f_l1bq"] = np.ascontiguousarray(
        inputs["L1_attn_b_qkv"].reshape(24, 128).T)
    base["f_l1bp"] = np.ascontiguousarray(
        inputs["L1_attn_b_proj"].reshape(8, 128).T)
    tbl = inputs["L1_attn_bias_table"]
    coords = np.stack(np.meshgrid(np.arange(8), np.arange(8), indexing="ij"))
    cf = coords.reshape(2, -1)
    rel = cf[:, :, None] - cf[:, None, :]
    rel = rel.transpose(1, 2, 0).astype(np.int64)
    rel[..., 0] += 7
    rel[..., 1] += 7
    rel[..., 0] *= 15
    idx = rel.sum(-1)[:8, :8]
    l1b = np.full((128, 2048), -1e30, np.float32)
    for h in range(16):
        bias8 = tbl[idx, h]
        for a in range(16):
            l1b[8 * a:8 * a + 8, 128 * h + 8 * a:128 * h + 8 * a + 8] = bias8.T
    base["f_l1bias"] = l1b
    base["f_wgT"] = np.ascontiguousarray(inputs["L1_ffn_w_gate"].T)
    base["f_wuT"] = np.ascontiguousarray(inputs["L1_ffn_w_up"].T)
    base["f_wdT"] = np.ascontiguousarray(inputs["L1_ffn_w_down"].T)
    jj, ii = np.meshgrid(np.arange(128), np.arange(128), indexing="ij")
    mku = np.where(jj >= ii, 0.0, NEGC).astype(np.float32)
    mkl = np.where(jj <= ii, 0.0, NEGC).astype(np.float32)
    allneg = np.full((128, 128), NEGC, np.float32)
    zeros128 = np.zeros((128, 128), np.float32)
    base["f_maskup"] = mku
    base["f_masklo"] = mkl

    def rope_tabs(tglob):
        half = np.arange(0, 64, 2, dtype=np.float64)
        inv_freq = 1.0 / 10000.0 ** (half / 64.0)
        fr = tglob[:, None] * inv_freq[None, :]
        fr = np.concatenate([fr, fr], 1)
        base_scale = (half + 0.4 * 64) / (1.4 * 64)
        power = (tglob - 8192 // 2) / (256 // 2)
        sc = base_scale[None, :] ** power[:, None]
        sc = np.concatenate([sc, sc], 1)
        cs, sn = np.cos(fr), np.sin(fr)
        sgn = np.ones(64); sgn[:32] = -1.0
        Aq = (cs * sc).T
        Bq = (sn * sc * sgn[None, :]).T
        inv = 1.0 / sc
        Ak = (cs * inv).T
        Bk = (sn * inv * sgn[None, :]).T
        dup = lambda M: np.concatenate([M, M], 0).astype(np.float32)
        return dup(Aq), dup(Bq), dup(Ak), dup(Bk)

    in_maps = []
    for c in range(NCORES):
        b, r = divmod(c, 4)
        m = dict(base)
        m["x_tok"] = np.ascontiguousarray(x[b, TC * r:TC * (r + 1), :])
        ccp = np.zeros((128, 4), np.float32)
        ccp[:, :r] = 1.0
        m["ccpre"] = ccp
        ccm = np.zeros((128, 4), np.float32)
        if r > 0:
            ccm[:, r - 1] = 1.0
        m["f_ccm"] = ccm
        tglob = np.arange(TC * r, TC * (r + 1), dtype=np.float64)
        qA, qB, kA, kB = rope_tabs(tglob)
        m["f_ropeqA"], m["f_ropeqB"] = qA, qB
        m["f_ropekA"], m["f_ropekB"] = kA, kB
        th = np.arange(TC * r - 256, TC * r, dtype=np.float64)
        _, _, kAh, kBh = rope_tabs(th)
        m["f_ropekAh"], m["f_ropekBh"] = kAh, kBh
        ccn = np.zeros((128, 4), np.float32)
        ccn[:, (r + 1) % 4] = 1.0
        m["f_ccn"] = ccn
        ccw = np.zeros((128, 4), np.float32)
        ccw[:, (r - 1) % 4] = 1.0
        m["f_ccpw"] = ccw
        m["f_m00"] = allneg if r == 0 else mku
        m["f_m01"] = allneg if r == 0 else zeros128
        m["f_m10"] = allneg if r == 0 else mku
        in_maps.append(m)
    return in_maps


def kernel(**inputs):
    in_maps = _host_prep(inputs)
    nc, stats = build_nc(PHASE_LIMIT)
    res = run_bass_kernel_spmd(nc, in_maps, core_ids=list(range(NCORES)))
    out = np.empty((B, L, D), np.float32)
    for c in range(NCORES):
        b, r = divmod(c, 4)
        out[b, TC * r:TC * (r + 1), :] = res.results[c]["y_tok"]
    return out


def build_l0ffn(kb, groups4, n_layers):
    nc, S = kb.nc, kb.S
    xT, W, P = kb.xT, kb.W, kb.P
    pA, pB = kb.pA, kb.pB
    s_cb, s_cbb, s_ones, s_stats = kb.s_cb, kb.s_cbb, kb.s_ones, kb.s_stats
    s_delta, s_aT = kb.s_delta, kb.s_aT
    sb = kb.sb
    STK = ("stats",)
    DK = lambda c: ("delta", c)
    AK = lambda c: ("aT", c)

    s_v = kb.s_v = sb("s_v", [128, 8 * 1040], BF16)
    s_wbuf = kb.s_wbuf = sb("s_wbuf", [128, 2816], BF16)
    s_zinv = kb.s_zinv = sb("s_zinv", [128, 16])
    l0stack = ExitStack()
    sb = lambda name, shape, dtype=F32: l0stack.enter_context(
        nc.sbuf_tensor(name, shape, dtype))
    s_khalo = sb("s_khalo", [128, 8 * 256], BF16)
    s_kcar = s_khalo
    s_xcol = sb("s_xcol", [128, 8])
    s_rqA = sb("s_rqA", [128, TC], BF16)
    s_rqB = sb("s_rqB", [128, TC], BF16)
    s_rkA = sb("s_rkA", [128, TC], BF16)
    s_rkB = sb("s_rkB", [128, TC], BF16)
    s_rkAh = sb("s_rkAh", [128, 256], BF16)
    s_rkBh = sb("s_rkBh", [128, 256], BF16)
    s_mku = sb("s_mku", [128, 128], BF16)
    s_mkl = sb("s_mkl", [128, 128], BF16)
    s_m00 = sb("s_m00", [128, 128], BF16)
    s_m01 = sb("s_m01", [128, 128], BF16)
    s_m10 = sb("s_m10", [128, 128], BF16)
    s_selE2 = sb("s_selE2", [128, 16 * 64], BF16)
    s_selN = sb("s_selN", [64, 16 * 128], BF16)
    s_ninv = sb("s_ninv", [64, TC], BF16)
    s_b1 = sb("s_b1", [128, 44])
    s_lng = sb("s_lng", [128, 8])
    s_lnbv = sb("s_lnbv", [128, 8])
    s_b2v = sb("s_b2v", [128, 8])
    s_qs8 = sb("s_qs8", [128, 1])
    s_ksc = sb("s_ksc", [128, 1])
    s_ccm = sb("s_ccm", [128, 4])

    ag_kv_in = nc.dram_tensor("ag_kv_in", [1, 8 * 128 * 256 + 128 * 4], BF16)
    ag_kv_out = nc.dram_tensor("ag_kv_out", [4, 8 * 128 * 256 + 128 * 4], BF16)
    ag_x2_in = nc.dram_tensor("ag_x2_in", [1, 128 * 4], F32)
    ag_x2_out = nc.dram_tensor("ag_x2_out", [4, 128 * 4], F32)

    FENCE = [("xT", d, c) for d in range(8) for c in range(4)]
    for nm, dst in [("f_ropeqA", s_rqA), ("f_ropeqB", s_rqB), ("f_ropekA", s_rkA),
                    ("f_ropekB", s_rkB), ("f_ropekAh", s_rkAh), ("f_ropekBh", s_rkBh),
                    ("f_maskup", s_mku), ("f_masklo", s_mkl), ("f_m00", s_m00),
                    ("f_m01", s_m01), ("f_m10", s_m10), ("f_selE2", s_selE2),
                    ("f_selN", s_selN), ("f_ccm", s_ccm)]:
        kb.load("gd", dst[:, :], kb.ap[nm], (nm,), reads=FENCE)
    S.checkpoint("gd")

    for t8 in range(8):
        S.op("v", lambda v, t8=t8: v.memset(
            s_v[:, 1040 * t8:1040 * (t8 + 1)]
            .rearrange("p (h e) -> p h e", e=65)[:, :, 64:65], 1.0),
            reads=FENCE, writes=[("sv", t8)], tag=f"vone{t8}")

    WBH = lambda par: [("wb", 2 * par), ("wb", 2 * par + 1)]

    def wload(tag, dram2d, par):
        kb.load("gd", s_wbuf[:, 1024 * par:1024 * (par + 1)]
                .rearrange("p (k c) -> p k c", k=8),
                dram2d.rearrange("(k p) c -> p k c", p=128),
                WBH(par), tag=tag)

    def ain(d):
        sl = W[d % 4][:, 1024 * (d // 4):1024 * (d // 4) + 1024]
        return sl, [("W", d % 4, 2 * (d // 4) + i) for i in range(2)]

    def qsl(p):
        return W[8 + p // 2][:, 1024 * (p % 2):1024 * (p % 2) + 1024]

    def qkeys(p):
        return [("W", 8 + p // 2, 2 * (p % 2) + i) for i in range(2)]

    def ksl(p):
        return W[12 + p // 2][:, 1024 * (p % 2):1024 * (p % 2) + 1024]

    def kkeys(p):
        return [("W", 12 + p // 2, 2 * (p % 2) + i) for i in range(2)]

    s_halo = W[4]
    HALOK = [("W", 4, c) for c in range(4)]
    otok = W[4]
    OTK = HALOK
    s_attnT = W[5]
    ATK = [("W", 5, c) for c in range(4)]

    def vhalo_ap(t2, h):
        return W[6 + t2][:, 65 * h:65 * h + 65]

    def vhk(t2):
        return [("W", 6 + t2, c) for c in range(3)]

    def emit_ain_half(l, hh, pref):
        c0 = 1024 * hh
        for d in range(8):
            sl, keys = ain(d)
            if d < 4:
                S.op("s", lambda sc, sl=sl, d=d, c0=c0: sc.activation(
                    sl[:, :], xT[d][:, c0:c0 + 1024], AF.Copy),
                    reads=[("xT", d, 2 * hh), ("xT", d, 2 * hh + 1)],
                    writes=keys, tag=f"{pref}ain{d}")
            else:
                S.op("s", lambda sc, sl=sl, d=d, c0=c0: sc.activation(
                    sl[:, 1:1024], xT[d][:, c0:c0 + 1023], AF.Copy),
                    reads=[("xT", d, 2 * hh), ("xT", d, 2 * hh + 1)],
                    writes=keys, tag=f"{pref}ain{d}")
                S.op("v", lambda v, sl=sl, d=d, hh=hh: v.tensor_copy(
                    sl[:, 0:1], s_xcol[:, 4 * hh + d - 4:4 * hh + d - 3]),
                    reads=[("xcol",)], writes=[keys[0]], tag=f"{pref}c0{d}")

    def rot_pair(l, tagp, src, skeys, scc, sck, bcast_ap, bck, tA, tAk, tB, tBk,
                 scr, klo, khi, ncols):
        h = ncols
        S.op("v", lambda v: v.scalar_tensor_tensor(
            scr[:, 0:h], src[:, :], scc[:, 0:1], bcast_ap, ALU.mult, ALU.mult),
            reads=skeys + [sck] + bck, writes=klo, tag=f"{tagp}qn")
        for qq, qs_ in [(0, 32), (32, 0), (64, 96), (96, 64)]:
            S.op("s", lambda sc, qq=qq, qs_=qs_: sc.activation(
                scr[qq:qq + 32, h:2 * h], scr[qs_:qs_ + 32, 0:h], AF.Copy),
                reads=klo, writes=khi, tag=f"{tagp}sw{qq}")
        S.op("v", lambda v: v.tensor_tensor(scr[:, h:2 * h], scr[:, h:2 * h],
                                            tB, ALU.mult),
             reads=khi + [tBk], writes=khi, tag=f"{tagp}rB")
        S.op("v", lambda v: v.tensor_tensor(src[:, :], scr[:, 0:h], tA, ALU.mult),
             reads=klo + [tAk], writes=skeys, tag=f"{tagp}rA")
        S.op("v", lambda v: v.tensor_tensor(src[:, :], src[:, :], scr[:, h:2 * h],
                                            ALU.add),
             reads=skeys + khi, writes=skeys, tag=f"{tagp}rS")

    def qk_prep(l, hh):
        c0 = 1024 * hh
        for p in range(8):
            for side in range(2):
                src = qsl(p) if side == 0 else ksl(p)
                keys = qkeys(p) if side == 0 else kkeys(p)
                scr = otok if (2 * p + side) % 2 == 0 else s_attnT
                sk = OTK if (2 * p + side) % 2 == 0 else ATK
                S.op("s", lambda sc, src=src, scr=scr: sc.activation(
                    scr[:, 0:1024], src[:, :], AF.Square),
                    reads=keys, writes=sk[:2], tag=f"L{l}h{hh}sq{side}_{p}")
                for ch in range(2):
                    S.op("t", lambda t, p=p, side=side, ch=ch, scr=scr: t.matmul(
                        pB[0:64, 512 * ch:512 * (ch + 1)],
                        s_selE2[:, 64 * (p + 8 * side):64 * (p + 8 * side) + 64],
                        scr[:, 512 * ch:512 * ch + 512],
                        start=(p == 0 and side == 0), stop=(p == 7 and side == 1)),
                        reads=[sk[ch], ("f_selE2",)], writes=[("P", 4 + ch)],
                        tag=f"L{l}h{hh}nm{side}_{p}_{ch}")
        S.op("s", lambda sc: sc.activation(s_stats[0:64, c0:c0 + 1024],
                                           pB[0:64, 0:1024], AF.Sqrt),
             reads=[("P", 4), ("P", 5)], writes=[STK], tag=f"L{l}h{hh}nsq")
        S.op("v", lambda v: v.tensor_scalar(s_stats[0:64, c0:c0 + 1024],
                                            s_stats[0:64, c0:c0 + 1024],
                                            1e-12, None, ALU.max),
             reads=[STK], writes=[STK], tag=f"L{l}h{hh}nmx")
        S.op("v", lambda v: v.reciprocal(s_stats[0:64, c0:c0 + 1024],
                                         s_stats[0:64, c0:c0 + 1024]),
             reads=[STK], writes=[STK], tag=f"L{l}h{hh}nrc")
        S.op("v", lambda v: v.tensor_copy(s_ninv[:, c0:c0 + 1024],
                                          s_stats[0:64, c0:c0 + 1024]),
             reads=[STK], writes=[("ninv",)], tag=f"L{l}h{hh}ncp")
        for p in range(8):
            for side in range(2):
                idx = 2 * p + side
                src = qsl(p) if side == 0 else ksl(p)
                keys = qkeys(p) if side == 0 else kkeys(p)
                tA = (s_rqA if side == 0 else s_rkA)[:, c0:c0 + 1024]
                tB = (s_rqB if side == 0 else s_rkB)[:, c0:c0 + 1024]
                tAk = ("f_ropeqA",) if side == 0 else ("f_ropekA",)
                tBk = ("f_ropeqB",) if side == 0 else ("f_ropekB",)
                scc = s_qs8 if side == 0 else s_ksc
                sck = "lv_qs" if side == 0 else "lv_ks"
                scr = otok if idx % 2 == 0 else s_attnT
                scrk = OTK if idx % 2 == 0 else ATK
                pbk = 2 * (idx % 2)
                S.op("t", lambda t, p=p, side=side, pbk=pbk: t.matmul(
                    pB[:, 512 * pbk:512 * pbk + 512],
                    s_selN[:, 128 * (p + 8 * side):128 * (p + 8 * side) + 128],
                    s_ninv[:, c0:c0 + 512], start=True, stop=True),
                    reads=[("ninv",), ("f_selN",)], writes=[("P", 4 + pbk)],
                    tag=f"L{l}h{hh}nb{idx}_0")
                S.op("t", lambda t, p=p, side=side, pbk=pbk: t.matmul(
                    pB[:, 512 * (pbk + 1):512 * (pbk + 1) + 512],
                    s_selN[:, 128 * (p + 8 * side):128 * (p + 8 * side) + 128],
                    s_ninv[:, c0 + 512:c0 + 1024], start=True, stop=True),
                    reads=[("ninv",), ("f_selN",)], writes=[("P", 4 + pbk + 1)],
                    tag=f"L{l}h{hh}nb{idx}_1")
                rot_pair(l, f"L{l}h{hh}p{idx}", src, keys, scc, sck,
                         pB[:, 512 * pbk:512 * pbk + 1024],
                         [("P", 4 + pbk), ("P", 5 + pbk)], tA, tAk, tB, tBk,
                         scr, scrk[:2], scrk[2:], 1024)

    def halo_k_prep(l):
        for p in range(8):
            S.op("s", lambda sc, p=p: sc.activation(
                s_attnT[:, 0:256], s_khalo[:, 256 * p:256 * (p + 1)], AF.Square),
                reads=[("khalo",)], writes=[ATK[0]], tag=f"L{l}hsq{p}")
            S.op("t", lambda t, p=p: t.matmul(
                pB[0:64, 0:256],
                s_selE2[:, 64 * (p + 8):64 * (p + 8) + 64],
                s_attnT[:, 0:256], start=(p == 0), stop=(p == 7)),
                reads=[ATK[0], ("f_selE2",)], writes=[("P", 4)],
                tag=f"L{l}hnm{p}")
        S.op("s", lambda sc: sc.activation(s_stats[0:64, 0:256],
                                           pB[0:64, 0:256], AF.Sqrt),
             reads=[("P", 4)], writes=[STK], tag=f"L{l}hnsq")
        S.op("v", lambda v: v.tensor_scalar(s_stats[0:64, 0:256],
                                            s_stats[0:64, 0:256], 1e-12, None,
                                            ALU.max),
             reads=[STK], writes=[STK], tag=f"L{l}hnmx")
        S.op("v", lambda v: v.reciprocal(s_stats[0:64, 0:256], s_stats[0:64, 0:256]),
             reads=[STK], writes=[STK], tag=f"L{l}hnrc")
        S.op("v", lambda v: v.tensor_copy(s_ninv[:, 0:256], s_stats[0:64, 0:256]),
             reads=[STK], writes=[("ninvh",)], tag=f"L{l}hncp")
        for p in range(8):
            S.op("t", lambda t, p=p: t.matmul(
                pB[:, 1024:1280],
                s_selN[:, 128 * (p + 8):128 * (p + 8) + 128],
                s_ninv[:, 0:256], start=True, stop=True),
                reads=[("ninvh",), ("f_selN",)], writes=[("P", 6)],
                tag=f"L{l}hnb{p}")
            rot_pair(l, f"L{l}hp{p}",
                     s_khalo[:, 256 * p:256 * (p + 1)], [("khalo",)],
                     s_ksc, "lv_ks", pB[:, 1024:1280], [("P", 6)],
                     s_rkAh[:, :], ("f_ropekAh",), s_rkBh[:, :], ("f_ropekBh",),
                     s_attnT, [ATK[0]], [ATK[0]], 256)

    for l in range(n_layers):
        Lw = kb.ap["f_wqkv"][l]
        Lo = kb.ap["f_wo"][l]
        L1w = kb.ap["f_w1"][l]
        L2w = kb.ap["f_w2"][l]
        for nm, dst, key in [("f_b1", s_b1, "lv_b1"), ("f_lng", s_lng, "lv_g"),
                             ("f_lnb2", s_lnbv, "lv_b"), ("f_b2", s_b2v, "lv_b2"),
                             ("f_qs8", s_qs8, "lv_qs"), ("f_ks", s_ksc, "lv_ks")]:
            kb.load("gd", dst[:, :], kb.ap[nm][l], key,
                    reads=FENCE if l == 0 else [], tag=f"L{l}{key}")

        for d in range(8):
            if d < 4:
                S.op("s", lambda sc, d=d: sc.activation(
                    s_khalo[:, 256 * d:256 * (d + 1)], xT[d][:, 1792:2048], AF.Copy),
                    reads=[("xT", d, 3)], writes=[("khalo",)], tag=f"L{l}tl{d}")
            else:
                S.op("s", lambda sc, d=d: sc.activation(
                    s_khalo[:, 256 * d:256 * (d + 1)], xT[d][:, 1791:2047], AF.Copy),
                    reads=[("xT", d, 3)], writes=[("khalo",)], tag=f"L{l}tl{d}")
        for i, d in enumerate(range(4, 8)):
            S.op("s", lambda sc, d=d, i=i: sc.activation(
                s_attnT[:, i:i + 1], xT[d][:, TC - 1:TC], AF.Copy),
                reads=[("xT", d, 3)], writes=[ATK[0]], tag=f"L{l}xc{i}")
            S.op("v", lambda v, d=d, i=i: v.tensor_copy(
                s_xcol[:, 4 + i:5 + i], xT[d][:, 1023:1024]),
                reads=[("xT", d, 1)], writes=[("xcol",)], tag=f"L{l}xm{i}")
        kb.store("gd", ag_kv_in.ap()[0, 0:8 * 128 * 256]
                 .rearrange("(p c) -> p c", p=128),
                 s_khalo[:, :], reads=[("khalo",)], key=("agkv",), tag=f"L{l}agst")
        kb.store("gd", ag_kv_in.ap()[0, 8 * 128 * 256:]
                 .rearrange("(p c) -> p c", p=128),
                 s_attnT[:, 0:4], reads=[ATK[0]], key=("agkvx",), tag=f"L{l}agstx")
        S.checkpoint("gd")
        S.cc(lambda g: g.collective_compute(
            "AllGather", ALU.bypass, replica_groups=groups4,
            ins=[ag_kv_in.ap()[:, :]], outs=[ag_kv_out.ap()[:, :]]),
            reads=[("agkv",), ("agkvx",)], writes=["ag_kv_out"], tag=f"L{l}agkv")
        S.op("v", lambda v: v.memset(s_halo[:, :], 0.0), writes=HALOK, tag=f"L{l}hz")
        S.op("v", lambda v: v.memset(s_xcol[:, 0:4], 0.0), writes=[("xcol",)],
             tag=f"L{l}xz")
        for j in range(4):
            kb.load("gd", s_attnT[:, :],
                    ag_kv_out.ap()[j, 0:8 * 128 * 256]
                    .rearrange("(p c) -> p c", p=128),
                    list(ATK), reads=["ag_kv_out"], tag=f"L{l}agld{j}")
            kb.load("gd", s_khalo[:, 2044:2048],
                    ag_kv_out.ap()[j, 8 * 128 * 256:]
                    .rearrange("(p c) -> p c", p=128),
                    ("khalo",), reads=["ag_kv_out"], tag=f"L{l}agldx{j}")
            S.op("v", lambda v, j=j: v.scalar_tensor_tensor(
                s_halo[:, :], s_attnT[:, :], s_ccm[:, j:j + 1], s_halo[:, :],
                ALU.mult, ALU.add),
                reads=list(ATK) + [("f_ccm",)] + HALOK, writes=HALOK,
                tag=f"L{l}hsel{j}")
            S.op("v", lambda v, j=j: v.scalar_tensor_tensor(
                s_xcol[:, 0:4], s_khalo[:, 2044:2048], s_ccm[:, j:j + 1],
                s_xcol[:, 0:4], ALU.mult, ALU.add),
                reads=[("khalo",), ("f_ccm",), ("xcol",)], writes=[("xcol",)],
                tag=f"L{l}xsel{j}")

        for p in range(8):
            par = p % 2
            wload(f"L{l}hkw{p}", Lw[:, 1024 + 128 * p:1024 + 128 * (p + 1)], par)
            fns = [lambda t, k=k, par=par: t.matmul(
                pB[:, 0:256],
                s_wbuf[:, 1024 * par + 128 * k:1024 * par + 128 * (k + 1)],
                s_halo[:, 256 * k:256 * (k + 1)],
                start=(k == 0), stop=(k == 7)) for k in range(8)]
            S.group("t", fns, reads=WBH(par) + HALOK, writes=[("P", 4)],
                    tag=f"L{l}hk{p}")
            S.op("s", lambda sc, p=p: sc.activation(
                s_khalo[:, 256 * p:256 * (p + 1)], pB[:, 0:256], AF.Copy),
                reads=[("P", 4)], writes=[("khalo",)], tag=f"L{l}hke{p}")
        for t2 in range(2):
            S.op("v", lambda v, t2=t2: v.memset(
                W[6 + t2][:, 0:1040].rearrange("p (h e) -> p h e", e=65)[:, :, 64:65],
                1.0),
                writes=vhk(t2), tag=f"L{l}vhone{t2}")
        def vw(k):
            if k < 4:
                return s_wbuf[:, 512 * k:512 * (k + 1)], ("wb", k)
            return W[5][:, 512 * (k - 4):512 * (k - 4) + 512], ATK[k - 4]
        for fc in range(2):
            for k in range(8):
                dst, key = vw(k)
                kb.load("gd", dst,
                        Lw[128 * k:128 * (k + 1),
                           2048 + 512 * fc:2048 + 512 * (fc + 1)],
                        key, tag=f"L{l}hvw{fc}_{k}")
            wrd = [vw(k)[1] for k in range(8)]
            for t2 in range(2):
                fns = [lambda t, t2=t2, k=k: t.matmul(
                    pB[:, 512 * (1 + t2):512 * (2 + t2)],
                    s_halo[:, 256 * k + 128 * t2:256 * k + 128 * t2 + 128],
                    vw(k)[0], start=(k == 0), stop=(k == 7))
                    for k in range(8)]
                S.group("t", fns, reads=wrd + HALOK,
                        writes=[("P", 5 + t2)], tag=f"L{l}hv{fc}_{t2}")
                S.op("s", lambda sc, t2=t2, fc=fc: sc.activation(
                    W[6 + t2][:, 520 * fc:520 * fc + 520]
                    .rearrange("p (h e) -> p h e", e=65)[:, :, 0:64],
                    pB[:, 512 * (1 + t2):512 * (2 + t2)]
                    .rearrange("p (h e) -> p h e", e=64), AF.Copy),
                    reads=[("P", 5 + t2)], writes=vhk(t2),
                    tag=f"L{l}hve{fc}_{t2}")
        halo_k_prep(l)

        for hh in range(2):
            c0 = 1024 * hh
            emit_ain_half(l, hh, f"L{l}h{hh}")
            for p in range(8):
                for side in range(2):
                    par = p % 2
                    wcol = (0 if side == 0 else 1024) + 128 * p
                    wload(f"L{l}h{hh}w{side}_{p}", Lw[:, wcol:wcol + 128], par)
                    dstsl = qsl(p) if side == 0 else ksl(p)
                    dkeys = qkeys(p) if side == 0 else kkeys(p)
                    for ch in range(2):
                        fns = []
                        rds = WBH(par)
                        for k in range(8):
                            asl, akeys = ain(k)
                            rds.append(akeys[ch])
                            fns.append(lambda t, k=k, par=par, ch=ch, asl=asl: t.matmul(
                                pA[:, 512 * ch:512 * (ch + 1)],
                                s_wbuf[:, 1024 * par + 128 * k:1024 * par + 128 * (k + 1)],
                                asl[:, 512 * ch:512 * ch + 512],
                                start=(k == 0), stop=(k == 7)))
                        S.group("t", fns, reads=rds, writes=[("P", ch)],
                                tag=f"L{l}h{hh}qk{side}_{p}_{ch}")
                        S.op("s", lambda sc, dstsl=dstsl, ch=ch: sc.activation(
                            dstsl[:, 512 * ch:512 * ch + 512],
                            pA[:, 512 * ch:512 * (ch + 1)], AF.Copy),
                            reads=[("P", ch)], writes=[dkeys[ch]],
                            tag=f"L{l}h{hh}qke{side}_{p}_{ch}")
            def vw(k):
                if k < 4:
                    return s_wbuf[:, 512 * k:512 * (k + 1)], ("wb", k)
                return W[5][:, 512 * (k - 4):512 * (k - 4) + 512], ATK[k - 4]
            for fc in range(2):
                for k in range(8):
                    dst, key = vw(k)
                    kb.load("gd", dst,
                            Lw[128 * k:128 * (k + 1),
                               2048 + 512 * fc:2048 + 512 * (fc + 1)],
                            key, tag=f"L{l}h{hh}vw{fc}_{k}")
                wrd = [vw(k)[1] for k in range(8)]
                for t8 in range(8):
                    fns = []
                    rds = list(wrd)
                    for k in range(8):
                        asl, akeys = ain(k)
                        rds.append(akeys[(128 * t8) // 512])
                        fns.append(lambda t, t8=t8, k=k, asl=asl: t.matmul(
                            pA[:, 512 * (t8 % 4):512 * (t8 % 4) + 512],
                            asl[:, 128 * t8:128 * t8 + 128],
                            vw(k)[0], start=(k == 0), stop=(k == 7)))
                    S.group("t", fns, reads=rds, writes=[("P", t8 % 4)],
                            tag=f"L{l}h{hh}v{fc}_{t8}")
                    S.op("s", lambda sc, t8=t8, fc=fc: sc.activation(
                        s_v[:, 1040 * t8 + 520 * fc:1040 * t8 + 520 * fc + 520]
                        .rearrange("p (h e) -> p h e", e=65)[:, :, 0:64],
                        pA[:, 512 * (t8 % 4):512 * (t8 % 4) + 512]
                        .rearrange("p (h e) -> p h e", e=64), AF.Copy),
                        reads=[("P", t8 % 4)], writes=[("sv", t8)],
                        tag=f"L{l}h{hh}ve{fc}_{t8}")
            qk_prep(l, hh)
            for sl_ in range(8):
                g = 8 * hh + sl_
                for hgp in range(4):
                    for hi in range(4):
                        h = 4 * hgp + hi
                        p, hp = h // 2, h % 2
                        qap = qsl(p)[64 * hp:64 * hp + 64, 128 * sl_:128 * sl_ + 128]
                        pc = 4 + 2 * (sl_ % 2) + (h % 2)
                        psim = pB[:, 512 * (pc - 4):512 * (pc - 4) + 384]
                        fns = []
                        rds = list(qkeys(p))
                        for jt in range(3):
                            gj = g - 2 + jt
                            if gj < 0:
                                kap = s_khalo[64 * hp:64 * hp + 64,
                                              256 * p + 128 * (gj + 2):
                                              256 * p + 128 * (gj + 2) + 128]
                                rds.append(("khalo",))
                            elif gj < 8 * hh:
                                kap = s_kcar[64 * hp:64 * hp + 64,
                                             256 * p + 128 * (gj - 6):
                                             256 * p + 128 * (gj - 6) + 128]
                                rds.append(("khalo",))
                            else:
                                lt = gj - 8 * hh
                                kap = ksl(p)[64 * hp:64 * hp + 64,
                                             128 * lt:128 * lt + 128]
                                rds += kkeys(p)
                            fns.append(lambda t, kap=kap, qap=qap, jt=jt, psim=psim:
                                       t.matmul(psim[:, 128 * jt:128 * (jt + 1)],
                                                kap, qap, start=(jt == 0), stop=False))
                        if g == 0:
                            mk = [(0, s_m00, ("f_m00",)), (1, s_m01, ("f_m01",)),
                                  (2, s_mkl, ("f_masklo",))]
                        elif g == 1:
                            mk = [(0, s_m10, ("f_m10",)), (2, s_mkl, ("f_masklo",))]
                        else:
                            mk = [(0, s_mku, ("f_maskup",)), (2, s_mkl, ("f_masklo",))]
                        for idx, (jt, mt, mkey) in enumerate(mk):
                            last = idx == len(mk) - 1
                            fns.append(lambda t, jt=jt, mt=mt, psim=psim, last=last:
                                       t.matmul(psim[:, 128 * jt:128 * (jt + 1)],
                                                kb.s_idb[:, :], mt[:, :],
                                                start=False, stop=last))
                            rds.append(mkey)
                        rds.append("s_idb")
                        S.group("t", fns, reads=rds, writes=[("P", pc)],
                                tag=f"L{l}g{g}h{h}sim")
                        ab = h % 2
                        S.op("s", lambda sc, psim=psim, ab=ab: sc.activation(
                            s_attnT[:, 512 * ab:512 * ab + 384], psim[:, :], AF.Exp),
                            reads=[("P", pc)], writes=[ATK[ab]],
                            tag=f"L{l}g{g}h{h}exp")
                        ofs = 128 * h
                        fns2 = []
                        rds2 = [ATK[ab]]
                        for jt in range(3):
                            gj = g - 2 + jt
                            if gj < 0:
                                vap = vhalo_ap(gj + 2, h)
                                rds2 += vhk(gj + 2)
                            elif gj < 8 * hh:
                                vap = vhalo_ap(gj - 6, h)
                                rds2 += vhk(gj - 6)
                            else:
                                lt = gj - 8 * hh
                                vap = s_v[:, 1040 * lt + 65 * h:1040 * lt + 65 * h + 65]
                                rds2.append(("sv", lt))
                            st = (jt == 0) and (hi == 0)
                            sp = (jt == 2) and (hi == 3)
                            fns2.append(lambda t, vap=vap, jt=jt, ab=ab, ofs=ofs,
                                        st=st, sp=sp:
                                        t.matmul(pA[:, ofs:ofs + 65],
                                                 s_attnT[:, 512 * ab + 128 * jt:
                                                         512 * ab + 128 * (jt + 1)],
                                                 vap, start=st, stop=sp))
                        S.group("t", fns2, reads=rds2, writes=[("P", hgp)],
                                tag=f"L{l}g{g}h{h}out")
                    S.op("v", lambda v, hgp=hgp: v.reciprocal(
                        s_zinv[:, 4 * hgp:4 * hgp + 4],
                        pA[:, 512 * hgp:512 * (hgp + 1)]
                        .rearrange("p (h e) -> p h e", e=128)[:, :, 64]),
                        reads=[("P", hgp)], writes=[("zinv",)],
                        tag=f"L{l}g{g}zi{hgp}")
                    S.op("v", lambda v, hgp=hgp: v.tensor_tensor(
                        otok[:, :].bitcast(F32)[:, 256 * hgp:256 * (hgp + 1)]
                        .rearrange("p (h e) -> p h e", e=64),
                        pA[:, 512 * hgp:512 * (hgp + 1)]
                        .rearrange("p (h e) -> p h e", e=128)[:, :, 0:64],
                        s_zinv[:, 4 * hgp:4 * hgp + 4].unsqueeze(2)
                        .broadcast_to([128, 4, 64]), ALU.mult),
                        reads=[("P", hgp), ("zinv",)], writes=[OTK[hgp]],
                        tag=f"L{l}g{g}ev{hgp}")
                for j8 in range(8):
                    tb_ = 4 + j8 % 2
                    S.op("t", lambda t, j8=j8, tb_=tb_: t.transpose(
                        pB[:, 512 * (tb_ - 4):512 * (tb_ - 4) + 128],
                        otok[:, :].bitcast(F32)[:, 128 * j8:128 * (j8 + 1)],
                        kb.s_id[:, :]),
                        reads=[OTK[j8 // 2], "s_id"], writes=[("P", tb_)],
                        tag=f"L{l}g{g}tr{j8}")
                    base = s_delta if j8 < 4 else s_aT
                    jj = j8 % 4
                    dstap = base[:, :].bitcast(BF16)[:, 1024 * jj + 128 * sl_:
                                                     1024 * jj + 128 * sl_ + 128]
                    S.op("s", lambda sc, dstap=dstap, tb_=tb_: sc.activation(
                        dstap, pB[:, 512 * (tb_ - 4):512 * (tb_ - 4) + 128], AF.Copy),
                        reads=[("P", tb_)], writes=[DK(jj) if j8 < 4 else AK(jj)],
                        tag=f"L{l}g{g}tre{j8}")
            if hh == 0:
                for p in range(8):
                    S.op("v", lambda v, p=p: v.tensor_copy(
                        s_kcar[:, 256 * p:256 * (p + 1)], ksl(p)[:, 768:1024]),
                        reads=kkeys(p), writes=[("khalo",)], tag=f"L{l}kc{p}")
                for t2 in range(2):
                    S.op("v", lambda v, t2=t2: v.tensor_copy(
                        W[6 + t2][:, 0:1040],
                        s_v[:, (6 + t2) * 1040:(7 + t2) * 1040]),
                        reads=[("sv", 6 + t2)], writes=vhk(t2), tag=f"L{l}vc{t2}")
            for m8 in range(8):
                wload(f"L{l}h{hh}wo{m8}", Lo[:, 128 * m8:128 * (m8 + 1)], m8 % 2)
                for ch in range(2):
                    fns = []
                    for k in range(8):
                        bt = s_delta if k < 4 else s_aT
                        jj = k % 4
                        rhs = bt[:, :].bitcast(BF16)[:, 1024 * jj + 512 * ch:
                                                     1024 * jj + 512 * ch + 512]
                        fns.append(lambda t, k=k, m8=m8, ch=ch, rhs=rhs: t.matmul(
                            pA[:, 1024 + 512 * ch:1024 + 512 * (ch + 1)],
                            s_wbuf[:, 1024 * (m8 % 2) + 128 * k:
                                   1024 * (m8 % 2) + 128 * (k + 1)],
                            rhs, start=(k == 0), stop=(k == 7)))
                    S.group("t", fns,
                            reads=WBH(m8 % 2) + [DK(c) for c in range(4)]
                            + [AK(c) for c in range(4)],
                            writes=[("P", 2 + ch)], tag=f"L{l}h{hh}wo{m8}_{ch}")
                    S.op("v", lambda v, m8=m8, ch=ch, c0=c0: v.tensor_tensor(
                        xT[m8][:, c0 + 512 * ch:c0 + 512 * ch + 512],
                        xT[m8][:, c0 + 512 * ch:c0 + 512 * ch + 512],
                        pA[:, 1024 + 512 * ch:1024 + 512 * (ch + 1)], ALU.add),
                        reads=[("xT", m8, 2 * hh + ch), ("P", 2 + ch)],
                        writes=[("xT", m8, 2 * hh + ch)], tag=f"L{l}h{hh}wor{m8}_{ch}")

        for i, d in enumerate(range(4, 8)):
            S.op("v", lambda v, d=d, i=i: v.tensor_copy(
                s_xcol[:, i:i + 1], xT[d][:, TC - 1:TC]),
                reads=[("xT", d, 3)], writes=[("xcol",)], tag=f"L{l}x2c{i}")
            S.op("v", lambda v, d=d, i=i: v.tensor_copy(
                s_xcol[:, 4 + i:5 + i], xT[d][:, 1023:1024]),
                reads=[("xT", d, 1)], writes=[("xcol",)], tag=f"L{l}x2m{i}")
        kb.store("gd", ag_x2_in.ap()[0, :].rearrange("(p c) -> p c", p=128),
                 s_xcol[:, 0:4], reads=[("xcol",)], key=("agx2",), tag=f"L{l}agx2st")
        S.checkpoint("gd")
        S.cc(lambda g: g.collective_compute(
            "AllGather", ALU.bypass, replica_groups=groups4,
            ins=[ag_x2_in.ap()[:, :]], outs=[ag_x2_out.ap()[:, :]]),
            reads=[("agx2",)], writes=["ag_x2_out"], tag=f"L{l}agx2")
        S.op("v", lambda v: v.memset(s_xcol[:, 0:4], 0.0), writes=[("xcol",)],
             tag=f"L{l}x2z")
        for j in range(4):
            kb.load("gd", s_attnT[:, :].bitcast(F32)[:, 0:4],
                    ag_x2_out.ap()[j, :].rearrange("(p c) -> p c", p=128),
                    ATK[0], reads=["ag_x2_out"], tag=f"L{l}agx2ld{j}")
            S.op("v", lambda v, j=j: v.scalar_tensor_tensor(
                s_xcol[:, 0:4], s_attnT[:, :].bitcast(F32)[:, 0:4],
                s_ccm[:, j:j + 1], s_xcol[:, 0:4], ALU.mult, ALU.add),
                reads=[ATK[0], ("f_ccm",), ("xcol",)], writes=[("xcol",)],
                tag=f"L{l}x2sel{j}")

        build_ffn(kb, l, L1w, L2w, s_b1, s_lng, s_lnbv, s_b2v, s_wbuf,
                  emit_ain_half, ain, WBH)
    l0stack.close()


def build_ffn(kb, l, L1w, L2w, s_b1, s_lng, s_lnbv, s_b2v, s_wbuf,
              emit_ain_half, ain, WBH):
    nc, S = kb.nc, kb.S
    xT, W, P = kb.xT, kb.W, kb.P
    pA, pB = kb.pA, kb.pB
    s_cb, s_cbb, s_ones, s_stats = kb.s_cb, kb.s_cbb, kb.s_ones, kb.s_stats
    s_delta, s_aT = kb.s_delta, kb.s_aT
    STK = ("stats",)
    DK = lambda c: ("delta", c)
    AK = lambda c: ("aT", c)
    otok = W[4]
    OTK = [("W", 4, c) for c in range(4)]

    def hga(i):
        if i < 16:
            sl = W[8 + i // 2][:, 1024 * (i % 2):1024 * (i % 2) + 1024]
            keys = [("W", 8 + i // 2, 2 * (i % 2) + j) for j in range(2)]
        elif i < 20:
            jj = i - 16
            sl = s_delta[:, :].bitcast(BF16)[:, 1024 * jj:1024 * (jj + 1)]
            keys = [DK(jj)]
        else:
            jj = i - 20
            sl = s_aT[:, :].bitcast(BF16)[:, 1024 * jj:1024 * (jj + 1)]
            keys = [AK(jj)]
        return sl, keys

    for hh in range(2):
        c0 = 1024 * hh
        emit_ain_half(l, hh, f"F{l}h{hh}")
        for ch in range(2):
            cc = 512 * ch
            fns = []
            rds = ["s_cbb"]
            for d in range(8):
                asl, akeys = ain(d)
                rds.append(akeys[ch])
                fns.append(lambda t, asl=asl, cc=cc, d=d: t.matmul(
                    P[0][0:1, 0:512], kb.s_cbb[:, 0:1], asl[:, cc:cc + 512],
                    start=(d == 0), stop=(d == 7)))
            S.group("t", fns, reads=rds, writes=[("P", 0)], tag=f"F{l}h{hh}sum{ch}")
            for d in range(8):
                asl, akeys = ain(d)
                S.op("s", lambda sc, asl=asl, cc=cc: sc.activation(
                    otok[:, 0:512], asl[:, cc:cc + 512], AF.Square),
                    reads=[akeys[ch]], writes=[OTK[0]], tag=f"F{l}h{hh}sq{ch}_{d}")
                S.op("t", lambda t, d=d: t.matmul(
                    P[1][0:1, 0:512], kb.s_cbb[:, 0:1], otok[:, 0:512],
                    start=(d == 0), stop=(d == 7)),
                    reads=[OTK[0], "s_cbb"], writes=[("P", 1)],
                    tag=f"F{l}h{hh}sq2{ch}_{d}")
            S.op("s", lambda sc, cc=cc, c0=c0: sc.activation(
                s_stats[32:33, c0 + cc:c0 + cc + 512], P[0][0:1, 0:512], AF.Copy),
                reads=[("P", 0)], writes=[STK], tag=f"F{l}h{hh}se{ch}")
            S.op("s", lambda sc, cc=cc, c0=c0: sc.activation(
                s_stats[64:65, c0 + cc:c0 + cc + 512], P[1][0:1, 0:512], AF.Copy),
                reads=[("P", 1)], writes=[STK], tag=f"F{l}h{hh}se2{ch}")
        S.op("v", lambda v, c0=c0: v.tensor_scalar(s_stats[32:33, c0:c0 + 1024],
                                            s_stats[32:33, c0:c0 + 1024],
                                            1.0 / 1024.0, None, ALU.mult),
             reads=[STK], writes=[STK], tag=f"F{l}h{hh}mu")
        for ch in range(2):
            cc = c0 + 512 * ch
            S.op("s", lambda sc, cc=cc: sc.activation(
                P[0][0:1, 0:512], s_stats[32:33, cc:cc + 512], AF.Square),
                reads=[STK], writes=[("P", 0)], tag=f"F{l}h{hh}mu2_{ch}")
            S.op("v", lambda v, cc=cc: v.tensor_scalar(
                s_stats[64:65, cc:cc + 512], s_stats[64:65, cc:cc + 512],
                1.0 / 1024.0, None, ALU.mult),
                reads=[STK], writes=[STK], tag=f"F{l}h{hh}ex2_{ch}")
            S.op("v", lambda v, cc=cc: v.tensor_tensor(
                s_stats[64:65, cc:cc + 512], s_stats[64:65, cc:cc + 512],
                P[0][0:1, 0:512], ALU.subtract),
                reads=[STK, ("P", 0)], writes=[STK], tag=f"F{l}h{hh}var_{ch}")
        S.op("s", lambda sc, c0=c0: sc.activation(s_stats[64:65, c0:c0 + 1024],
                                           s_stats[64:65, c0:c0 + 1024], AF.Sqrt,
                                           bias=s_cb[0:1, CB_EPS5:CB_EPS5 + 1]),
             reads=[STK, "s_cb"], writes=[STK], tag=f"F{l}h{hh}sd")
        S.op("v", lambda v, c0=c0: v.reciprocal(s_stats[64:65, c0:c0 + 1024],
                                         s_stats[64:65, c0:c0 + 1024]),
             reads=[STK], writes=[STK], tag=f"F{l}h{hh}rs")
        for ch in range(2):
            cc = c0 + 512 * ch
            S.op("t", lambda t, cc=cc, ch=ch: t.matmul(
                pA[:, 512 * ch:512 * (ch + 1)], s_ones[32:33, :],
                s_stats[32:33, cc:cc + 512], start=True, stop=True),
                reads=[STK, "s_ones"], writes=[("P", ch)], tag=f"F{l}h{hh}bm{ch}")
            S.op("t", lambda t, cc=cc, ch=ch: t.matmul(
                pA[:, 1024 + 512 * ch:1024 + 512 * (ch + 1)], s_ones[64:65, :],
                s_stats[64:65, cc:cc + 512], start=True, stop=True),
                reads=[STK, "s_ones"], writes=[("P", 2 + ch)], tag=f"F{l}h{hh}br{ch}")
        for d in range(8):
            asl, akeys = ain(d)
            for ch in range(2):
                cc = 512 * ch
                S.op("v", lambda v, asl=asl, cc=cc, ch=ch: v.tensor_tensor(
                    asl[:, cc:cc + 512], asl[:, cc:cc + 512],
                    pA[:, 512 * ch:512 * (ch + 1)], ALU.subtract),
                    reads=[akeys[ch], ("P", ch)], writes=[akeys[ch]],
                    tag=f"F{l}h{hh}a1{d}_{ch}")
                S.op("v", lambda v, asl=asl, cc=cc, ch=ch, d=d: v.scalar_tensor_tensor(
                    asl[:, cc:cc + 512], asl[:, cc:cc + 512], s_lng[:, d:d + 1],
                    pA[:, 1024 + 512 * ch:1024 + 512 * (ch + 1)], ALU.mult, ALU.mult),
                    reads=[akeys[ch], "lv_g", ("P", 2 + ch)], writes=[akeys[ch]],
                    tag=f"F{l}h{hh}a2{d}_{ch}")
                S.op("v", lambda v, asl=asl, cc=cc, d=d: v.tensor_scalar(
                    asl[:, cc:cc + 512], asl[:, cc:cc + 512],
                    s_lnbv[:, d:d + 1], None, ALU.add),
                    reads=[akeys[ch], "lv_b"], writes=[akeys[ch]],
                    tag=f"F{l}h{hh}a3{d}_{ch}")
        W5 = W[5]
        ATK5 = [("W", 5, c) for c in range(4)]

        def w1buf(fp, part):
            if fp % 2 == 0:
                return (s_wbuf[:, 1024 * part:1024 * (part + 1)], WBH(part))
            return (W5[:, 1024 * part:1024 * (part + 1)],
                    ATK5[2 * part:2 * part + 2])
        for fp in range(22):
            for part, foff in [(0, fp), (1, fp + 22)]:
                bsl, bkeys = w1buf(fp, part)
                kb.load("gd", bsl.rearrange("p (k c) -> p k c", k=8),
                        L1w[:, 128 * foff:128 * (foff + 1)]
                        .rearrange("(k p) c -> p k c", p=128),
                        list(bkeys), tag=f"F{l}h{hh}w1{fp}_{part}")
            for ch in range(2):
                for part in range(2):
                    bsl, bkeys = w1buf(fp, part)
                    fns = []
                    rds = list(bkeys)
                    for k in range(8):
                        asl, akeys = ain(k)
                        rds.append(akeys[ch])
                        fns.append(lambda t, k=k, ch=ch, asl=asl, bsl=bsl,
                                   part=part: t.matmul(
                            pB[:, 512 * (2 * ch + part):512 * (2 * ch + part) + 512],
                            bsl[:, 128 * k:128 * (k + 1)],
                            asl[:, 512 * ch:512 * ch + 512],
                            start=(k == 0), stop=(k == 7)))
                    S.group("t", fns, reads=rds, writes=[("P", 4 + 2 * ch + part)],
                            tag=f"F{l}h{hh}w1m{fp}_{ch}_{part}")
                S.op("s", lambda sc, fp=fp, ch=ch: sc.activation(
                    otok[:, 1536:2048], pB[:, 1024 * ch + 512:1024 * ch + 1024],
                    AF.Gelu, bias=s_b1[:, 22 + fp:23 + fp]),
                    reads=[("P", 4 + 2 * ch + 1), "lv_b1"], writes=[OTK[3]],
                    tag=f"F{l}h{hh}ge{fp}_{ch}")
                sl, keys = hga(fp)
                S.op("v", lambda v, sl=sl, ch=ch, fp=fp: v.scalar_tensor_tensor(
                    sl[:, 512 * ch:512 * ch + 512],
                    pB[:, 1024 * ch:1024 * ch + 512], s_b1[:, fp:fp + 1],
                    otok[:, 1536:2048], ALU.add, ALU.mult),
                    reads=[("P", 4 + 2 * ch), "lv_b1", OTK[3]],
                    writes=[keys[min(ch, len(keys) - 1)]],
                    tag=f"F{l}h{hh}hg{fp}_{ch}")
        def w2buf(m8, k):
            if m8 % 2 == 0:
                return (s_wbuf[:, 128 * k:128 * (k + 1)], ("wb", k // 4))
            if k < 16:
                return (otok[:, 128 * k:128 * (k + 1)], OTK[k // 4])
            return (W5[:, 128 * (k - 16):128 * (k - 15)], ATK5[(k - 16) // 4])
        for m8 in range(8):
            if m8 % 2 == 0:
                kb.load("gd", s_wbuf[:, 0:2816].rearrange("p (k c) -> p k c", k=22),
                        L2w[:, 128 * m8:128 * (m8 + 1)]
                        .rearrange("(k p) c -> p k c", p=128),
                        [("wb", q) for q in range(6)], tag=f"F{l}h{hh}w2{m8}")
            else:
                kb.load("gd", otok[:, 0:2048].rearrange("p (k c) -> p k c", k=16),
                        L2w[0:2048, 128 * m8:128 * (m8 + 1)]
                        .rearrange("(k p) c -> p k c", p=128),
                        list(OTK), tag=f"F{l}h{hh}w2a{m8}")
                kb.load("gd", W5[:, 0:768].rearrange("p (k c) -> p k c", k=6),
                        L2w[2048:2816, 128 * m8:128 * (m8 + 1)]
                        .rearrange("(k p) c -> p k c", p=128),
                        [ATK5[0], ATK5[1]], tag=f"F{l}h{hh}w2b{m8}")
            wkk = [w2buf(m8, k)[1] for k in range(22)]
            for ch in range(2):
                fns = []
                rds = list(dict.fromkeys(wkk))
                for k in range(22):
                    sl, keys = hga(k)
                    rds += keys
                    lhs = w2buf(m8, k)[0]
                    fns.append(lambda t, k=k, sl=sl, ch=ch, lhs=lhs: t.matmul(
                        pA[:, 512 * ch:512 * (ch + 1)],
                        lhs,
                        sl[:, 512 * ch:512 * ch + 512],
                        start=(k == 0), stop=(k == 21)))
                S.group("t", fns, reads=rds, writes=[("P", ch)],
                        tag=f"F{l}h{hh}w2m{m8}_{ch}")
                S.op("v", lambda v, m8=m8, ch=ch, c0=c0: v.scalar_tensor_tensor(
                    xT[m8][:, c0 + 512 * ch:c0 + 512 * ch + 512],
                    pA[:, 512 * ch:512 * (ch + 1)], s_b2v[:, m8:m8 + 1],
                    xT[m8][:, c0 + 512 * ch:c0 + 512 * ch + 512], ALU.add, ALU.add),
                    reads=[("P", ch), "lv_b2", ("xT", m8, 2 * hh + ch)],
                    writes=[("xT", m8, 2 * hh + ch)], tag=f"F{l}h{hh}w2r{m8}_{ch}")


def build_l1attn(kb, groups4):
    nc, S = kb.nc, kb.S
    xT, W, P = kb.xT, kb.W, kb.P
    pA, pB = kb.pA, kb.pB
    s_cb, s_ones, s_stats = kb.s_cb, kb.s_ones, kb.s_stats
    s_delta, s_aT = kb.s_delta, kb.s_aT
    s_v, s_wbuf, s_zinv = kb.s_v, kb.s_wbuf, kb.s_zinv
    sb = kb.sb
    STK = ("stats",)
    DK = lambda c: ("delta", c)
    AK = lambda c: ("aT", c)
    FENCE = [("xT", d, c) for d in range(8) for c in range(4)]

    s_l1b = sb("s_l1b", [128, 2048], BF16)
    s_bq = sb("s_bq", [128, 24])
    s_bp = sb("s_bp", [128, 8])
    s_ccn = sb("s_ccn", [128, 4])
    s_ccpw = sb("s_ccpw", [128, 4])
    s_r4 = sb("s_r4", [128, 32])
    s_r4b = sb("s_r4b", [128, 32])
    ag_r_in = nc.dram_tensor("ag_r_in", [1, 128 * 32], F32)
    ag_r_out = nc.dram_tensor("ag_r_out", [4, 128 * 32], F32)
    ag_u_in = nc.dram_tensor("ag_u_in", [1, 128 * 32], BF16)
    ag_u_out = nc.dram_tensor("ag_u_out", [4, 128 * 32], BF16)

    for nm, dst, key in [("f_l1bias", s_l1b, ("l1b",)), ("f_l1bq", s_bq, ("l1bq",)),
                         ("f_l1bp", s_bp, ("l1bp",)), ("f_ccn", s_ccn, ("ccn",)),
                         ("f_ccpw", s_ccpw, ("ccpw",))]:
        kb.load("gd", dst[:, :], kb.ap[nm], key, reads=FENCE)
    S.checkpoint("gd")

    Lw = kb.ap["f_l1w"]
    Lp = kb.ap["f_l1proj"]
    WBH = lambda par: [("wb", 2 * par), ("wb", 2 * par + 1)]

    def wload(tag, dram2d, par):
        kb.load("gd", s_wbuf[:, 1024 * par:1024 * (par + 1)]
                .rearrange("p (k c) -> p k c", k=8),
                dram2d.rearrange("(k p) c -> p k c", p=128),
                WBH(par), tag=tag)

    def ain(d):
        sl = W[d % 4][:, 1024 * (d // 4):1024 * (d // 4) + 1024]
        return sl, [("W", d % 4, 2 * (d // 4) + i) for i in range(2)]

    def qsl(p):
        return W[8 + p // 2][:, 1024 * (p % 2):1024 * (p % 2) + 1024]

    def qkeys(p):
        return [("W", 8 + p // 2, 2 * (p % 2) + i) for i in range(2)]

    def ksl(p):
        return W[12 + p // 2][:, 1024 * (p % 2):1024 * (p % 2) + 1024]

    def kkeys(p):
        return [("W", 12 + p // 2, 2 * (p % 2) + i) for i in range(2)]

    def orl(d):
        sl = W[12 + d // 2][:, 1024 * (d % 2):1024 * (d % 2) + 1024]
        return sl, [("W", 12 + d // 2, 2 * (d % 2) + i) for i in range(2)]

    otok = W[4]
    OTK = [("W", 4, c) for c in range(4)]
    s_attnT = W[5]
    ATK = [("W", 5, c) for c in range(4)]

    def vw(k):
        if k < 4:
            return s_wbuf[:, 512 * k:512 * (k + 1)], ("wb", k)
        return W[5][:, 512 * (k - 4):512 * (k - 4) + 512], ATK[k - 4]

    for d in range(8):
        kb.store("gd", ag_r_in.ap()[0, :].rearrange("(p c) -> p c", p=128)
                 [:, 4 * d:4 * d + 4],
                 xT[d][:, 0:4], reads=[("xT", d, 0)], key=("agr",), tag=f"agr{d}")
    S.checkpoint("gd")
    S.cc(lambda g: g.collective_compute(
        "AllGather", ALU.bypass, replica_groups=groups4,
        ins=[ag_r_in.ap()[:, :]], outs=[ag_r_out.ap()[:, :]]),
        reads=[("agr",)], writes=["ag_r_out"], tag="agroll")
    S.op("v", lambda v: v.memset(s_r4[:, :], 0.0), reads=FENCE, writes=[("r4",)],
         tag="r4z")
    for j in range(4):
        kb.load("gd", s_stats[:, 0:32],
                ag_r_out.ap()[j, :].rearrange("(p c) -> p c", p=128),
                STK, reads=["ag_r_out"], tag=f"agrl{j}")
        S.op("v", lambda v, j=j: v.scalar_tensor_tensor(
            s_r4[:, :], s_stats[:, 0:32], s_ccn[:, j:j + 1], s_r4[:, :],
            ALU.mult, ALU.add),
            reads=[STK, ("ccn",), ("r4",)], writes=[("r4",)], tag=f"agrs{j}")

    for hh in range(2):
        for d in range(8):
            sl, keys = ain(d)
            if hh == 0:
                S.op("s", lambda sc, sl=sl, d=d: sc.activation(
                    sl[:, :], xT[d][:, 4:1028], AF.Copy),
                    reads=[("xT", d, 0), ("xT", d, 1), ("xT", d, 2)],
                    writes=keys, tag=f"A1h0ain{d}")
            else:
                S.op("s", lambda sc, sl=sl, d=d: sc.activation(
                    sl[:, 0:1020], xT[d][:, 1028:2048], AF.Copy),
                    reads=[("xT", d, 2), ("xT", d, 3)],
                    writes=keys, tag=f"A1h1ain{d}")
                S.op("v", lambda v, sl=sl, d=d: v.tensor_copy(
                    sl[:, 1020:1024], s_r4[:, 4 * d:4 * d + 4]),
                    reads=[("r4",)], writes=[keys[1]], tag=f"A1h1t{d}")
        for p in range(8):
            for side in range(2):
                par = p % 2
                wcol = (0 if side == 0 else 1024) + 128 * p
                wload(f"A1h{hh}w{side}_{p}", Lw[:, wcol:wcol + 128], par)
                dstsl = qsl(p) if side == 0 else ksl(p)
                dkeys = qkeys(p) if side == 0 else kkeys(p)
                for ch in range(2):
                    fns = []
                    rds = WBH(par)
                    for k in range(8):
                        asl, akeys = ain(k)
                        rds.append(akeys[ch])
                        fns.append(lambda t, k=k, par=par, ch=ch, asl=asl: t.matmul(
                            pA[:, 512 * ch:512 * (ch + 1)],
                            s_wbuf[:, 1024 * par + 128 * k:1024 * par + 128 * (k + 1)],
                            asl[:, 512 * ch:512 * ch + 512],
                            start=(k == 0), stop=(k == 7)))
                    S.group("t", fns, reads=rds, writes=[("P", ch)],
                            tag=f"A1h{hh}qk{side}_{p}_{ch}")
                    if side == 0:
                        S.op("v", lambda v, dstsl=dstsl, ch=ch, p=p: v.tensor_scalar(
                            dstsl[:, 512 * ch:512 * ch + 512],
                            pA[:, 512 * ch:512 * (ch + 1)],
                            s_bq[:, p:p + 1], None, ALU.add),
                            reads=[("P", ch), ("l1bq",)], writes=[dkeys[ch]],
                            tag=f"A1h{hh}qe{p}_{ch}")
                    else:
                        S.op("v", lambda v, dstsl=dstsl, ch=ch, p=p: v.tensor_scalar(
                            dstsl[:, 512 * ch:512 * ch + 512],
                            pA[:, 512 * ch:512 * (ch + 1)],
                            s_bq[:, 8 + p:9 + p], 0.125, ALU.add, ALU.mult),
                            reads=[("P", ch), ("l1bq",)], writes=[dkeys[ch]],
                            tag=f"A1h{hh}ke{p}_{ch}")
        for fc in range(2):
            for k in range(8):
                dst, key = vw(k)
                kb.load("gd", dst,
                        Lw[128 * k:128 * (k + 1),
                           2048 + 512 * fc:2048 + 512 * (fc + 1)],
                        key, tag=f"A1h{hh}vw{fc}_{k}")
            wrd = [vw(k)[1] for k in range(8)]
            for t8 in range(8):
                fns = []
                rds = list(wrd)
                for k in range(8):
                    asl, akeys = ain(k)
                    rds.append(akeys[(128 * t8) // 512])
                    fns.append(lambda t, t8=t8, k=k, asl=asl: t.matmul(
                        pA[:, 512 * (t8 % 4):512 * (t8 % 4) + 512],
                        asl[:, 128 * t8:128 * t8 + 128],
                        vw(k)[0], start=(k == 0), stop=(k == 7)))
                S.group("t", fns, reads=rds, writes=[("P", t8 % 4)],
                        tag=f"A1h{hh}v{fc}_{t8}")
                S.op("s", lambda sc, t8=t8, fc=fc: sc.activation(
                    s_v[:, 1040 * t8 + 520 * fc:1040 * t8 + 520 * fc + 520]
                    .rearrange("p (h e) -> p h e", e=65)[:, :, 0:64],
                    pA[:, 512 * (t8 % 4):512 * (t8 % 4) + 512]
                    .rearrange("p (h e) -> p h e", e=64), AF.Copy),
                    reads=[("P", t8 % 4)], writes=[("sv", t8)],
                    tag=f"A1h{hh}ve{fc}_{t8}")
        for sl_ in range(8):
            for b in range(4):
                fns = []
                rds = ["s_idb", ("l1b",)]
                for hi in range(4):
                    h = 4 * b + hi
                    p, hp = h // 2, h % 2
                    qap = qsl(p)[64 * hp:64 * hp + 64, 128 * sl_:128 * sl_ + 128]
                    kap = ksl(p)[64 * hp:64 * hp + 64, 128 * sl_:128 * sl_ + 128]
                    rds += qkeys(p) + kkeys(p)
                    dst = pB[:, 512 * b + 128 * hi:512 * b + 128 * hi + 128]
                    fns.append(lambda t, kap=kap, qap=qap, dst=dst, hi=hi: t.matmul(
                        dst, kap, qap, start=(hi == 0), stop=False))
                    fns.append(lambda t, dst=dst, h=h, hi=hi: t.matmul(
                        dst, kb.s_idb[:, :], s_l1b[:, 128 * h:128 * (h + 1)],
                        start=False, stop=(hi == 3)))
                S.group("t", fns, reads=rds, writes=[("P", 4 + b)],
                        tag=f"A1h{hh}s{sl_}_{b}")
                S.op("s", lambda sc, b=b: sc.activation(
                    s_attnT[:, 512 * b:512 * (b + 1)],
                    pB[:, 512 * b:512 * (b + 1)], AF.Exp),
                    reads=[("P", 4 + b)], writes=[ATK[b]],
                    tag=f"A1h{hh}e{sl_}_{b}")
                fns2 = []
                rds2 = [ATK[b]]
                for hi in range(4):
                    h = 4 * b + hi
                    vap = s_v[:, 1040 * sl_ + 65 * h:1040 * sl_ + 65 * h + 65]
                    rds2.append(("sv", sl_))
                    fns2.append(lambda t, h=h, hi=hi, vap=vap: t.matmul(
                        pA[:, 128 * h:128 * h + 65],
                        s_attnT[:, 128 * h:128 * (h + 1)], vap,
                        start=(hi == 0), stop=(hi == 3)))
                S.group("t", fns2, reads=rds2, writes=[("P", b)],
                        tag=f"A1h{hh}o{sl_}_{b}")
                S.op("v", lambda v, b=b: v.reciprocal(
                    s_zinv[:, 4 * b:4 * b + 4],
                    pA[:, 512 * b:512 * (b + 1)]
                    .rearrange("p (h e) -> p h e", e=128)[:, :, 64]),
                    reads=[("P", b)], writes=[("zinv",)], tag=f"A1h{hh}z{sl_}_{b}")
                S.op("v", lambda v, b=b: v.tensor_tensor(
                    otok[:, :].bitcast(F32)[:, 256 * b:256 * (b + 1)]
                    .rearrange("p (h e) -> p h e", e=64),
                    pA[:, 512 * b:512 * (b + 1)]
                    .rearrange("p (h e) -> p h e", e=128)[:, :, 0:64],
                    s_zinv[:, 4 * b:4 * b + 4].unsqueeze(2)
                    .broadcast_to([128, 4, 64]), ALU.mult),
                    reads=[("P", b), ("zinv",)], writes=[OTK[b]],
                    tag=f"A1h{hh}n{sl_}_{b}")
            for j8 in range(8):
                tb_ = 4 + j8 % 2
                S.op("t", lambda t, j8=j8, tb_=tb_: t.transpose(
                    pB[:, 512 * (tb_ - 4):512 * (tb_ - 4) + 128],
                    otok[:, :].bitcast(F32)[:, 128 * j8:128 * (j8 + 1)],
                    kb.s_id[:, :]),
                    reads=[OTK[j8 // 2], "s_id"], writes=[("P", tb_)],
                    tag=f"A1h{hh}tr{sl_}_{j8}")
                bt = s_delta if j8 < 4 else s_aT
                jj = j8 % 4
                dstap = bt[:, :].bitcast(BF16)[:, 1024 * jj + 128 * sl_:
                                               1024 * jj + 128 * sl_ + 128]
                S.op("v", lambda v, dstap=dstap, tb_=tb_, j8=j8: v.tensor_scalar(
                    dstap, pB[:, 512 * (tb_ - 4):512 * (tb_ - 4) + 128],
                    s_bq[:, 16 + j8:17 + j8], None, ALU.add),
                    reads=[("P", tb_), ("l1bq",)],
                    writes=[DK(jj) if j8 < 4 else AK(jj)],
                    tag=f"A1h{hh}te{sl_}_{j8}")
        for m8 in range(8):
            wload(f"A1h{hh}pw{m8}", Lp[:, 128 * m8:128 * (m8 + 1)], m8 % 2)
            osl, okeys = orl(m8)
            for ch in range(2):
                fns = []
                for k in range(8):
                    bt = s_delta if k < 4 else s_aT
                    jj = k % 4
                    rhs = bt[:, :].bitcast(BF16)[:, 1024 * jj + 512 * ch:
                                                 1024 * jj + 512 * ch + 512]
                    fns.append(lambda t, k=k, m8=m8, ch=ch, rhs=rhs: t.matmul(
                        pA[:, 1024 + 512 * ch:1024 + 512 * (ch + 1)],
                        s_wbuf[:, 1024 * (m8 % 2) + 128 * k:
                               1024 * (m8 % 2) + 128 * (k + 1)],
                        rhs, start=(k == 0), stop=(k == 7)))
                S.group("t", fns,
                        reads=WBH(m8 % 2) + [DK(c) for c in range(4)]
                        + [AK(c) for c in range(4)],
                        writes=[("P", 2 + ch)], tag=f"A1h{hh}pj{m8}_{ch}")
                S.op("s", lambda sc, osl=osl, ch=ch: sc.activation(
                    osl[:, 512 * ch:512 * ch + 512],
                    pA[:, 1024 + 512 * ch:1024 + 512 * (ch + 1)], AF.Copy),
                    reads=[("P", 2 + ch)], writes=[okeys[ch]],
                    tag=f"A1h{hh}pe{m8}_{ch}")
        for d in range(8):
            osl, okeys = orl(d)
            if hh == 0:
                S.op("v", lambda v, d=d, osl=osl: v.scalar_tensor_tensor(
                    xT[d][:, 4:1028], osl[:, 0:1024], s_bp[:, d:d + 1],
                    xT[d][:, 4:1028], ALU.add, ALU.add),
                    reads=okeys + [("l1bp",)] + [("xT", d, c) for c in range(3)],
                    writes=[("xT", d, c) for c in range(3)], tag=f"A1r0_{d}")
            else:
                S.op("v", lambda v, d=d, osl=osl: v.scalar_tensor_tensor(
                    xT[d][:, 1028:2048], osl[:, 0:1020], s_bp[:, d:d + 1],
                    xT[d][:, 1028:2048], ALU.add, ALU.add),
                    reads=okeys + [("l1bp",)] + [("xT", d, 2), ("xT", d, 3)],
                    writes=[("xT", d, 2), ("xT", d, 3)], tag=f"A1r1_{d}")

    for d in range(8):
        osl, okeys = orl(d)
        kb.store("gd", ag_u_in.ap()[0, :].rearrange("(p c) -> p c", p=128)
                 [:, 4 * d:4 * d + 4],
                 osl[:, 1020:1024], reads=okeys, key=("agu",), tag=f"agu{d}")
    S.checkpoint("gd")
    S.cc(lambda g: g.collective_compute(
        "AllGather", ALU.bypass, replica_groups=groups4,
        ins=[ag_u_in.ap()[:, :]], outs=[ag_u_out.ap()[:, :]]),
        reads=[("agu",)], writes=["ag_u_out"], tag="agunroll")
    S.op("v", lambda v: v.memset(s_r4b[:, :], 0.0), writes=[("r4b",)], tag="r4bz")
    for j in range(4):
        kb.load("gd", s_attnT[:, 0:32],
                ag_u_out.ap()[j, :].rearrange("(p c) -> p c", p=128),
                ATK[0], reads=["ag_u_out"], tag=f"agul{j}")
        S.op("v", lambda v, j=j: v.scalar_tensor_tensor(
            s_r4b[:, :], s_attnT[:, 0:32], s_ccpw[:, j:j + 1], s_r4b[:, :],
            ALU.mult, ALU.add),
            reads=[ATK[0], ("ccpw",), ("r4b",)], writes=[("r4b",)], tag=f"agus{j}")
    for d in range(8):
        S.op("v", lambda v, d=d: v.scalar_tensor_tensor(
            xT[d][:, 0:4], s_r4b[:, 4 * d:4 * d + 4], s_bp[:, d:d + 1],
            xT[d][:, 0:4], ALU.add, ALU.add),
            reads=[("r4b",), ("l1bp",), ("xT", d, 0)],
            writes=[("xT", d, 0)], tag=f"A1patch{d}")

    build_topln(kb, tag="ln3")


def build_l1ffn(kb):
    nc, S = kb.nc, kb.S
    xT, W, P = kb.xT, kb.W, kb.P
    pA, pB = kb.pA, kb.pB
    s_cb, s_stats = kb.s_cb, kb.s_stats
    s_delta, s_aT = kb.s_delta, kb.s_aT
    s_wbuf = kb.s_wbuf
    sb = kb.sb
    STK = ("stats",)
    DK = lambda c: ("delta", c)
    AK = lambda c: ("aT", c)
    FENCE = [("xT", d, c) for d in range(8) for c in range(4)]
    s_wd2 = sb("s_wd2", [128, 512], BF16)
    s_wbuf2 = sb("s_wbuf2", [128, 2816], BF16)
    WB2 = lambda q: ("wb2", q)
    Wg = kb.ap["f_wgT"]
    Wu = kb.ap["f_wuT"]
    Wd = kb.ap["f_wdT"]

    def hid(i):
        if i < 16:
            sl = W[8 + i // 2][:, 1024 * (i % 2):1024 * (i % 2) + 1024]
            keys = [("W", 8 + i // 2, 2 * (i % 2) + j) for j in range(2)]
        elif i < 20:
            jj = i - 16
            sl = s_delta[:, :].bitcast(BF16)[:, 1024 * jj:1024 * (jj + 1)]
            keys = [DK(jj)]
        else:
            jj = i - 20
            sl = s_aT[:, :].bitcast(BF16)[:, 1024 * jj:1024 * (jj + 1)]
            keys = [AK(jj)]
        return sl, keys

    for d in range(8):
        S.op("s", lambda sc, d=d: sc.activation(W[d][:, :], xT[d][:, :], AF.Copy),
             reads=[("xT", d, c) for c in range(4)],
             writes=[("W", d, c) for c in range(4)], tag=f"Fx3c{d}")

    for hh in range(2):
        c0 = 1024 * hh
        for hi in range(24):
            buf = s_wbuf if hi % 2 == 0 else s_wbuf2
            KW = (lambda q: ("wb", q)) if hi % 2 == 0 else WB2
            kb.load("gd", buf[:, 0:1024].rearrange("p (k c) -> p k c", k=8),
                    Wg[:, 128 * hi:128 * (hi + 1)]
                    .rearrange("(k p) c -> p k c", p=128),
                    [KW(0), KW(1)], tag=f"Fg{hh}_{hi}")
            kb.load("gd", buf[:, 1024:2048].rearrange("p (k c) -> p k c", k=8),
                    Wu[:, 128 * hi:128 * (hi + 1)]
                    .rearrange("(k p) c -> p k c", p=128),
                    [KW(2), KW(3)], tag=f"Fu{hh}_{hi}")
            hsl, hkeys = hid(hi)
            for ch in range(2):
                cc = c0 + 512 * ch
                for part in range(2):
                    fns = []
                    rds = [KW(2 * part), KW(2 * part + 1)]
                    for k in range(8):
                        rds.append(("W", k, cc // 512))
                        fns.append(lambda t, k=k, part=part, ch=ch, cc=cc,
                                   buf=buf: t.matmul(
                            pA[:, 1024 * part + 512 * ch:
                               1024 * part + 512 * ch + 512],
                            buf[:, 1024 * part + 128 * k:
                                1024 * part + 128 * (k + 1)],
                            W[k][:, cc:cc + 512], start=(k == 0), stop=(k == 7)))
                    S.group("t", fns, reads=rds, writes=[("P", 2 * part + ch)],
                            tag=f"F1{hh}m{hi}_{ch}_{part}")
                S.op("s", lambda sc, ch=ch: sc.activation(
                    s_stats[:, 512 * ch:512 * ch + 512],
                    pA[:, 512 * ch:512 * (ch + 1)], AF.Silu,
                    scale=s_cb[:, CB_I127:CB_I127 + 1]),
                    reads=[("P", ch), "s_cb"], writes=[STK],
                    tag=f"F1{hh}sig{hi}_{ch}")
                S.op("v", lambda v, hsl=hsl, ch=ch: v.scalar_tensor_tensor(
                    hsl[:, 512 * ch:512 * ch + 512],
                    pA[:, 1024 + 512 * ch:1024 + 512 * (ch + 1)],
                    s_cb[:, CB_I127:CB_I127 + 1],
                    s_stats[:, 512 * ch:512 * ch + 512], ALU.mult, ALU.mult),
                    reads=[("P", 2 + ch), "s_cb", STK],
                    writes=[hkeys[min(ch, len(hkeys) - 1)]],
                    tag=f"F1{hh}h{hi}_{ch}")
        for m8 in range(8):
            buf = s_wbuf if m8 % 2 == 0 else s_wbuf2
            KW = (lambda q: ("wb", q)) if m8 % 2 == 0 else WB2
            d2sl = s_wd2[:, 256 * (m8 % 2):256 * (m8 % 2) + 256]
            d2k = ("wd2", m8 % 2)
            kb.load("gd", buf[:, 0:2816].rearrange("p (k c) -> p k c", k=22),
                    Wd[0:2816, 128 * m8:128 * (m8 + 1)]
                    .rearrange("(k p) c -> p k c", p=128),
                    [KW(q) for q in range(6)],
                    reads=FENCE if (hh == 0 and m8 == 0) else [],
                    tag=f"Fd{hh}_{m8}")
            kb.load("gd", d2sl.rearrange("p (k c) -> p k c", k=2),
                    Wd[2816:3072, 128 * m8:128 * (m8 + 1)]
                    .rearrange("(k p) c -> p k c", p=128),
                    d2k, reads=FENCE if (hh == 0 and m8 == 0) else [],
                    tag=f"Fd2{hh}_{m8}")
            for ch in range(2):
                fns = []
                rds = [KW(q) for q in range(6)] + [d2k]
                for k in range(24):
                    hsl, hkeys = hid(k)
                    rds += hkeys
                    lhs = (buf[:, 128 * k:128 * (k + 1)] if k < 22
                           else d2sl[:, 128 * (k - 22):128 * (k - 21)])
                    fns.append(lambda t, k=k, ch=ch, hsl=hsl, lhs=lhs: t.matmul(
                        pB[:, 512 * ch:512 * (ch + 1)], lhs,
                        hsl[:, 512 * ch:512 * ch + 512],
                        start=(k == 0), stop=(k == 23)))
                S.group("t", fns, reads=rds, writes=[("P", 4 + ch)],
                        tag=f"F1{hh}o{m8}_{ch}")
                S.op("v", lambda v, m8=m8, ch=ch, c0=c0: v.scalar_tensor_tensor(
                    xT[m8][:, c0 + 512 * ch:c0 + 512 * ch + 512],
                    pB[:, 512 * ch:512 * (ch + 1)],
                    s_cb[:, CB_I127:CB_I127 + 1],
                    xT[m8][:, c0 + 512 * ch:c0 + 512 * ch + 512],
                    ALU.mult, ALU.add),
                    reads=[("P", 4 + ch), "s_cb", ("xT", m8, 2 * hh + ch)],
                    writes=[("xT", m8, 2 * hh + ch)], tag=f"F1{hh}r{m8}_{ch}")

    build_topln(kb, tag="ln4")


# version 04 (final)
# speedup vs baseline: 1.2024x; 1.0073x over previous
import numpy as np
import ml_dtypes
from collections import defaultdict
from contextlib import ExitStack

import concourse.bass as bass
import concourse.mybir as mybir
from concourse.bass_utils import run_bass_kernel_spmd

F32 = mybir.dt.float32
BF16 = mybir.dt.bfloat16
I32 = mybir.dt.int32
I32 = mybir.dt.int32
AF = mybir.ActivationFunctionType
ALU = mybir.AluOpType
AX = mybir.AxisListType
bf16 = ml_dtypes.bfloat16

COMPUTE = ("t", "v", "s", "g", "y")
ASYNC = ("yd", "gd", "sd", "gc")
ISSUER = {"yd": "y", "gd": "g", "sd": "s", "gc": "g"}


class _Op:
    __slots__ = ("eng", "fns", "reads", "writes", "seq", "amount", "tag",
                 "dep_ops", "waits", "inc_needed", "val", "is_ckpt",
                 "ckpt_queue", "ckpt_cover")

    def __init__(self, eng, fns, reads, writes, seq, amount, tag):
        self.eng = eng; self.fns = fns; self.reads = reads; self.writes = writes
        self.seq = seq; self.amount = amount; self.tag = tag
        self.dep_ops = []; self.waits = {}
        self.inc_needed = False; self.val = None
        self.is_ckpt = False; self.ckpt_queue = None; self.ckpt_cover = 0


class Seq:
    def __init__(self):
        self.ops = []
        self.per_eng = defaultdict(list)
        self.writer = {}
        self.readers = defaultdict(dict)
        self.ckpts = defaultdict(list)

    def _mkop(self, eng, fns, reads, writes, amount, tag):
        op = _Op(eng, fns, list(reads), list(writes),
                 len(self.per_eng[eng]) + 1, amount, tag)
        self.ops.append(op)
        self.per_eng[eng].append(op)
        return op

    def checkpoint(self, queue, tag=None):
        assert queue in ASYNC
        cover = len(self.per_eng[queue])
        if cover == 0:
            return None
        if self.ckpts[queue] and self.ckpts[queue][-1].ckpt_cover >= cover:
            return self.ckpts[queue][-1]
        op = self._mkop(ISSUER[queue], [lambda e: e.nop()], [], [], 1,
                        tag or f"ckpt_{queue}_{cover}")
        op.is_ckpt = True; op.ckpt_queue = queue; op.ckpt_cover = cover
        self.ckpts[queue].append(op)
        return op

    def _dep_for(self, dep_op):
        if dep_op.eng in ASYNC:
            for cp in self.ckpts[dep_op.eng]:
                if cp.ckpt_cover >= dep_op.seq:
                    return cp
            return self.checkpoint(dep_op.eng)
        return dep_op

    def _record(self, eng, fns, reads, writes, amount, tag):
        deps = {}

        def need(d):
            if d is None:
                return
            prev = deps.get(d.eng)
            if prev is None or d.seq > prev.seq:
                deps[d.eng] = d

        for b in reads:
            w = self.writer.get(b)
            if w is not None:
                need(self._dep_for(w))
        for b in writes:
            w = self.writer.get(b)
            if w is not None:
                need(self._dep_for(w))
            for rop in self.readers.get(b, {}).values():
                need(self._dep_for(rop))
        op = self._mkop(eng, fns, reads, writes, amount, tag)
        op.dep_ops = [d for d in deps.values()
                      if not (d.eng == eng and d.seq >= op.seq)]
        for b in op.writes:
            self.writer[b] = op
            self.readers[b] = {}
        for b in op.reads:
            self.readers[b][eng] = op
        return op

    def op(self, eng, fn, reads=(), writes=(), tag=None):
        assert eng in COMPUTE
        return self._record(eng, [fn], reads, writes, 1, tag)

    def group(self, eng, fns, reads=(), writes=(), tag=None):
        assert eng in COMPUTE
        return self._record(eng, list(fns), reads, writes, 1, tag)

    def dma(self, queue, fn, reads=(), writes=(), tag=None):
        assert queue in ("yd", "gd", "sd")
        return self._record(queue, [fn], reads, writes, 16, tag)

    def cc(self, fn, reads=(), writes=(), tag=None):
        return self._record("gc", [fn], reads, writes, 1, tag)

    def finalize(self, nc, block):
        targets = defaultdict(set)
        for op in self.ops:
            for d in op.dep_ops:
                targets[d.eng].add(d.seq)
        for e, ops in self.per_eng.items():
            v = 0
            always = e in ASYNC
            for o in ops:
                if always or o.seq in targets[e] or o.is_ckpt:
                    o.inc_needed = True
                    v += o.amount
                o.val = v
        for op in self.ops:
            rw = {}
            for d in op.dep_ops:
                assert d.inc_needed
                if d.val > 0 and rw.get(d.eng, 0) < d.val:
                    rw[d.eng] = d.val
            op.waits = rw
            if op.is_ckpt:
                q = op.ckpt_queue
                op.waits[q] = self.per_eng[q][op.ckpt_cover - 1].val

        stack = ExitStack()
        sems = {e: stack.enter_context(nc.semaphore(f"sem_{e}"))
                for e in list(self.per_eng)}
        self._semstack = stack
        phys_of = {"t": "t", "v": "v", "s": "s", "g": "g", "y": "y",
                   "yd": "y", "gd": "g", "sd": "s", "gc": "g"}
        by_phys = defaultdict(list)
        for op in self.ops:
            by_phys[phys_of[op.eng]].append(op)
        decorators = {"t": block.tensor, "v": block.vector, "s": block.scalar,
                      "g": block.gpsimd, "y": block.sync}
        stats = {"n_ops": len(self.ops), "n_incs": 0, "n_waits": 0}
        for pe, ops_list in by_phys.items():
            tail = [(q, self.per_eng[q][-1].val) for q in ASYNC
                    if self.per_eng[q] and ISSUER[q] == pe]

            def body(eng, _ops=ops_list, _tail=tail):
                obs = defaultdict(int)
                for o in _ops:
                    for e2 in sorted(o.waits, key=str):
                        v = o.waits[e2]
                        if obs[e2] < v:
                            eng.wait_ge(sems[e2], v)
                            obs[e2] = v
                            stats["n_waits"] += 1
                    inst = None
                    for fn in o.fns:
                        inst = fn(eng)
                    if o.inc_needed:
                        inst.then_inc(sems[o.eng], o.amount)
                        stats["n_incs"] += 1
                for q, v in _tail:
                    if obs[q] < v:
                        eng.wait_ge(sems[q], v)
            decorators[pe](body)
        self.stats = stats
        return sems


NCORES = 8
B, L, D = 2, 8192, 1024
TC = 2048
DT = 8
NEG = -1e30
PHASE_LIMIT = 99

CB_ONE = 0
CB_EPS5 = 1
CB_I127 = 3


class KB:

    def __init__(self, nc, S, ctx):
        self.nc = nc
        self.S = S
        self.ctx = ctx
        self.ap = {}

    def inp(self, name, shape, dtype=F32):
        self.ap[name] = self.nc.dram_tensor(name, list(shape), dtype,
                                            kind="ExternalInput").ap()
        return self.ap[name]

    def sb(self, name, shape, dtype=F32):
        return self.ctx.enter_context(self.nc.sbuf_tensor(name, shape, dtype))

    def psum(self, name, shape, dtype=F32):
        return self.ctx.enter_context(self.nc.psum_tensor(name, shape, dtype))

    def load(self, queue, dst_ap, src_ap, key, reads=(), tag=None):
        wk = key if isinstance(key, list) else [key]
        self.S.dma(queue, lambda g, d=dst_ap, s=src_ap: g.dma_start(out=d, in_=s),
                   writes=wk, reads=list(reads), tag=tag)

    def store(self, queue, dst_ap, src_ap, reads, key=None, tag=None):
        wk = key if isinstance(key, list) else ([key] if key else [])
        self.S.dma(queue, lambda g, d=dst_ap, s=src_ap: g.dma_start(out=d, in_=s),
                   reads=list(reads), writes=wk, tag=tag)


def wkeys(i, c0=0, c1=4):
    return [("W", i, c) for c in range(c0, c1)]


def xkeys(d, c0=0, c1=4):
    return [("xT", d, c) for c in range(c0, c1)]


def build_nc(phase_limit=99):
    nc = bass.Bass("TRN2", target_bir_lowering=False, dynamic_dma_scratch_size=16384)
    S = Seq()
    with ExitStack() as ctx:
        kb = KB(nc, S, ctx)
        inp, sb = kb.inp, kb.sb

        inp("x_tok", [TC, D])
        inp("m_xprojT", [D, 96])
        inp("m_dtwT", [64, D])
        inp("m_dtb", [128, 8])
        inp("m_Alog", [128, 128])
        inp("m_D", [128, 8])
        inp("m_lna", [128, 8])
        inp("m_lnb", [128, 8])
        inp("identity", [128, 128])
        inp("ones_row", [128, 128])
        inp("cbank", [128, 8])
        inp("selBC", [32, 32 * 128])
        inp("ccpre", [128, 4])
        inp("f_wqkv", [6, D, 3072])
        inp("f_wqkvT", [6, 16, 128, 8, 128])
        inp("f_woT", [6, 8, 128, 8, 128])
        inp("f_w1T", [6, 44, 128, 8, 128])
        inp("f_w2T", [6, 8, 128, 22, 128])
        inp("f_b1", [6, 128, 44])
        inp("f_lng", [6, 128, 8])
        inp("f_lnb2", [6, 128, 8])
        inp("f_b2", [6, 128, 8])
        inp("f_qs8", [6, 128, 1])
        inp("f_ks", [6, 128, 1])
        inp("f_ropeqA", [128, TC])
        inp("f_ropeqB", [128, TC])
        inp("f_ropekA", [128, TC])
        inp("f_ropekB", [128, TC])
        inp("f_ropekAh", [128, 256])
        inp("f_ropekBh", [128, 256])
        inp("f_maskup", [128, 128])
        inp("f_masklo", [128, 128])
        inp("f_m00", [128, 128])
        inp("f_m01", [128, 128])
        inp("f_m10", [128, 128])
        inp("f_selE2", [128, 16 * 64])
        inp("f_selN", [64, 16 * 128])
        inp("f_ccm", [128, 4])
        inp("f_l1w", [D, 3072])
        inp("f_l1wT", [16, 128, 8, 128])
        inp("f_l1projT", [8, 128, 8, 128])
        inp("f_l1bq", [128, 24])
        inp("f_l1bp", [128, 8])
        inp("f_l1bias", [128, 2048])
        inp("f_ccn", [128, 4])
        inp("f_ccpw", [128, 4])
        inp("f_wgT", [24, 128, 8, 128], I32)
        inp("f_wuT", [24, 128, 8, 128], I32)
        inp("f_wdT", [8, 128, 24, 128], I32)
        y_tok = nc.dram_tensor("y_tok", [TC, D], F32, kind="ExternalOutput").ap()

        ag_scan_in = nc.dram_tensor("ag_scan_in", [1, 128 * 136], F32)
        kb.y_scr = nc.dram_tensor("y_scr", [8, 128, 2048], F32, kind="ExternalOutput")
        ag_scan_out = nc.dram_tensor("ag_scan_out", [4, 128 * 136], F32)
        groups4 = [[0, 1, 2, 3], [4, 5, 6, 7]]

        kb.xT = [sb(f"xT{d}", [128, TC]) for d in range(DT)]
        kb.s_id = sb("s_id", [128, 128])
        kb.s_idb = sb("s_idb", [128, 128], BF16)
        kb.s_cb = sb("s_cb", [128, 8])
        kb.s_cbb = sb("s_cbb", [128, 8], BF16)
        kb.s_ones = sb("s_ones", [128, 128])
        kb.W = [sb(f"W{i}", [128, TC], BF16) for i in range(16)]
        kb.s_stats = sb("s_stats", [128, TC])
        kb.s_delta = sb("s_delta", [128, TC])
        kb.s_aT = sb("s_aT", [128, TC])
        kb.pA = kb.psum("pA", [128, 2048])
        kb.pB = kb.psum("pB", [128, 2048])
        kb.P = ([kb.pA[:, 512 * i:512 * (i + 1)] for i in range(4)]
                + [kb.pB[:, 512 * i:512 * (i + 1)] for i in range(4)])

        block = ctx.enter_context(nc.Block())
        xT, W, P = kb.xT, kb.W, kb.P
        s_id, s_cb = kb.s_id, kb.s_cb

        kb.load("yd", s_id[:, :], kb.ap["identity"], "s_id")
        kb.load("yd", kb.s_ones[:, :], kb.ap["ones_row"], "s_ones")
        kb.load("yd", s_cb[:, :], kb.ap["cbank"], "s_cb")
        S.checkpoint("yd")
        S.op("s", lambda sc: sc.activation(kb.s_idb[:, :], s_id[:, :], AF.Copy),
             reads=["s_id"], writes=["s_idb"])
        S.op("s", lambda sc: sc.activation(kb.s_cbb[:, :], s_cb[:, :], AF.Copy),
             reads=["s_cb"], writes=["s_cbb"])

        for tt in range(16):
            stg = kb.W[tt % 2][:, :].bitcast(F32)
            kb.load("yd", stg[:, :], kb.ap["x_tok"][128 * tt:128 * (tt + 1), :],
                    ("W", tt % 2, 0), tag=f"ldx{tt}")
            for d in range(DT):
                bk = (tt * 8 + d) % 4
                S.op("t", lambda t, stg=stg, d=d, bk=bk:
                     t.transpose(P[bk][:, 0:128], stg[:, 128 * d:128 * (d + 1)], s_id[:, :]),
                     reads=[("W", tt % 2, 0), "s_id"], writes=[("P", bk)],
                     tag=f"xtr{tt}_{d}")
                S.op("s", lambda sc, d=d, tt=tt, bk=bk:
                     sc.activation(xT[d][:, 128 * tt:128 * (tt + 1)], P[bk][:, 0:128],
                                   AF.Copy),
                     reads=[("P", bk)], writes=[("xT", d, tt // 4)],
                     tag=f"xev{tt}_{d}")

        if phase_limit >= 1:
            build_mamba(kb, groups4, ag_scan_in, ag_scan_out)

        n_l0 = max(0, min(6, phase_limit - 1))
        if n_l0 > 0:
            for d in range(DT):
                kb.store("gd", kb.y_scr.ap()[d, :, 0:1024], xT[d][:, 0:1024],
                         reads=xkeys(d), key=("yscr", d, 0), tag=f"x1sp{d}a")
                kb.store("gd", kb.y_scr.ap()[d, :, 1024:2048], xT[d][:, 1024:2048],
                         reads=xkeys(d), key=("yscr", d, 1), tag=f"x1sp{d}b")
            build_l0ffn(kb, groups4, n_l0)
        if phase_limit >= 8:
            for d in range(DT):
                w0, w1 = W[2 * (d % 2)], W[2 * (d % 2) + 1]
                k0 = ("W", 2 * (d % 2), 0)
                k1 = ("W", 2 * (d % 2) + 1, 0)
                kb.load("gd", w0[:, :].bitcast(F32), kb.y_scr.ap()[d, :, 0:1024],
                        k0, reads=[("yscr", d, 0)], tag=f"x1ld{d}a")
                kb.load("gd", w1[:, :].bitcast(F32), kb.y_scr.ap()[d, :, 1024:2048],
                        k1, reads=[("yscr", d, 1)], tag=f"x1ld{d}b")
                S.op("v", lambda v, d=d, w0=w0: v.tensor_tensor(
                    xT[d][:, 0:1024], xT[d][:, 0:1024], w0[:, :].bitcast(F32),
                    ALU.add),
                    reads=[("xT", d, 0), ("xT", d, 1), k0],
                    writes=[("xT", d, 0), ("xT", d, 1)], tag=f"x1ad{d}a")
                S.op("v", lambda v, d=d, w1=w1: v.tensor_tensor(
                    xT[d][:, 1024:2048], xT[d][:, 1024:2048], w1[:, :].bitcast(F32),
                    ALU.add),
                    reads=[("xT", d, 2), ("xT", d, 3), k1],
                    writes=[("xT", d, 2), ("xT", d, 3)], tag=f"x1ad{d}b")
            build_topln(kb, tag="ln2")
        if phase_limit >= 9:
            build_l1attn(kb, groups4)
        if phase_limit >= 10:
            build_l1ffn(kb)

        for tt in range(16):
            stg = kb.W[tt % 2][:, :].bitcast(F32)
            for d in range(DT):
                bk = 4 + (tt * 8 + d) % 4
                S.op("t", lambda t, d=d, tt=tt, bk=bk:
                     t.transpose(P[bk][:, 0:128], xT[d][:, 128 * tt:128 * (tt + 1)],
                                 s_id[:, :]),
                     reads=[("xT", d, tt // 4), "s_id"], writes=[("P", bk)],
                     tag=f"otr{tt}_{d}")
                S.op("s", lambda sc, d=d, bk=bk, stg=stg, tt=tt:
                     sc.activation(stg[:, 128 * d:128 * (d + 1)], P[bk][:, 0:128],
                                   AF.Copy),
                     reads=[("P", bk)], writes=[("W", tt % 2, 0)],
                     tag=f"oev{tt}_{d}")
            kb.store("yd", y_tok[128 * tt:128 * (tt + 1), :], stg[:, :],
                     reads=[("W", tt % 2, 0)],
                     key=("ytok", tt), tag=f"sty{tt}")

        S.finalize(nc, block)
    return nc, S.stats


def build_mamba(kb, groups4, ag_scan_in, ag_scan_out):
    nc, S = kb.nc, kb.S
    xT, W, P = kb.xT, kb.W, kb.P
    s_cb, s_cbb, s_ones, s_stats = kb.s_cb, kb.s_cbb, kb.s_ones, kb.s_stats
    mstack = ExitStack()
    sb = lambda name, shape, dtype=F32: mstack.enter_context(
        nc.sbuf_tensor(name, shape, dtype))
    kb._mamba_stack = mstack

    s_xproj = sb("s_xproj", [128, 8 * 96], BF16)
    s_dtw = sb("s_dtw", [64, D], BF16)
    s_dtb = sb("s_dtb", [128, 8])
    s_Aneg = sb("s_Aneg", [128, 128])
    s_mD = sb("s_mD", [128, 8])
    s_lna = sb("s_lna", [128, 8])
    s_lnb = sb("s_lnb", [128, 8])
    s_selBC = sb("s_selBC", [32, 32 * 128], BF16)
    s_bc = sb("s_bc", [32, TC], BF16)
    s_ccp = sb("s_ccp", [128, 4])
    s_om = sb("s_om", [128, 4])
    s_hlast = sb("s_hlast", [128, 128])
    s_sumd = sb("s_sumd", [128, 8])
    s_hin = sb("s_hin", [128, 128])
    s_agscan = sb("s_agscan", [128, 4 * 136])
    s_dbcd = sb("s_dbcd", [64, TC], BF16)
    s_delta = kb.s_delta
    s_aT = kb.s_aT
    s_h32 = sb("s_h32", [128, TC])
    s_Ptmp = sb("s_Ptmp", [128, 128])

    for k in range(8):
        kb.load("gd", s_xproj[:, 96 * k:96 * (k + 1)],
                kb.ap["m_xprojT"][128 * k:128 * (k + 1), :], ("xproj", k))
    kb.load("gd", s_dtw[:, :], kb.ap["m_dtwT"], "dtw")
    kb.load("gd", s_dtb[:, :], kb.ap["m_dtb"], "dtb")
    kb.load("gd", s_Aneg[:, :], kb.ap["m_Alog"], "Alog")
    kb.load("gd", s_mD[:, :], kb.ap["m_D"], "mD")
    kb.load("gd", s_lna[:, :], kb.ap["m_lna"], "lna")
    kb.load("gd", s_lnb[:, :], kb.ap["m_lnb"], "lnb")
    kb.load("gd", s_selBC[:, :], kb.ap["selBC"], "selBC")
    kb.load("gd", s_ccp[:, :], kb.ap["ccpre"], "ccp")
    S.checkpoint("gd")

    S.op("s", lambda sc: sc.activation(s_Aneg[:, :], s_Aneg[:, :], AF.Exp),
         reads=["Alog"], writes=["Aneg_t"])
    S.op("v", lambda v: v.tensor_scalar(s_Aneg[:, :], s_Aneg[:, :], -1.0, None,
                                        ALU.mult),
         reads=["Aneg_t"], writes=["Aneg"])
    S.op("v", lambda v: v.tensor_scalar(s_om[:, :], s_ccp[:, :], -1.0, 1.0,
                                        ALU.mult, ALU.add),
         reads=["ccp"], writes=["s_om"])

    for ch in range(4):
        for k in range(8):
            wslot = W[0 + k % 2]
            S.op("s", lambda sc, k=k, ch=ch, wslot=wslot:
                 sc.activation(wslot[:, 0:512], xT[k][:, 512 * ch:512 * (ch + 1)],
                               AF.Copy),
                 reads=[("xT", k, ch)], writes=[("W", 0 + k % 2, 0)],
                 tag=f"xb{ch}_{k}")
            S.group("t",
                    [lambda t, k=k, ch=ch, wslot=wslot:
                     t.matmul(P[ch][0:64, 0:512], s_xproj[:, 96 * k:96 * k + 64],
                              wslot[:, 0:512], start=(k == 0), stop=(k == 7)),
                     lambda t, k=k, ch=ch, wslot=wslot:
                     t.matmul(P[4 + ch][0:32, 0:512],
                              s_xproj[:, 96 * k + 64:96 * (k + 1)],
                              wslot[:, 0:512], start=(k == 0), stop=(k == 7))],
                    reads=[("xproj", k), ("W", 0 + k % 2, 0)],
                    writes=[("P", ch), ("P", 4 + ch)], tag=f"dbc{ch}_{k}")
        S.op("s", lambda sc, ch=ch: sc.activation(
            s_dbcd[:, 512 * ch:512 * (ch + 1)], P[ch][0:64, 0:512], AF.Copy),
            reads=[("P", ch)], writes=[("dbcd", ch)], tag=f"dbcev{ch}")
        S.op("v", lambda v, ch=ch: v.tensor_copy(
            s_bc[:, 512 * ch:512 * (ch + 1)], P[4 + ch][0:32, 0:512]),
            reads=[("P", 4 + ch)], writes=[("stats",)], tag=f"bcev{ch}")

    dX = [W[8 + i] for i in range(8)]

    def emit_delta(d, p):
        for ch in range(4):
            bk = ch % 2
            S.op("t", lambda t, d=d, ch=ch, bk=bk:
                 t.matmul(P[bk][:, 0:512], s_dtw[:, 128 * d:128 * (d + 1)],
                          s_dbcd[:, 512 * ch:512 * (ch + 1)], start=True, stop=True),
                 reads=["dtw", ("dbcd", ch)], writes=[("P", bk)], tag=f"dt{p}{d}_{ch}")
            S.op("s", lambda sc, d=d, bk=bk:
                 sc.activation(P[bk][:, 0:512], P[bk][:, 0:512], AF.Exp,
                               bias=s_dtb[:, d:d + 1]),
                 reads=[("P", bk), "dtb"], writes=[("P", bk)], tag=f"dte{p}{d}_{ch}")
            S.op("s", lambda sc, d=d, ch=ch, bk=bk:
                 sc.activation(s_delta[:, 512 * ch:512 * (ch + 1)], P[bk][:, 0:512],
                               AF.Ln, bias=s_cb[:, CB_ONE:CB_ONE + 1]),
                 reads=[("P", bk), "s_cb"], writes=[("delta", ch)], tag=f"dtl{p}{d}_{ch}")
        if p == 1:
            S.op("v", lambda v, d=d: v.tensor_tensor(dX[d][:, :], s_delta[:, :],
                                                     xT[d][:, :], ALU.mult),
                 reads=[("delta", c) for c in range(4)] + xkeys(d),
                 writes=wkeys(8 + d), tag=f"dx{d}")
            S.op("v", lambda v, d=d: v.tensor_reduce(s_sumd[:, d:d + 1],
                                                     s_delta[:, :], AX.X, ALU.add),
                 reads=[("delta", c) for c in range(4)], writes=[("sumd", d)],
                 tag=f"sumd{d}")

    aT = s_aT
    hT, bT = W[5], W[6]
    for d in range(8):
        emit_delta(d, 1)
        for n in range(16):
            S.op("s", lambda sc, d=d, n=n:
                 sc.activation(aT[:, :], s_delta[:, :], AF.Exp,
                               scale=s_Aneg[:, 16 * d + n:16 * d + n + 1]),
                 reads=[("delta", c) for c in range(4)] + ["Aneg"],
                 writes=[("aT", c) for c in range(4)], tag=f"p1a{d}_{n}")
            for ch in range(4):
                S.op("t", lambda t, n=n, ch=ch:
                     t.matmul(P[4 + ch][:, 0:512], s_selBC[:, 128 * n:128 * (n + 1)],
                              s_bc[:, 512 * ch:512 * (ch + 1)],
                              start=True, stop=True),
                     reads=["selBC"] + [("stats",)],
                     writes=[("P", 4 + ch)], tag=f"p1B{d}_{n}_{ch}")
                S.op("v", lambda v, d=d, ch=ch:
                     v.tensor_tensor(bT[:, 512 * ch:512 * (ch + 1)],
                                     dX[d][:, 512 * ch:512 * (ch + 1)],
                                     P[4 + ch][:, 0:512], ALU.mult),
                     reads=[("W", 8 + d, ch)] + [("P", 4 + ch)],
                     writes=[("W", 6, ch)], tag=f"p1b{d}_{n}_{ch}")
            S.op("v", lambda v:
                 v.tensor_tensor_scan(s_h32[:, :], aT[:, :], bT[:, :], 0.0,
                                      ALU.mult, ALU.add),
                 reads=[("aT", c) for c in range(4)] + wkeys(6), writes=[("h32",)],
                 tag=f"p1s{d}_{n}")
            S.op("v", lambda v, d=d, n=n:
                 v.tensor_copy(s_hlast[:, 16 * d + n:16 * d + n + 1],
                               s_h32[:, TC - 1:TC]),
                 reads=[("h32",)], writes=[("hlast", d, n)], tag=f"p1h{d}_{n}")

    kb.store("gd", ag_scan_in.ap()[0, 0:128 * 128].rearrange("(p c) -> p c", p=128),
             s_hlast[:, :],
             reads=[("hlast", d, n) for d in range(8) for n in range(16)],
             key="agin_h")
    kb.store("gd", ag_scan_in.ap()[0, 128 * 128:128 * 136].rearrange("(p c) -> p c", p=128),
             s_sumd[:, :], reads=[("sumd", d) for d in range(8)], key="agin_s")
    S.checkpoint("gd")
    S.cc(lambda g: g.collective_compute(
        "AllGather", ALU.bypass, replica_groups=groups4,
        ins=[ag_scan_in.ap()[:, :]], outs=[ag_scan_out.ap()[:, :]]),
        reads=["agin_h", "agin_s"], writes=["ag_scan_out"], tag="ag_scan")
    for j in range(4):
        kb.load("gd", s_agscan[:, 136 * j:136 * j + 128],
                ag_scan_out.ap()[j, 0:128 * 128].rearrange("(p c) -> p c", p=128),
                ("agscan", j), reads=["ag_scan_out"])
        kb.load("gd", s_agscan[:, 136 * j + 128:136 * (j + 1)],
                ag_scan_out.ap()[j, 128 * 128:128 * 136].rearrange("(p c) -> p c", p=128),
                ("agscan", j, "s"), reads=["ag_scan_out"])

    S.op("v", lambda v: v.memset(s_hin[:, :], 0.0), writes=[("hin",)], tag="hin0")
    for j in range(4):
        for d in range(8):
            S.op("s", lambda sc, j=j, d=d:
                 sc.activation(s_Ptmp[:, 16 * d:16 * (d + 1)],
                               s_Aneg[:, 16 * d:16 * (d + 1)], AF.Exp,
                               scale=s_agscan[:, 136 * j + 128 + d:136 * j + 129 + d]),
                 reads=[("agscan", j), ("agscan", j, "s"), "Aneg"], writes=[("Ptmp", d)], tag=f"cmP{j}_{d}")
        PK = [("Ptmp", d) for d in range(8)]
        S.op("v", lambda v, j=j: v.tensor_scalar(
            s_Ptmp[:, :], s_Ptmp[:, :], s_ccp[:, j:j + 1], None, ALU.mult),
            reads=PK + ["ccp"], writes=PK, tag=f"cmb{j}")
        S.op("v", lambda v, j=j: v.tensor_scalar(
            s_Ptmp[:, :], s_Ptmp[:, :], s_om[:, j:j + 1], None, ALU.add),
            reads=PK + ["s_om"], writes=PK, tag=f"cmb2{j}")
        S.op("v", lambda v: v.tensor_tensor(s_hin[:, :], s_hin[:, :],
                                            s_Ptmp[:, :], ALU.mult),
             reads=[("hin",)] + PK, writes=[("hin",)], tag=f"cmm{j}")
        S.op("v", lambda v, j=j: v.scalar_tensor_tensor(
            s_hin[:, :], s_agscan[:, 136 * j:136 * j + 128], s_ccp[:, j:j + 1],
            s_hin[:, :], ALU.mult, ALU.add),
            reads=[("agscan", j), "ccp", ("hin",)], writes=[("hin",)], tag=f"cma{j}")


    hC = W[7]
    for d in range(8):
        emit_delta(d, 2)
        for n in range(16):
            S.op("s", lambda sc, d=d, n=n:
                 sc.activation(aT[:, :], s_delta[:, :], AF.Exp,
                               scale=s_Aneg[:, 16 * d + n:16 * d + n + 1]),
                 reads=[("delta", c) for c in range(4)] + ["Aneg"],
                 writes=[("aT", c) for c in range(4)], tag=f"p2a{d}_{n}")
            for ch in range(4):
                S.op("t", lambda t, n=n, ch=ch:
                     t.matmul(P[4 + ch][:, 0:512], s_selBC[:, 128 * n:128 * (n + 1)],
                              s_bc[:, 512 * ch:512 * (ch + 1)],
                              start=True, stop=True),
                     reads=["selBC"] + [("stats",)],
                     writes=[("P", 4 + ch)], tag=f"p2B{d}_{n}_{ch}")
                S.op("v", lambda v, d=d, ch=ch:
                     v.tensor_tensor(bT[:, 512 * ch:512 * (ch + 1)],
                                     dX[d][:, 512 * ch:512 * (ch + 1)],
                                     P[4 + ch][:, 0:512], ALU.mult),
                     reads=[("W", 8 + d, ch)] + [("P", 4 + ch)],
                     writes=[("W", 6, ch)], tag=f"p2b{d}_{n}_{ch}")
            S.op("v", lambda v, d=d, n=n:
                 v.tensor_tensor_scan(hT[:, :], aT[:, :], bT[:, :],
                                      s_hin[:, 16 * d + n:16 * d + n + 1],
                                      ALU.mult, ALU.add),
                 reads=[("aT", c) for c in range(4)] + wkeys(6) + [("hin",)],
                 writes=wkeys(5), tag=f"p2s{d}_{n}")
            for ch in range(4):
                S.op("t", lambda t, n=n, ch=ch:
                     t.matmul(P[4 + ch][:, 0:512],
                              s_selBC[:, 128 * (16 + n):128 * (17 + n)],
                              s_bc[:, 512 * ch:512 * (ch + 1)],
                              start=True, stop=True),
                     reads=["selBC"] + [("stats",)],
                     writes=[("P", 4 + ch)], tag=f"p2C{d}_{n}_{ch}")
                S.op("v", lambda v, ch=ch:
                     v.tensor_tensor(hC[:, 512 * ch:512 * (ch + 1)],
                                     hT[:, 512 * ch:512 * (ch + 1)],
                                     P[4 + ch][:, 0:512], ALU.mult),
                     reads=wkeys(5, ch, ch + 1) + [("P", 4 + ch)],
                     writes=[("W", 7, ch)], tag=f"p2hc{d}_{n}_{ch}")
            yv0 = W[2].bitcast(F32)
            yv1 = W[3].bitcast(F32)
            if n == 0:
                S.op("v", lambda v, yv0=yv0: v.tensor_copy(yv0[:, :], hC[:, 0:1024]),
                     reads=[("W", 7, 0), ("W", 7, 1)],
                     writes=[("W", 2, c) for c in range(4)], tag=f"p2y0{d}")
                S.op("v", lambda v, yv1=yv1: v.tensor_copy(yv1[:, :], hC[:, 1024:2048]),
                     reads=[("W", 7, 2), ("W", 7, 3)],
                     writes=[("W", 3, c) for c in range(4)], tag=f"p2y1{d}")
            else:
                S.op("v", lambda v, yv0=yv0: v.tensor_tensor(
                    yv0[:, :], yv0[:, :], hC[:, 0:1024], ALU.add),
                    reads=[("W", 7, 0), ("W", 7, 1)] + [("W", 2, c) for c in range(4)],
                    writes=[("W", 2, c) for c in range(4)], tag=f"p2ya{d}_{n}")
                S.op("v", lambda v, yv1=yv1: v.tensor_tensor(
                    yv1[:, :], yv1[:, :], hC[:, 1024:2048], ALU.add),
                    reads=[("W", 7, 2), ("W", 7, 3)] + [("W", 3, c) for c in range(4)],
                    writes=[("W", 3, c) for c in range(4)], tag=f"p2yb{d}_{n}")
        S.op("v", lambda v, d=d: v.scalar_tensor_tensor(
            W[2].bitcast(F32)[:, :], xT[d][:, 0:1024], s_mD[:, d:d + 1],
            W[2].bitcast(F32)[:, :], ALU.mult, ALU.add),
            reads=xkeys(d, 0, 2) + ["mD"] + [("W", 2, c) for c in range(4)],
            writes=[("W", 2, c) for c in range(4)], tag=f"yD{d}a")
        S.op("v", lambda v, d=d: v.scalar_tensor_tensor(
            W[3].bitcast(F32)[:, :], xT[d][:, 1024:2048], s_mD[:, d:d + 1],
            W[3].bitcast(F32)[:, :], ALU.mult, ALU.add),
            reads=xkeys(d, 2, 4) + ["mD"] + [("W", 3, c) for c in range(4)],
            writes=[("W", 3, c) for c in range(4)], tag=f"yD{d}b")
        for ch in range(4):
            yv = W[2].bitcast(F32) if ch < 2 else W[3].bitcast(F32)
            sl = slice(512 * (ch % 2), 512 * (ch % 2 + 1))
            S.op("t", lambda t, yv=yv, sl=sl: t.matmul(
                P[2][0:1, 0:512], s_cb[:, CB_ONE:CB_ONE + 1], yv[:, sl],
                start=True, stop=True),
                reads=[("W", 2 + ch // 2, c) for c in range(4)] + ["s_cb"],
                writes=[("P", 2)], tag=f"yst{d}_{ch}")
            if d == 0:
                S.op("s", lambda sc, ch=ch: sc.activation(
                    s_stats[32:33, 512 * ch:512 * (ch + 1)], P[2][0:1, 0:512],
                    AF.Copy),
                    reads=[("P", 2)], writes=[("stats",)], tag=f"ysc{d}_{ch}")
            else:
                S.op("v", lambda v, ch=ch: v.tensor_tensor(
                    s_stats[32:33, 512 * ch:512 * (ch + 1)],
                    s_stats[32:33, 512 * ch:512 * (ch + 1)], P[2][0:1, 0:512],
                    ALU.add),
                    reads=[("P", 2), ("stats",)], writes=[("stats",)],
                    tag=f"ysa{d}_{ch}")
            S.op("s", lambda sc, yv=yv, sl=sl, ch=ch: sc.activation(
                s_aT[:, 512 * ch:512 * (ch + 1)], yv[:, sl], AF.Square),
                reads=[("W", 2 + ch // 2, c) for c in range(4)],
                writes=[("aT", ch)], tag=f"ysq{d}_{ch}")
            S.op("t", lambda t, ch=ch: t.matmul(
                P[3][0:1, 0:512], s_cb[:, CB_ONE:CB_ONE + 1],
                s_aT[:, 512 * ch:512 * (ch + 1)], start=True, stop=True),
                reads=[("aT", ch), "s_cb"], writes=[("P", 3)], tag=f"ys2{d}_{ch}")
            if d == 0:
                S.op("s", lambda sc, ch=ch: sc.activation(
                    s_stats[64:65, 512 * ch:512 * (ch + 1)], P[3][0:1, 0:512],
                    AF.Copy),
                    reads=[("P", 3)], writes=[("stats",)], tag=f"y2c{d}_{ch}")
            else:
                S.op("v", lambda v, ch=ch: v.tensor_tensor(
                    s_stats[64:65, 512 * ch:512 * (ch + 1)],
                    s_stats[64:65, 512 * ch:512 * (ch + 1)], P[3][0:1, 0:512],
                    ALU.add),
                    reads=[("P", 3), ("stats",)], writes=[("stats",)],
                    tag=f"y2a{d}_{ch}")
        kb.store("gd", kb.y_scr.ap()[d, :, 0:1024], W[2].bitcast(F32)[:, :],
                 reads=[("W", 2, c) for c in range(4)], key=("yscr", d, 0),
                 tag=f"yspill{d}a")
        kb.store("gd", kb.y_scr.ap()[d, :, 1024:2048], W[3].bitcast(F32)[:, :],
                 reads=[("W", 3, c) for c in range(4)], key=("yscr", d, 1),
                 tag=f"yspill{d}b")

    S.op("v", lambda v: v.tensor_scalar(s_stats[32:33, :], s_stats[32:33, :],
                                        1.0 / 1024.0, None, ALU.mult),
         reads=[("stats",) for c in range(4)], writes=[("stats",) for c in range(4)],
         tag="mu")
    for ch in range(4):
        S.op("s", lambda sc, ch=ch: sc.activation(
            P[ch][0:1, 0:512], s_stats[32:33, 512 * ch:512 * (ch + 1)], AF.Square,
            scale=s_cb[0:1, 2:3]),
            reads=[("stats",), "s_cb"], writes=[("P", ch)], tag=f"mu2_{ch}")
    S.op("v", lambda v: v.tensor_scalar(s_stats[64:65, :], s_stats[64:65, :],
                                        1.0 / 1023.0, None, ALU.mult),
         reads=[("stats",) for c in range(4)], writes=[("stats",) for c in range(4)],
         tag="sqs")
    for ch in range(4):
        S.op("v", lambda v, ch=ch: v.tensor_tensor(
            s_stats[64:65, 512 * ch:512 * (ch + 1)],
            s_stats[64:65, 512 * ch:512 * (ch + 1)], P[ch][0:1, 0:512], ALU.subtract),
            reads=[("stats",), ("P", ch)], writes=[("stats",)],
            tag=f"var{ch}")
    S.op("s", lambda sc: sc.activation(s_stats[64:65, :], s_stats[64:65, :], AF.Sqrt),
         reads=[("stats",) for c in range(4)],
         writes=[("stats",) for c in range(4)], tag="std")
    S.op("v", lambda v: v.tensor_scalar(s_stats[64:65, :], s_stats[64:65, :],
                                        1e-12, None, ALU.add),
         reads=[("stats",) for c in range(4)],
         writes=[("stats",) for c in range(4)], tag="stde")
    S.op("v", lambda v: v.reciprocal(s_stats[64:65, :], s_stats[64:65, :]),
         reads=[("stats",) for c in range(4)],
         writes=[("stats",) for c in range(4)], tag="dinv")
    for ch in range(4):
        S.op("t", lambda t, ch=ch: t.matmul(
            P[ch][:, 0:512], s_ones[32:33, :], s_stats[32:33, 512 * ch:512 * (ch + 1)],
            start=True, stop=True),
            reads=[("stats",), "s_ones"], writes=[("P", ch)], tag=f"bmu{ch}")
        S.op("t", lambda t, ch=ch: t.matmul(
            P[4 + ch][:, 0:512], s_ones[64:65, :], s_stats[64:65, 512 * ch:512 * (ch + 1)],
            start=True, stop=True),
            reads=[("stats",), "s_ones"], writes=[("P", 4 + ch)], tag=f"bdi{ch}")
    for d in range(8):
        kb.load("gd", s_h32[:, 0:1024], kb.y_scr.ap()[d, :, 0:1024],
                ("h32",), reads=[("yscr", d, 0)], tag=f"yld{d}a")
        kb.load("gd", s_h32[:, 1024:2048], kb.y_scr.ap()[d, :, 1024:2048],
                ("h32",), reads=[("yscr", d, 1)], tag=f"yld{d}b")
        for ch in range(4):
            S.op("v", lambda v, ch=ch: v.tensor_tensor(
                s_delta[:, 512 * ch:512 * (ch + 1)],
                s_h32[:, 512 * ch:512 * (ch + 1)],
                P[ch][:, 0:512], ALU.subtract),
                reads=[("h32",), ("P", ch)], writes=[("delta", ch)],
                tag=f"ap1{d}_{ch}")
            S.op("v", lambda v, ch=ch: v.tensor_tensor(
                s_aT[:, 512 * ch:512 * (ch + 1)], s_delta[:, 512 * ch:512 * (ch + 1)],
                P[4 + ch][:, 0:512], ALU.mult),
                reads=[("delta", ch), ("P", 4 + ch)], writes=[("aT", ch)],
                tag=f"ap2{d}_{ch}")
        S.op("v", lambda v, d=d: v.scalar_tensor_tensor(
            xT[d][:, :], s_aT[:, :], s_lna[:, d:d + 1], xT[d][:, :],
            ALU.mult, ALU.add),
            reads=[("aT", c) for c in range(4)] + ["lna"] + xkeys(d),
            writes=xkeys(d), tag=f"ap3{d}")
        S.op("v", lambda v, d=d: v.tensor_scalar(
            xT[d][:, :], xT[d][:, :], s_lnb[:, d:d + 1], None, ALU.add),
            reads=["lnb"] + xkeys(d), writes=xkeys(d), tag=f"ap4{d}")

    build_topln(kb, tag="ln1")
    mstack.close()


def build_topln(kb, tag):
    S = kb.S
    xT, W, P = kb.xT, kb.W, kb.P
    s_cb, s_cbb, s_ones, s_stats = kb.s_cb, kb.s_cbb, kb.s_ones, kb.s_stats
    for ch in range(4):
        for d in range(8):
            S.op("s", lambda sc, d=d, ch=ch: sc.activation(
                W[4][:, 0:512], xT[d][:, 512 * ch:512 * (ch + 1)], AF.Copy),
                reads=[("xT", d, ch)], writes=[("W", 4, 0)], tag=f"{tag}c{ch}_{d}")
            S.op("s", lambda sc, d=d, ch=ch: sc.activation(
                W[5][:, 0:512], xT[d][:, 512 * ch:512 * (ch + 1)], AF.Square),
                reads=[("xT", d, ch)], writes=[("W", 5, 0)], tag=f"{tag}q{ch}_{d}")
            S.op("t", lambda t, d=d, ch=ch: t.matmul(
                P[ch][0:1, 0:512], s_cbb[:, 0:1], W[4][:, 0:512],
                start=(d == 0), stop=(d == 7)),
                reads=[("W", 4, 0), "s_cbb"], writes=[("P", ch)], tag=f"{tag}s{ch}_{d}")
            S.op("t", lambda t, d=d, ch=ch: t.matmul(
                P[4 + ch][0:1, 0:512], s_cbb[:, 0:1], W[5][:, 0:512],
                start=(d == 0), stop=(d == 7)),
                reads=[("W", 5, 0), "s_cbb"], writes=[("P", 4 + ch)],
                tag=f"{tag}s2{ch}_{d}")
    for ch in range(4):
        S.op("s", lambda sc, ch=ch: sc.activation(
            s_stats[32:33, 512 * ch:512 * (ch + 1)], P[ch][0:1, 0:512], AF.Copy),
            reads=[("P", ch)], writes=[("stats",)], tag=f"{tag}e{ch}")
        S.op("s", lambda sc, ch=ch: sc.activation(
            s_stats[64:65, 512 * ch:512 * (ch + 1)], P[4 + ch][0:1, 0:512], AF.Copy),
            reads=[("P", 4 + ch)], writes=[("stats",)], tag=f"{tag}e2{ch}")
    S.op("v", lambda v: v.tensor_scalar(s_stats[32:33, :], s_stats[32:33, :],
                                        1.0 / 1024.0, None, ALU.mult),
         reads=[("stats",) for c in range(4)],
         writes=[("stats",) for c in range(4)], tag=f"{tag}mu")
    for ch in range(4):
        S.op("s", lambda sc, ch=ch: sc.activation(
            P[ch][0:1, 0:512], s_stats[32:33, 512 * ch:512 * (ch + 1)], AF.Square),
            reads=[("stats",)], writes=[("P", ch)], tag=f"{tag}mu2_{ch}")
    S.op("v", lambda v: v.tensor_scalar(s_stats[64:65, :], s_stats[64:65, :],
                                        1.0 / 1024.0, None, ALU.mult),
         reads=[("stats",) for c in range(4)],
         writes=[("stats",) for c in range(4)], tag=f"{tag}ex2")
    for ch in range(4):
        S.op("v", lambda v, ch=ch: v.tensor_tensor(
            s_stats[64:65, 512 * ch:512 * (ch + 1)],
            s_stats[64:65, 512 * ch:512 * (ch + 1)], P[ch][0:1, 0:512], ALU.subtract),
            reads=[("stats",), ("P", ch)], writes=[("stats",)],
            tag=f"{tag}var{ch}")
    S.op("s", lambda sc: sc.activation(s_stats[64:65, :], s_stats[64:65, :],
                                       AF.Sqrt, bias=s_cb[0:1, CB_EPS5:CB_EPS5 + 1]),
         reads=[("stats",) for c in range(4)] + ["s_cb"],
         writes=[("stats",) for c in range(4)], tag=f"{tag}sd")
    S.op("v", lambda v: v.reciprocal(s_stats[64:65, :], s_stats[64:65, :]),
         reads=[("stats",) for c in range(4)],
         writes=[("stats",) for c in range(4)], tag=f"{tag}rs")
    for ch in range(4):
        S.op("t", lambda t, ch=ch: t.matmul(
            P[ch][:, 0:512], s_ones[32:33, :], s_stats[32:33, 512 * ch:512 * (ch + 1)],
            start=True, stop=True),
            reads=[("stats",), "s_ones"], writes=[("P", ch)], tag=f"{tag}bm{ch}")
        S.op("t", lambda t, ch=ch: t.matmul(
            P[4 + ch][:, 0:512], s_ones[64:65, :], s_stats[64:65, 512 * ch:512 * (ch + 1)],
            start=True, stop=True),
            reads=[("stats",), "s_ones"], writes=[("P", 4 + ch)],
            tag=f"{tag}br{ch}")
    for d in range(8):
        for ch in range(4):
            S.op("v", lambda v, d=d, ch=ch: v.tensor_tensor(
                xT[d][:, 512 * ch:512 * (ch + 1)], xT[d][:, 512 * ch:512 * (ch + 1)],
                P[ch][:, 0:512], ALU.subtract),
                reads=[("xT", d, ch), ("P", ch)], writes=[("xT", d, ch)],
                tag=f"{tag}a{d}_{ch}")
            S.op("v", lambda v, d=d, ch=ch: v.tensor_tensor(
                xT[d][:, 512 * ch:512 * (ch + 1)], xT[d][:, 512 * ch:512 * (ch + 1)],
                P[4 + ch][:, 0:512], ALU.mult),
                reads=[("xT", d, ch), ("P", 4 + ch)], writes=[("xT", d, ch)],
                tag=f"{tag}b{d}_{ch}")


def _host_prep(inputs):
    x = np.ascontiguousarray(inputs["L0_attn_x"])
    ident = np.eye(128, dtype=np.float32)
    ones_row = np.ones((128, 128), np.float32)
    cbank = np.zeros((128, 8), np.float32)
    cbank[:, CB_ONE] = 1.0
    cbank[:, CB_EPS5] = 1e-5
    cbank[:, 2] = np.sqrt(1024.0 / 1023.0)
    cbank[:, CB_I127] = 1.0 / 127.0
    selBC = np.zeros((32, 32 * 128), np.float32)
    for n in range(32):
        selBC[n, 128 * n:128 * (n + 1)] = 1.0

    base = {
        "m_xprojT": np.ascontiguousarray(inputs["L0_attn_x_proj_w"].T),
        "m_dtwT": np.ascontiguousarray(inputs["L0_attn_dt_proj_w"].T),
        "m_dtb": np.ascontiguousarray(inputs["L0_attn_dt_proj_b"].reshape(8, 128).T),
        "m_Alog": np.ascontiguousarray(
            inputs["L0_attn_A_log"].reshape(8, 128, 16).transpose(1, 0, 2).reshape(128, 128)),
        "m_D": np.ascontiguousarray(inputs["L0_attn_D_param"].reshape(8, 128).T),
        "m_lna": np.ascontiguousarray(inputs["L0_attn_ln_a"].reshape(8, 128).T),
        "m_lnb": np.ascontiguousarray(inputs["L0_attn_ln_b"].reshape(8, 128).T),
        "identity": ident,
        "ones_row": ones_row,
        "cbank": cbank,
        "selBC": selBC,
    }
    NEGC = np.float32(-1e30)
    W1 = inputs["L0_ffn_W1"]
    W2 = inputs["L0_ffn_W2"]
    b1 = inputs["L0_ffn_b1"]
    w1p = np.zeros((6, D, 5632), np.float32)
    w1p[:, :, 0:2730] = W1[:, :, 0:2730]
    w1p[:, :, 2816:5546] = W1[:, :, 2730:5460]
    w2p = np.zeros((6, 2816, D), np.float32)
    w2p[:, 0:2730, :] = W2
    b1p = np.zeros((6, 5632), np.float32)
    b1p[:, 0:2730] = b1[:, 0:2730]
    b1p[:, 2816:5546] = b1[:, 2730:5460]
    qs = inputs["L0_ffn_q_scale"]
    ks = inputs["L0_ffn_k_scale"]
    def _tile(Wm, nk, ncb):
        return np.ascontiguousarray(
            Wm.reshape(Wm.shape[0], nk, 128, ncb, 128).transpose(0, 3, 2, 1, 4))
    base.update({
        "f_wqkv": np.ascontiguousarray(inputs["L0_ffn_Wqkv"]),
        "f_wqkvT": _tile(inputs["L0_ffn_Wqkv"][:, :, 0:2048], 8, 16),
        "f_woT": _tile(inputs["L0_ffn_Wo"], 8, 8),
        "f_w1T": _tile(w1p, 8, 44),
        "f_w2T": _tile(w2p, 22, 8),
        "f_b1": np.ascontiguousarray(b1p.reshape(6, 44, 128).transpose(0, 2, 1)),
        "f_lng": np.ascontiguousarray(inputs["L0_ffn_ln_g"].reshape(6, 8, 128).transpose(0, 2, 1)),
        "f_lnb2": np.ascontiguousarray(inputs["L0_ffn_ln_b"].reshape(6, 8, 128).transpose(0, 2, 1)),
        "f_b2": np.ascontiguousarray(inputs["L0_ffn_b2"].reshape(6, 8, 128).transpose(0, 2, 1)),
        "f_qs8": np.ascontiguousarray((8.0 * np.concatenate([qs, qs], 1))[:, :, None]),
        "f_ks": np.ascontiguousarray(np.concatenate([ks, ks], 1)[:, :, None]),
    })
    selE2 = np.zeros((128, 16 * 64), np.float32)
    selN = np.zeros((64, 16 * 128), np.float32)
    for p in range(8):
        for side in range(2):
            bidx = p + 8 * side
            for k in range(128):
                selE2[k, 64 * bidx + 32 * side + 2 * p + (k >= 64)] = 1.0
            for m in range(128):
                selN[32 * side + 2 * p + (m >= 64), 128 * bidx + m] = 1.0
    base["f_selE2"] = selE2
    base["f_selN"] = selN
    def _tile1(Wm, nk, ncb):
        return np.ascontiguousarray(
            Wm.reshape(nk, 128, ncb, 128).transpose(2, 1, 0, 3))
    base["f_l1w"] = np.ascontiguousarray(inputs["L1_attn_w_qkv"])
    base["f_l1wT"] = _tile1(inputs["L1_attn_w_qkv"][:, 0:2048], 8, 16)
    base["f_l1projT"] = _tile1(inputs["L1_attn_w_proj"], 8, 8)
    base["f_l1bq"] = np.ascontiguousarray(
        inputs["L1_attn_b_qkv"].reshape(24, 128).T)
    base["f_l1bp"] = np.ascontiguousarray(
        inputs["L1_attn_b_proj"].reshape(8, 128).T)
    tbl = inputs["L1_attn_bias_table"]
    coords = np.stack(np.meshgrid(np.arange(8), np.arange(8), indexing="ij"))
    cf = coords.reshape(2, -1)
    rel = cf[:, :, None] - cf[:, None, :]
    rel = rel.transpose(1, 2, 0).astype(np.int64)
    rel[..., 0] += 7
    rel[..., 1] += 7
    rel[..., 0] *= 15
    idx = rel.sum(-1)[:8, :8]
    l1b = np.full((128, 2048), -1e30, np.float32)
    for h in range(16):
        bias8 = tbl[idx, h]
        for a in range(16):
            l1b[8 * a:8 * a + 8, 128 * h + 8 * a:128 * h + 8 * a + 8] = bias8.T
    base["f_l1bias"] = l1b
    base["f_wgT"] = _tile1(inputs["L1_ffn_w_gate"].T, 8, 24)
    base["f_wuT"] = _tile1(inputs["L1_ffn_w_up"].T, 8, 24)
    base["f_wdT"] = _tile1(inputs["L1_ffn_w_down"].T, 24, 8)
    jj, ii = np.meshgrid(np.arange(128), np.arange(128), indexing="ij")
    mku = np.where(jj >= ii, 0.0, NEGC).astype(np.float32)
    mkl = np.where(jj <= ii, 0.0, NEGC).astype(np.float32)
    allneg = np.full((128, 128), NEGC, np.float32)
    zeros128 = np.zeros((128, 128), np.float32)
    base["f_maskup"] = mku
    base["f_masklo"] = mkl

    def rope_tabs(tglob):
        half = np.arange(0, 64, 2, dtype=np.float64)
        inv_freq = 1.0 / 10000.0 ** (half / 64.0)
        fr = tglob[:, None] * inv_freq[None, :]
        fr = np.concatenate([fr, fr], 1)
        base_scale = (half + 0.4 * 64) / (1.4 * 64)
        power = (tglob - 8192 // 2) / (256 // 2)
        sc = base_scale[None, :] ** power[:, None]
        sc = np.concatenate([sc, sc], 1)
        cs, sn = np.cos(fr), np.sin(fr)
        sgn = np.ones(64); sgn[:32] = -1.0
        Aq = (cs * sc).T
        Bq = (sn * sc * sgn[None, :]).T
        inv = 1.0 / sc
        Ak = (cs * inv).T
        Bk = (sn * inv * sgn[None, :]).T
        dup = lambda M: np.concatenate([M, M], 0).astype(np.float32)
        return dup(Aq), dup(Bq), dup(Ak), dup(Bk)

    in_maps = []
    for c in range(NCORES):
        b, r = divmod(c, 4)
        m = dict(base)
        m["x_tok"] = np.ascontiguousarray(x[b, TC * r:TC * (r + 1), :])
        ccp = np.zeros((128, 4), np.float32)
        ccp[:, :r] = 1.0
        m["ccpre"] = ccp
        ccm = np.zeros((128, 4), np.float32)
        if r > 0:
            ccm[:, r - 1] = 1.0
        m["f_ccm"] = ccm
        tglob = np.arange(TC * r, TC * (r + 1), dtype=np.float64)
        qA, qB, kA, kB = rope_tabs(tglob)
        m["f_ropeqA"], m["f_ropeqB"] = qA, qB
        m["f_ropekA"], m["f_ropekB"] = kA, kB
        th = np.arange(TC * r - 256, TC * r, dtype=np.float64)
        _, _, kAh, kBh = rope_tabs(th)
        m["f_ropekAh"], m["f_ropekBh"] = kAh, kBh
        ccn = np.zeros((128, 4), np.float32)
        ccn[:, (r + 1) % 4] = 1.0
        m["f_ccn"] = ccn
        ccw = np.zeros((128, 4), np.float32)
        ccw[:, (r - 1) % 4] = 1.0
        m["f_ccpw"] = ccw
        m["f_m00"] = allneg if r == 0 else mku
        m["f_m01"] = allneg if r == 0 else zeros128
        m["f_m10"] = allneg if r == 0 else mku
        in_maps.append(m)
    return in_maps


def kernel(**inputs):
    in_maps = _host_prep(inputs)
    nc, stats = build_nc(PHASE_LIMIT)
    res = run_bass_kernel_spmd(nc, in_maps, core_ids=list(range(NCORES)))
    out = np.empty((B, L, D), np.float32)
    for c in range(NCORES):
        b, r = divmod(c, 4)
        out[b, TC * r:TC * (r + 1), :] = res.results[c]["y_tok"]
    return out


def build_l0ffn(kb, groups4, n_layers):
    nc, S = kb.nc, kb.S
    xT, W, P = kb.xT, kb.W, kb.P
    pA, pB = kb.pA, kb.pB
    s_cb, s_cbb, s_ones, s_stats = kb.s_cb, kb.s_cbb, kb.s_ones, kb.s_stats
    s_delta, s_aT = kb.s_delta, kb.s_aT
    sb = kb.sb
    STK = ("stats",)
    DK = lambda c: ("delta", c)
    AK = lambda c: ("aT", c)

    s_v = kb.s_v = sb("s_v", [128, 8 * 1040], BF16)
    s_wbuf = kb.s_wbuf = sb("s_wbuf", [128, 2816], BF16)
    s_zinv = kb.s_zinv = sb("s_zinv", [128, 16])
    l0stack = ExitStack()
    sb = lambda name, shape, dtype=F32: l0stack.enter_context(
        nc.sbuf_tensor(name, shape, dtype))
    s_khalo = sb("s_khalo", [128, 8 * 256], BF16)
    s_kcar = s_khalo
    s_xcol = sb("s_xcol", [128, 8])
    s_rqA = sb("s_rqA", [128, TC], BF16)
    s_rqB = sb("s_rqB", [128, TC], BF16)
    s_rkA = sb("s_rkA", [128, TC], BF16)
    s_rkB = sb("s_rkB", [128, TC], BF16)
    s_rkAh = sb("s_rkAh", [128, 256], BF16)
    s_rkBh = sb("s_rkBh", [128, 256], BF16)
    s_mku = sb("s_mku", [128, 128], BF16)
    s_mkl = sb("s_mkl", [128, 128], BF16)
    s_m00 = sb("s_m00", [128, 128], BF16)
    s_m01 = sb("s_m01", [128, 128], BF16)
    s_m10 = sb("s_m10", [128, 128], BF16)
    s_selE2 = sb("s_selE2", [128, 16 * 64], BF16)
    s_selN = sb("s_selN", [64, 16 * 128], BF16)
    s_ninv = sb("s_ninv", [64, TC], BF16)
    s_b1 = sb("s_b1", [128, 44])
    s_lng = sb("s_lng", [128, 8])
    s_lnbv = sb("s_lnbv", [128, 8])
    s_b2v = sb("s_b2v", [128, 8])
    s_qs8 = sb("s_qs8", [128, 1])
    s_ksc = sb("s_ksc", [128, 1])
    s_ccm = sb("s_ccm", [128, 4])

    ag_kv_in = nc.dram_tensor("ag_kv_in", [1, 8 * 128 * 256 + 128 * 4], BF16)
    ag_kv_out = nc.dram_tensor("ag_kv_out", [4, 8 * 128 * 256 + 128 * 4], BF16)
    ag_x2_in = nc.dram_tensor("ag_x2_in", [1, 128 * 4], F32)
    ag_x2_out = nc.dram_tensor("ag_x2_out", [4, 128 * 4], F32)

    FENCE = [("xT", d, c) for d in range(8) for c in range(4)]
    for nm, dst in [("f_ropeqA", s_rqA), ("f_ropeqB", s_rqB), ("f_ropekA", s_rkA),
                    ("f_ropekB", s_rkB), ("f_ropekAh", s_rkAh), ("f_ropekBh", s_rkBh),
                    ("f_maskup", s_mku), ("f_masklo", s_mkl), ("f_m00", s_m00),
                    ("f_m01", s_m01), ("f_m10", s_m10), ("f_selE2", s_selE2),
                    ("f_selN", s_selN), ("f_ccm", s_ccm)]:
        kb.load("gd", dst[:, :], kb.ap[nm], (nm,), reads=FENCE)
    S.checkpoint("gd")

    for t8 in range(8):
        S.op("v", lambda v, t8=t8: v.memset(
            s_v[:, 1040 * t8:1040 * (t8 + 1)]
            .rearrange("p (h e) -> p h e", e=65)[:, :, 64:65], 1.0),
            reads=FENCE, writes=[("sv", t8)], tag=f"vone{t8}")

    WBH = lambda par: [("wb", 2 * par), ("wb", 2 * par + 1)]

    def wload(tag, tile3d, par):
        kb.load("gd", s_wbuf[:, 1024 * par:1024 * (par + 1)]
                .rearrange("p (k c) -> p k c", k=8),
                tile3d, WBH(par), tag=tag)

    def ain(d):
        sl = W[d % 4][:, 1024 * (d // 4):1024 * (d // 4) + 1024]
        return sl, [("W", d % 4, 2 * (d // 4) + i) for i in range(2)]

    def qsl(p):
        return W[8 + p // 2][:, 1024 * (p % 2):1024 * (p % 2) + 1024]

    def qkeys(p):
        return [("W", 8 + p // 2, 2 * (p % 2) + i) for i in range(2)]

    def ksl(p):
        return W[12 + p // 2][:, 1024 * (p % 2):1024 * (p % 2) + 1024]

    def kkeys(p):
        return [("W", 12 + p // 2, 2 * (p % 2) + i) for i in range(2)]

    s_halo = W[4]
    HALOK = [("W", 4, c) for c in range(4)]
    otok = W[4]
    OTK = HALOK
    s_attnT = W[5]
    ATK = [("W", 5, c) for c in range(4)]

    def vhalo_ap(t2, h):
        return W[6 + t2][:, 65 * h:65 * h + 65]

    def vhk(t2):
        return [("W", 6 + t2, c) for c in range(3)]

    def emit_ain_half(l, hh, pref):
        c0 = 1024 * hh
        for d in range(8):
            sl, keys = ain(d)
            if d < 4:
                S.op("s", lambda sc, sl=sl, d=d, c0=c0: sc.activation(
                    sl[:, :], xT[d][:, c0:c0 + 1024], AF.Copy),
                    reads=[("xT", d, 2 * hh), ("xT", d, 2 * hh + 1)],
                    writes=keys, tag=f"{pref}ain{d}")
            else:
                S.op("s", lambda sc, sl=sl, d=d, c0=c0: sc.activation(
                    sl[:, 1:1024], xT[d][:, c0:c0 + 1023], AF.Copy),
                    reads=[("xT", d, 2 * hh), ("xT", d, 2 * hh + 1)],
                    writes=keys, tag=f"{pref}ain{d}")
                S.op("v", lambda v, sl=sl, d=d, hh=hh: v.tensor_copy(
                    sl[:, 0:1], s_xcol[:, 4 * hh + d - 4:4 * hh + d - 3]),
                    reads=[("xcol",)], writes=[keys[0]], tag=f"{pref}c0{d}")

    def rot_pair(l, tagp, src, skeys, scc, sck, bcast_ap, bck, tA, tAk, tB, tBk,
                 scr, klo, khi, ncols):
        h = ncols
        S.op("v", lambda v: v.scalar_tensor_tensor(
            scr[:, 0:h], src[:, :], scc[:, 0:1], bcast_ap, ALU.mult, ALU.mult),
            reads=skeys + [sck] + bck, writes=klo, tag=f"{tagp}qn")
        for qq, qs_ in [(0, 32), (32, 0), (64, 96), (96, 64)]:
            S.op("s", lambda sc, qq=qq, qs_=qs_: sc.activation(
                scr[qq:qq + 32, h:2 * h], scr[qs_:qs_ + 32, 0:h], AF.Copy),
                reads=klo, writes=khi, tag=f"{tagp}sw{qq}")
        S.op("v", lambda v: v.tensor_tensor(scr[:, h:2 * h], scr[:, h:2 * h],
                                            tB, ALU.mult),
             reads=khi + [tBk], writes=khi, tag=f"{tagp}rB")
        S.op("v", lambda v: v.tensor_tensor(src[:, :], scr[:, 0:h], tA, ALU.mult),
             reads=klo + [tAk], writes=skeys, tag=f"{tagp}rA")
        S.op("v", lambda v: v.tensor_tensor(src[:, :], src[:, :], scr[:, h:2 * h],
                                            ALU.add),
             reads=skeys + khi, writes=skeys, tag=f"{tagp}rS")

    def qk_prep(l, hh):
        c0 = 1024 * hh
        for p in range(8):
            for side in range(2):
                src = qsl(p) if side == 0 else ksl(p)
                keys = qkeys(p) if side == 0 else kkeys(p)
                scr = otok if (2 * p + side) % 2 == 0 else s_attnT
                sk = OTK if (2 * p + side) % 2 == 0 else ATK
                S.op("s", lambda sc, src=src, scr=scr: sc.activation(
                    scr[:, 0:1024], src[:, :], AF.Square),
                    reads=keys, writes=sk[:2], tag=f"L{l}h{hh}sq{side}_{p}")
                for ch in range(2):
                    S.op("t", lambda t, p=p, side=side, ch=ch, scr=scr: t.matmul(
                        pB[0:64, 512 * ch:512 * (ch + 1)],
                        s_selE2[:, 64 * (p + 8 * side):64 * (p + 8 * side) + 64],
                        scr[:, 512 * ch:512 * ch + 512],
                        start=(p == 0 and side == 0), stop=(p == 7 and side == 1)),
                        reads=[sk[ch], ("f_selE2",)], writes=[("P", 4 + ch)],
                        tag=f"L{l}h{hh}nm{side}_{p}_{ch}")
        S.op("s", lambda sc: sc.activation(s_stats[0:64, c0:c0 + 1024],
                                           pB[0:64, 0:1024], AF.Sqrt),
             reads=[("P", 4), ("P", 5)], writes=[STK], tag=f"L{l}h{hh}nsq")
        S.op("v", lambda v: v.tensor_scalar(s_stats[0:64, c0:c0 + 1024],
                                            s_stats[0:64, c0:c0 + 1024],
                                            1e-12, None, ALU.max),
             reads=[STK], writes=[STK], tag=f"L{l}h{hh}nmx")
        S.op("v", lambda v: v.reciprocal(s_stats[0:64, c0:c0 + 1024],
                                         s_stats[0:64, c0:c0 + 1024]),
             reads=[STK], writes=[STK], tag=f"L{l}h{hh}nrc")
        S.op("v", lambda v: v.tensor_copy(s_ninv[:, c0:c0 + 1024],
                                          s_stats[0:64, c0:c0 + 1024]),
             reads=[STK], writes=[("ninv",)], tag=f"L{l}h{hh}ncp")
        for p in range(8):
            for side in range(2):
                idx = 2 * p + side
                src = qsl(p) if side == 0 else ksl(p)
                keys = qkeys(p) if side == 0 else kkeys(p)
                tA = (s_rqA if side == 0 else s_rkA)[:, c0:c0 + 1024]
                tB = (s_rqB if side == 0 else s_rkB)[:, c0:c0 + 1024]
                tAk = ("f_ropeqA",) if side == 0 else ("f_ropekA",)
                tBk = ("f_ropeqB",) if side == 0 else ("f_ropekB",)
                scc = s_qs8 if side == 0 else s_ksc
                sck = "lv_qs" if side == 0 else "lv_ks"
                scr = otok if idx % 2 == 0 else s_attnT
                scrk = OTK if idx % 2 == 0 else ATK
                pbk = 2 * (idx % 2)
                S.op("t", lambda t, p=p, side=side, pbk=pbk: t.matmul(
                    pB[:, 512 * pbk:512 * pbk + 512],
                    s_selN[:, 128 * (p + 8 * side):128 * (p + 8 * side) + 128],
                    s_ninv[:, c0:c0 + 512], start=True, stop=True),
                    reads=[("ninv",), ("f_selN",)], writes=[("P", 4 + pbk)],
                    tag=f"L{l}h{hh}nb{idx}_0")
                S.op("t", lambda t, p=p, side=side, pbk=pbk: t.matmul(
                    pB[:, 512 * (pbk + 1):512 * (pbk + 1) + 512],
                    s_selN[:, 128 * (p + 8 * side):128 * (p + 8 * side) + 128],
                    s_ninv[:, c0 + 512:c0 + 1024], start=True, stop=True),
                    reads=[("ninv",), ("f_selN",)], writes=[("P", 4 + pbk + 1)],
                    tag=f"L{l}h{hh}nb{idx}_1")
                rot_pair(l, f"L{l}h{hh}p{idx}", src, keys, scc, sck,
                         pB[:, 512 * pbk:512 * pbk + 1024],
                         [("P", 4 + pbk), ("P", 5 + pbk)], tA, tAk, tB, tBk,
                         scr, scrk[:2], scrk[2:], 1024)

    def halo_k_prep(l):
        for p in range(8):
            S.op("s", lambda sc, p=p: sc.activation(
                s_attnT[:, 0:256], s_khalo[:, 256 * p:256 * (p + 1)], AF.Square),
                reads=[("khalo",)], writes=[ATK[0]], tag=f"L{l}hsq{p}")
            S.op("t", lambda t, p=p: t.matmul(
                pB[0:64, 0:256],
                s_selE2[:, 64 * (p + 8):64 * (p + 8) + 64],
                s_attnT[:, 0:256], start=(p == 0), stop=(p == 7)),
                reads=[ATK[0], ("f_selE2",)], writes=[("P", 4)],
                tag=f"L{l}hnm{p}")
        S.op("s", lambda sc: sc.activation(s_stats[0:64, 0:256],
                                           pB[0:64, 0:256], AF.Sqrt),
             reads=[("P", 4)], writes=[STK], tag=f"L{l}hnsq")
        S.op("v", lambda v: v.tensor_scalar(s_stats[0:64, 0:256],
                                            s_stats[0:64, 0:256], 1e-12, None,
                                            ALU.max),
             reads=[STK], writes=[STK], tag=f"L{l}hnmx")
        S.op("v", lambda v: v.reciprocal(s_stats[0:64, 0:256], s_stats[0:64, 0:256]),
             reads=[STK], writes=[STK], tag=f"L{l}hnrc")
        S.op("v", lambda v: v.tensor_copy(s_ninv[:, 0:256], s_stats[0:64, 0:256]),
             reads=[STK], writes=[("ninvh",)], tag=f"L{l}hncp")
        for p in range(8):
            S.op("t", lambda t, p=p: t.matmul(
                pB[:, 1024:1280],
                s_selN[:, 128 * (p + 8):128 * (p + 8) + 128],
                s_ninv[:, 0:256], start=True, stop=True),
                reads=[("ninvh",), ("f_selN",)], writes=[("P", 6)],
                tag=f"L{l}hnb{p}")
            rot_pair(l, f"L{l}hp{p}",
                     s_khalo[:, 256 * p:256 * (p + 1)], [("khalo",)],
                     s_ksc, "lv_ks", pB[:, 1024:1280], [("P", 6)],
                     s_rkAh[:, :], ("f_ropekAh",), s_rkBh[:, :], ("f_ropekBh",),
                     s_attnT, [ATK[0]], [ATK[0]], 256)

    for l in range(n_layers):
        Lw = kb.ap["f_wqkv"][l]
        LwT = kb.ap["f_wqkvT"][l]
        LoT = kb.ap["f_woT"][l]
        L1wT = kb.ap["f_w1T"][l]
        L2wT = kb.ap["f_w2T"][l]
        for nm, dst, key in [("f_b1", s_b1, "lv_b1"), ("f_lng", s_lng, "lv_g"),
                             ("f_lnb2", s_lnbv, "lv_b"), ("f_b2", s_b2v, "lv_b2"),
                             ("f_qs8", s_qs8, "lv_qs"), ("f_ks", s_ksc, "lv_ks")]:
            kb.load("gd", dst[:, :], kb.ap[nm][l], key,
                    reads=FENCE if l == 0 else [], tag=f"L{l}{key}")

        for d in range(8):
            if d < 4:
                S.op("s", lambda sc, d=d: sc.activation(
                    s_khalo[:, 256 * d:256 * (d + 1)], xT[d][:, 1792:2048], AF.Copy),
                    reads=[("xT", d, 3)], writes=[("khalo",)], tag=f"L{l}tl{d}")
            else:
                S.op("s", lambda sc, d=d: sc.activation(
                    s_khalo[:, 256 * d:256 * (d + 1)], xT[d][:, 1791:2047], AF.Copy),
                    reads=[("xT", d, 3)], writes=[("khalo",)], tag=f"L{l}tl{d}")
        for i, d in enumerate(range(4, 8)):
            S.op("s", lambda sc, d=d, i=i: sc.activation(
                s_attnT[:, i:i + 1], xT[d][:, TC - 1:TC], AF.Copy),
                reads=[("xT", d, 3)], writes=[ATK[0]], tag=f"L{l}xc{i}")
            S.op("v", lambda v, d=d, i=i: v.tensor_copy(
                s_xcol[:, 4 + i:5 + i], xT[d][:, 1023:1024]),
                reads=[("xT", d, 1)], writes=[("xcol",)], tag=f"L{l}xm{i}")
        kb.store("gd", ag_kv_in.ap()[0, 0:8 * 128 * 256]
                 .rearrange("(p c) -> p c", p=128),
                 s_khalo[:, :], reads=[("khalo",)], key=("agkv",), tag=f"L{l}agst")
        kb.store("gd", ag_kv_in.ap()[0, 8 * 128 * 256:]
                 .rearrange("(p c) -> p c", p=128),
                 s_attnT[:, 0:4], reads=[ATK[0]], key=("agkvx",), tag=f"L{l}agstx")
        S.checkpoint("gd")
        S.cc(lambda g: g.collective_compute(
            "AllGather", ALU.bypass, replica_groups=groups4,
            ins=[ag_kv_in.ap()[:, :]], outs=[ag_kv_out.ap()[:, :]]),
            reads=[("agkv",), ("agkvx",)], writes=["ag_kv_out"], tag=f"L{l}agkv")
        S.op("v", lambda v: v.memset(s_halo[:, :], 0.0), writes=HALOK, tag=f"L{l}hz")
        S.op("v", lambda v: v.memset(s_xcol[:, 0:4], 0.0), writes=[("xcol",)],
             tag=f"L{l}xz")
        for j in range(4):
            kb.load("gd", s_attnT[:, :],
                    ag_kv_out.ap()[j, 0:8 * 128 * 256]
                    .rearrange("(p c) -> p c", p=128),
                    list(ATK), reads=["ag_kv_out"], tag=f"L{l}agld{j}")
            kb.load("gd", s_khalo[:, 2044:2048],
                    ag_kv_out.ap()[j, 8 * 128 * 256:]
                    .rearrange("(p c) -> p c", p=128),
                    ("khalo",), reads=["ag_kv_out"], tag=f"L{l}agldx{j}")
            S.op("v", lambda v, j=j: v.scalar_tensor_tensor(
                s_halo[:, :], s_attnT[:, :], s_ccm[:, j:j + 1], s_halo[:, :],
                ALU.mult, ALU.add),
                reads=list(ATK) + [("f_ccm",)] + HALOK, writes=HALOK,
                tag=f"L{l}hsel{j}")
            S.op("v", lambda v, j=j: v.scalar_tensor_tensor(
                s_xcol[:, 0:4], s_khalo[:, 2044:2048], s_ccm[:, j:j + 1],
                s_xcol[:, 0:4], ALU.mult, ALU.add),
                reads=[("khalo",), ("f_ccm",), ("xcol",)], writes=[("xcol",)],
                tag=f"L{l}xsel{j}")

        for p in range(8):
            par = p % 2
            wload(f"L{l}hkw{p}", LwT[8 + p], par)
            fns = [lambda t, k=k, par=par: t.matmul(
                pB[:, 0:256],
                s_wbuf[:, 1024 * par + 128 * k:1024 * par + 128 * (k + 1)],
                s_halo[:, 256 * k:256 * (k + 1)],
                start=(k == 0), stop=(k == 7)) for k in range(8)]
            S.group("t", fns, reads=WBH(par) + HALOK, writes=[("P", 4)],
                    tag=f"L{l}hk{p}")
            S.op("s", lambda sc, p=p: sc.activation(
                s_khalo[:, 256 * p:256 * (p + 1)], pB[:, 0:256], AF.Copy),
                reads=[("P", 4)], writes=[("khalo",)], tag=f"L{l}hke{p}")
        for t2 in range(2):
            S.op("v", lambda v, t2=t2: v.memset(
                W[6 + t2][:, 0:1040].rearrange("p (h e) -> p h e", e=65)[:, :, 64:65],
                1.0),
                writes=vhk(t2), tag=f"L{l}vhone{t2}")
        def vw(k):
            if k < 4:
                return s_wbuf[:, 512 * k:512 * (k + 1)], ("wb", k)
            return W[5][:, 512 * (k - 4):512 * (k - 4) + 512], ATK[k - 4]
        for fc in range(2):
            for k in range(8):
                dst, key = vw(k)
                kb.load("gd", dst,
                        Lw[128 * k:128 * (k + 1),
                           2048 + 512 * fc:2048 + 512 * (fc + 1)],
                        key, tag=f"L{l}hvw{fc}_{k}")
            wrd = [vw(k)[1] for k in range(8)]
            for t2 in range(2):
                fns = [lambda t, t2=t2, k=k: t.matmul(
                    pB[:, 512 * (1 + t2):512 * (2 + t2)],
                    s_halo[:, 256 * k + 128 * t2:256 * k + 128 * t2 + 128],
                    vw(k)[0], start=(k == 0), stop=(k == 7))
                    for k in range(8)]
                S.group("t", fns, reads=wrd + HALOK,
                        writes=[("P", 5 + t2)], tag=f"L{l}hv{fc}_{t2}")
                S.op("s", lambda sc, t2=t2, fc=fc: sc.activation(
                    W[6 + t2][:, 520 * fc:520 * fc + 520]
                    .rearrange("p (h e) -> p h e", e=65)[:, :, 0:64],
                    pB[:, 512 * (1 + t2):512 * (2 + t2)]
                    .rearrange("p (h e) -> p h e", e=64), AF.Copy),
                    reads=[("P", 5 + t2)], writes=vhk(t2),
                    tag=f"L{l}hve{fc}_{t2}")
        halo_k_prep(l)

        for hh in range(2):
            c0 = 1024 * hh
            emit_ain_half(l, hh, f"L{l}h{hh}")
            for p in range(8):
                for side in range(2):
                    par = p % 2
                    wload(f"L{l}h{hh}w{side}_{p}", LwT[8 * side + p], par)
                    dstsl = qsl(p) if side == 0 else ksl(p)
                    dkeys = qkeys(p) if side == 0 else kkeys(p)
                    po = 1024 * (p % 2)
                    for ch in range(2):
                        fns = []
                        rds = WBH(par)
                        for k in range(8):
                            asl, akeys = ain(k)
                            rds.append(akeys[ch])
                            fns.append(lambda t, k=k, par=par, ch=ch, asl=asl,
                                       po=po: t.matmul(
                                pA[:, po + 512 * ch:po + 512 * (ch + 1)],
                                s_wbuf[:, 1024 * par + 128 * k:1024 * par + 128 * (k + 1)],
                                asl[:, 512 * ch:512 * ch + 512],
                                start=(k == 0), stop=(k == 7)))
                        S.group("t", fns, reads=rds,
                                writes=[("P", 2 * (p % 2) + ch)],
                                tag=f"L{l}h{hh}qk{side}_{p}_{ch}")
                        S.op("s", lambda sc, dstsl=dstsl, ch=ch, po=po: sc.activation(
                            dstsl[:, 512 * ch:512 * ch + 512],
                            pA[:, po + 512 * ch:po + 512 * (ch + 1)], AF.Copy),
                            reads=[("P", 2 * (p % 2) + ch)], writes=[dkeys[ch]],
                            tag=f"L{l}h{hh}qke{side}_{p}_{ch}")
            def vw(k):
                if k < 4:
                    return s_wbuf[:, 512 * k:512 * (k + 1)], ("wb", k)
                return W[5][:, 512 * (k - 4):512 * (k - 4) + 512], ATK[k - 4]
            for fc in range(2):
                for k in range(8):
                    dst, key = vw(k)
                    kb.load("gd", dst,
                            Lw[128 * k:128 * (k + 1),
                               2048 + 512 * fc:2048 + 512 * (fc + 1)],
                            key, tag=f"L{l}h{hh}vw{fc}_{k}")
                wrd = [vw(k)[1] for k in range(8)]
                for t8 in range(8):
                    fns = []
                    rds = list(wrd)
                    for k in range(8):
                        asl, akeys = ain(k)
                        rds.append(akeys[(128 * t8) // 512])
                        fns.append(lambda t, t8=t8, k=k, asl=asl: t.matmul(
                            pA[:, 512 * (t8 % 4):512 * (t8 % 4) + 512],
                            asl[:, 128 * t8:128 * t8 + 128],
                            vw(k)[0], start=(k == 0), stop=(k == 7)))
                    S.group("t", fns, reads=rds, writes=[("P", t8 % 4)],
                            tag=f"L{l}h{hh}v{fc}_{t8}")
                    S.op("s", lambda sc, t8=t8, fc=fc: sc.activation(
                        s_v[:, 1040 * t8 + 520 * fc:1040 * t8 + 520 * fc + 520]
                        .rearrange("p (h e) -> p h e", e=65)[:, :, 0:64],
                        pA[:, 512 * (t8 % 4):512 * (t8 % 4) + 512]
                        .rearrange("p (h e) -> p h e", e=64), AF.Copy),
                        reads=[("P", t8 % 4)], writes=[("sv", t8)],
                        tag=f"L{l}h{hh}ve{fc}_{t8}")
            qk_prep(l, hh)
            for sl_ in range(8):
                g = 8 * hh + sl_
                for hgp in range(4):
                    for hi in range(4):
                        h = 4 * hgp + hi
                        p, hp = h // 2, h % 2
                        qap = qsl(p)[64 * hp:64 * hp + 64, 128 * sl_:128 * sl_ + 128]
                        pc = 4 + 2 * (sl_ % 2) + (h % 2)
                        psim = pB[:, 512 * (pc - 4):512 * (pc - 4) + 384]
                        fns = []
                        rds = list(qkeys(p))
                        for jt in range(3):
                            gj = g - 2 + jt
                            if gj < 0:
                                kap = s_khalo[64 * hp:64 * hp + 64,
                                              256 * p + 128 * (gj + 2):
                                              256 * p + 128 * (gj + 2) + 128]
                                rds.append(("khalo",))
                            elif gj < 8 * hh:
                                kap = s_kcar[64 * hp:64 * hp + 64,
                                             256 * p + 128 * (gj - 6):
                                             256 * p + 128 * (gj - 6) + 128]
                                rds.append(("khalo",))
                            else:
                                lt = gj - 8 * hh
                                kap = ksl(p)[64 * hp:64 * hp + 64,
                                             128 * lt:128 * lt + 128]
                                rds += kkeys(p)
                            fns.append(lambda t, kap=kap, qap=qap, jt=jt, psim=psim:
                                       t.matmul(psim[:, 128 * jt:128 * (jt + 1)],
                                                kap, qap, start=(jt == 0), stop=False))
                        if g == 0:
                            mk = [(0, s_m00, ("f_m00",)), (1, s_m01, ("f_m01",)),
                                  (2, s_mkl, ("f_masklo",))]
                        elif g == 1:
                            mk = [(0, s_m10, ("f_m10",)), (2, s_mkl, ("f_masklo",))]
                        else:
                            mk = [(0, s_mku, ("f_maskup",)), (2, s_mkl, ("f_masklo",))]
                        for idx, (jt, mt, mkey) in enumerate(mk):
                            last = idx == len(mk) - 1
                            fns.append(lambda t, jt=jt, mt=mt, psim=psim, last=last:
                                       t.matmul(psim[:, 128 * jt:128 * (jt + 1)],
                                                kb.s_idb[:, :], mt[:, :],
                                                start=False, stop=last))
                            rds.append(mkey)
                        rds.append("s_idb")
                        S.group("t", fns, reads=rds, writes=[("P", pc)],
                                tag=f"L{l}g{g}h{h}sim")
                        ab = h % 2
                        S.op("s", lambda sc, psim=psim, ab=ab: sc.activation(
                            s_attnT[:, 512 * ab:512 * ab + 384], psim[:, :], AF.Exp),
                            reads=[("P", pc)], writes=[ATK[ab]],
                            tag=f"L{l}g{g}h{h}exp")
                        ofs = 128 * h
                        fns2 = []
                        rds2 = [ATK[ab]]
                        for jt in range(3):
                            gj = g - 2 + jt
                            if gj < 0:
                                vap = vhalo_ap(gj + 2, h)
                                rds2 += vhk(gj + 2)
                            elif gj < 8 * hh:
                                vap = vhalo_ap(gj - 6, h)
                                rds2 += vhk(gj - 6)
                            else:
                                lt = gj - 8 * hh
                                vap = s_v[:, 1040 * lt + 65 * h:1040 * lt + 65 * h + 65]
                                rds2.append(("sv", lt))
                            st = (jt == 0) and (hi == 0)
                            sp = (jt == 2) and (hi == 3)
                            fns2.append(lambda t, vap=vap, jt=jt, ab=ab, ofs=ofs,
                                        st=st, sp=sp:
                                        t.matmul(pA[:, ofs:ofs + 65],
                                                 s_attnT[:, 512 * ab + 128 * jt:
                                                         512 * ab + 128 * (jt + 1)],
                                                 vap, start=st, stop=sp))
                        S.group("t", fns2, reads=rds2, writes=[("P", hgp)],
                                tag=f"L{l}g{g}h{h}out")
                    S.op("v", lambda v, hgp=hgp: v.reciprocal(
                        s_zinv[:, 4 * hgp:4 * hgp + 4],
                        pA[:, 512 * hgp:512 * (hgp + 1)]
                        .rearrange("p (h e) -> p h e", e=128)[:, :, 64]),
                        reads=[("P", hgp)], writes=[("zinv",)],
                        tag=f"L{l}g{g}zi{hgp}")
                    S.op("v", lambda v, hgp=hgp: v.tensor_tensor(
                        otok[:, :].bitcast(F32)[:, 256 * hgp:256 * (hgp + 1)]
                        .rearrange("p (h e) -> p h e", e=64),
                        pA[:, 512 * hgp:512 * (hgp + 1)]
                        .rearrange("p (h e) -> p h e", e=128)[:, :, 0:64],
                        s_zinv[:, 4 * hgp:4 * hgp + 4].unsqueeze(2)
                        .broadcast_to([128, 4, 64]), ALU.mult),
                        reads=[("P", hgp), ("zinv",)], writes=[OTK[hgp]],
                        tag=f"L{l}g{g}ev{hgp}")
                for j8 in range(8):
                    tb_ = 4 + j8 % 2
                    S.op("t", lambda t, j8=j8, tb_=tb_: t.transpose(
                        pB[:, 512 * (tb_ - 4):512 * (tb_ - 4) + 128],
                        otok[:, :].bitcast(F32)[:, 128 * j8:128 * (j8 + 1)],
                        kb.s_id[:, :]),
                        reads=[OTK[j8 // 2], "s_id"], writes=[("P", tb_)],
                        tag=f"L{l}g{g}tr{j8}")
                    base = s_delta if j8 < 4 else s_aT
                    jj = j8 % 4
                    dstap = base[:, :].bitcast(BF16)[:, 1024 * jj + 128 * sl_:
                                                     1024 * jj + 128 * sl_ + 128]
                    S.op("s", lambda sc, dstap=dstap, tb_=tb_: sc.activation(
                        dstap, pB[:, 512 * (tb_ - 4):512 * (tb_ - 4) + 128], AF.Copy),
                        reads=[("P", tb_)], writes=[DK(jj) if j8 < 4 else AK(jj)],
                        tag=f"L{l}g{g}tre{j8}")
            if hh == 0:
                for p in range(8):
                    S.op("v", lambda v, p=p: v.tensor_copy(
                        s_kcar[:, 256 * p:256 * (p + 1)], ksl(p)[:, 768:1024]),
                        reads=kkeys(p), writes=[("khalo",)], tag=f"L{l}kc{p}")
                for t2 in range(2):
                    S.op("v", lambda v, t2=t2: v.tensor_copy(
                        W[6 + t2][:, 0:1040],
                        s_v[:, (6 + t2) * 1040:(7 + t2) * 1040]),
                        reads=[("sv", 6 + t2)], writes=vhk(t2), tag=f"L{l}vc{t2}")
            for m8 in range(8):
                wload(f"L{l}h{hh}wo{m8}", LoT[m8], m8 % 2)
                po = 1024 * (m8 % 2)
                for ch in range(2):
                    fns = []
                    for k in range(8):
                        bt = s_delta if k < 4 else s_aT
                        jj = k % 4
                        rhs = bt[:, :].bitcast(BF16)[:, 1024 * jj + 512 * ch:
                                                     1024 * jj + 512 * ch + 512]
                        fns.append(lambda t, k=k, m8=m8, ch=ch, rhs=rhs,
                                   po=po: t.matmul(
                            pA[:, po + 512 * ch:po + 512 * (ch + 1)],
                            s_wbuf[:, 1024 * (m8 % 2) + 128 * k:
                                   1024 * (m8 % 2) + 128 * (k + 1)],
                            rhs, start=(k == 0), stop=(k == 7)))
                    S.group("t", fns,
                            reads=WBH(m8 % 2) + [DK(c) for c in range(4)]
                            + [AK(c) for c in range(4)],
                            writes=[("P", 2 * (m8 % 2) + ch)],
                            tag=f"L{l}h{hh}wo{m8}_{ch}")
                    S.op("v", lambda v, m8=m8, ch=ch, c0=c0, po=po: v.tensor_tensor(
                        xT[m8][:, c0 + 512 * ch:c0 + 512 * ch + 512],
                        xT[m8][:, c0 + 512 * ch:c0 + 512 * ch + 512],
                        pA[:, po + 512 * ch:po + 512 * (ch + 1)], ALU.add),
                        reads=[("xT", m8, 2 * hh + ch), ("P", 2 * (m8 % 2) + ch)],
                        writes=[("xT", m8, 2 * hh + ch)], tag=f"L{l}h{hh}wor{m8}_{ch}")

        for i, d in enumerate(range(4, 8)):
            S.op("v", lambda v, d=d, i=i: v.tensor_copy(
                s_xcol[:, i:i + 1], xT[d][:, TC - 1:TC]),
                reads=[("xT", d, 3)], writes=[("xcol",)], tag=f"L{l}x2c{i}")
            S.op("v", lambda v, d=d, i=i: v.tensor_copy(
                s_xcol[:, 4 + i:5 + i], xT[d][:, 1023:1024]),
                reads=[("xT", d, 1)], writes=[("xcol",)], tag=f"L{l}x2m{i}")
        kb.store("gd", ag_x2_in.ap()[0, :].rearrange("(p c) -> p c", p=128),
                 s_xcol[:, 0:4], reads=[("xcol",)], key=("agx2",), tag=f"L{l}agx2st")
        S.checkpoint("gd")
        S.cc(lambda g: g.collective_compute(
            "AllGather", ALU.bypass, replica_groups=groups4,
            ins=[ag_x2_in.ap()[:, :]], outs=[ag_x2_out.ap()[:, :]]),
            reads=[("agx2",)], writes=["ag_x2_out"], tag=f"L{l}agx2")
        S.op("v", lambda v: v.memset(s_xcol[:, 0:4], 0.0), writes=[("xcol",)],
             tag=f"L{l}x2z")
        for j in range(4):
            kb.load("gd", s_attnT[:, :].bitcast(F32)[:, 0:4],
                    ag_x2_out.ap()[j, :].rearrange("(p c) -> p c", p=128),
                    ATK[0], reads=["ag_x2_out"], tag=f"L{l}agx2ld{j}")
            S.op("v", lambda v, j=j: v.scalar_tensor_tensor(
                s_xcol[:, 0:4], s_attnT[:, :].bitcast(F32)[:, 0:4],
                s_ccm[:, j:j + 1], s_xcol[:, 0:4], ALU.mult, ALU.add),
                reads=[ATK[0], ("f_ccm",), ("xcol",)], writes=[("xcol",)],
                tag=f"L{l}x2sel{j}")

        build_ffn(kb, l, L1wT, L2wT, s_b1, s_lng, s_lnbv, s_b2v, s_wbuf,
                  emit_ain_half, ain, WBH)
    l0stack.close()


def build_ffn(kb, l, L1w, L2w, s_b1, s_lng, s_lnbv, s_b2v, s_wbuf,
              emit_ain_half, ain, WBH):
    nc, S = kb.nc, kb.S
    xT, W, P = kb.xT, kb.W, kb.P
    pA, pB = kb.pA, kb.pB
    s_cb, s_cbb, s_ones, s_stats = kb.s_cb, kb.s_cbb, kb.s_ones, kb.s_stats
    s_delta, s_aT = kb.s_delta, kb.s_aT
    STK = ("stats",)
    DK = lambda c: ("delta", c)
    AK = lambda c: ("aT", c)
    otok = W[4]
    OTK = [("W", 4, c) for c in range(4)]

    def hga(i):
        if i < 16:
            sl = W[8 + i // 2][:, 1024 * (i % 2):1024 * (i % 2) + 1024]
            keys = [("W", 8 + i // 2, 2 * (i % 2) + j) for j in range(2)]
        elif i < 20:
            jj = i - 16
            sl = s_delta[:, :].bitcast(BF16)[:, 1024 * jj:1024 * (jj + 1)]
            keys = [DK(jj)]
        else:
            jj = i - 20
            sl = s_aT[:, :].bitcast(BF16)[:, 1024 * jj:1024 * (jj + 1)]
            keys = [AK(jj)]
        return sl, keys

    for hh in range(2):
        c0 = 1024 * hh
        emit_ain_half(l, hh, f"F{l}h{hh}")
        for ch in range(2):
            cc = 512 * ch
            fns = []
            rds = ["s_cbb"]
            for d in range(8):
                asl, akeys = ain(d)
                rds.append(akeys[ch])
                fns.append(lambda t, asl=asl, cc=cc, d=d: t.matmul(
                    P[0][0:1, 0:512], kb.s_cbb[:, 0:1], asl[:, cc:cc + 512],
                    start=(d == 0), stop=(d == 7)))
            S.group("t", fns, reads=rds, writes=[("P", 0)], tag=f"F{l}h{hh}sum{ch}")
            for d in range(8):
                asl, akeys = ain(d)
                S.op("s", lambda sc, asl=asl, cc=cc: sc.activation(
                    otok[:, 0:512], asl[:, cc:cc + 512], AF.Square),
                    reads=[akeys[ch]], writes=[OTK[0]], tag=f"F{l}h{hh}sq{ch}_{d}")
                S.op("t", lambda t, d=d: t.matmul(
                    P[1][0:1, 0:512], kb.s_cbb[:, 0:1], otok[:, 0:512],
                    start=(d == 0), stop=(d == 7)),
                    reads=[OTK[0], "s_cbb"], writes=[("P", 1)],
                    tag=f"F{l}h{hh}sq2{ch}_{d}")
            S.op("s", lambda sc, cc=cc, c0=c0: sc.activation(
                s_stats[32:33, c0 + cc:c0 + cc + 512], P[0][0:1, 0:512], AF.Copy),
                reads=[("P", 0)], writes=[STK], tag=f"F{l}h{hh}se{ch}")
            S.op("s", lambda sc, cc=cc, c0=c0: sc.activation(
                s_stats[64:65, c0 + cc:c0 + cc + 512], P[1][0:1, 0:512], AF.Copy),
                reads=[("P", 1)], writes=[STK], tag=f"F{l}h{hh}se2{ch}")
        S.op("v", lambda v, c0=c0: v.tensor_scalar(s_stats[32:33, c0:c0 + 1024],
                                            s_stats[32:33, c0:c0 + 1024],
                                            1.0 / 1024.0, None, ALU.mult),
             reads=[STK], writes=[STK], tag=f"F{l}h{hh}mu")
        for ch in range(2):
            cc = c0 + 512 * ch
            S.op("s", lambda sc, cc=cc: sc.activation(
                P[0][0:1, 0:512], s_stats[32:33, cc:cc + 512], AF.Square),
                reads=[STK], writes=[("P", 0)], tag=f"F{l}h{hh}mu2_{ch}")
            S.op("v", lambda v, cc=cc: v.tensor_scalar(
                s_stats[64:65, cc:cc + 512], s_stats[64:65, cc:cc + 512],
                1.0 / 1024.0, None, ALU.mult),
                reads=[STK], writes=[STK], tag=f"F{l}h{hh}ex2_{ch}")
            S.op("v", lambda v, cc=cc: v.tensor_tensor(
                s_stats[64:65, cc:cc + 512], s_stats[64:65, cc:cc + 512],
                P[0][0:1, 0:512], ALU.subtract),
                reads=[STK, ("P", 0)], writes=[STK], tag=f"F{l}h{hh}var_{ch}")
        S.op("s", lambda sc, c0=c0: sc.activation(s_stats[64:65, c0:c0 + 1024],
                                           s_stats[64:65, c0:c0 + 1024], AF.Sqrt,
                                           bias=s_cb[0:1, CB_EPS5:CB_EPS5 + 1]),
             reads=[STK, "s_cb"], writes=[STK], tag=f"F{l}h{hh}sd")
        S.op("v", lambda v, c0=c0: v.reciprocal(s_stats[64:65, c0:c0 + 1024],
                                         s_stats[64:65, c0:c0 + 1024]),
             reads=[STK], writes=[STK], tag=f"F{l}h{hh}rs")
        for ch in range(2):
            cc = c0 + 512 * ch
            S.op("t", lambda t, cc=cc, ch=ch: t.matmul(
                pA[:, 512 * ch:512 * (ch + 1)], s_ones[32:33, :],
                s_stats[32:33, cc:cc + 512], start=True, stop=True),
                reads=[STK, "s_ones"], writes=[("P", ch)], tag=f"F{l}h{hh}bm{ch}")
            S.op("t", lambda t, cc=cc, ch=ch: t.matmul(
                pA[:, 1024 + 512 * ch:1024 + 512 * (ch + 1)], s_ones[64:65, :],
                s_stats[64:65, cc:cc + 512], start=True, stop=True),
                reads=[STK, "s_ones"], writes=[("P", 2 + ch)], tag=f"F{l}h{hh}br{ch}")
        for d in range(8):
            asl, akeys = ain(d)
            for ch in range(2):
                cc = 512 * ch
                S.op("v", lambda v, asl=asl, cc=cc, ch=ch: v.tensor_tensor(
                    asl[:, cc:cc + 512], asl[:, cc:cc + 512],
                    pA[:, 512 * ch:512 * (ch + 1)], ALU.subtract),
                    reads=[akeys[ch], ("P", ch)], writes=[akeys[ch]],
                    tag=f"F{l}h{hh}a1{d}_{ch}")
                S.op("v", lambda v, asl=asl, cc=cc, ch=ch, d=d: v.scalar_tensor_tensor(
                    asl[:, cc:cc + 512], asl[:, cc:cc + 512], s_lng[:, d:d + 1],
                    pA[:, 1024 + 512 * ch:1024 + 512 * (ch + 1)], ALU.mult, ALU.mult),
                    reads=[akeys[ch], "lv_g", ("P", 2 + ch)], writes=[akeys[ch]],
                    tag=f"F{l}h{hh}a2{d}_{ch}")
                S.op("v", lambda v, asl=asl, cc=cc, d=d: v.tensor_scalar(
                    asl[:, cc:cc + 512], asl[:, cc:cc + 512],
                    s_lnbv[:, d:d + 1], None, ALU.add),
                    reads=[akeys[ch], "lv_b"], writes=[akeys[ch]],
                    tag=f"F{l}h{hh}a3{d}_{ch}")
        W5 = W[5]
        ATK5 = [("W", 5, c) for c in range(4)]

        def w1buf(fp, part):
            if fp % 2 == 0:
                return (s_wbuf[:, 1024 * part:1024 * (part + 1)], WBH(part))
            return (W5[:, 1024 * part:1024 * (part + 1)],
                    ATK5[2 * part:2 * part + 2])
        for fp in range(22):
            for part, foff in [(0, fp), (1, fp + 22)]:
                bsl, bkeys = w1buf(fp, part)
                kb.load("gd", bsl.rearrange("p (k c) -> p k c", k=8),
                        L1w[foff], list(bkeys), tag=f"F{l}h{hh}w1{fp}_{part}")
            pbuf = pB if fp % 2 == 0 else pA
            kofs = 4 if fp % 2 == 0 else 0
            gsc = 1536 if fp % 2 == 0 else 1024
            gk = OTK[3] if fp % 2 == 0 else OTK[2]
            for ch in range(2):
                for part in range(2):
                    bsl, bkeys = w1buf(fp, part)
                    fns = []
                    rds = list(bkeys)
                    for k in range(8):
                        asl, akeys = ain(k)
                        rds.append(akeys[ch])
                        fns.append(lambda t, k=k, ch=ch, asl=asl, bsl=bsl,
                                   part=part, pbuf=pbuf: t.matmul(
                            pbuf[:, 512 * (2 * ch + part):512 * (2 * ch + part) + 512],
                            bsl[:, 128 * k:128 * (k + 1)],
                            asl[:, 512 * ch:512 * ch + 512],
                            start=(k == 0), stop=(k == 7)))
                    S.group("t", fns, reads=rds,
                            writes=[("P", kofs + 2 * ch + part)],
                            tag=f"F{l}h{hh}w1m{fp}_{ch}_{part}")
                S.op("s", lambda sc, fp=fp, ch=ch, pbuf=pbuf, gsc=gsc: sc.activation(
                    otok[:, gsc:gsc + 512], pbuf[:, 1024 * ch + 512:1024 * ch + 1024],
                    AF.Gelu, bias=s_b1[:, 22 + fp:23 + fp]),
                    reads=[("P", kofs + 2 * ch + 1), "lv_b1"], writes=[gk],
                    tag=f"F{l}h{hh}ge{fp}_{ch}")
                sl, keys = hga(fp)
                S.op("v", lambda v, sl=sl, ch=ch, fp=fp, pbuf=pbuf,
                     gsc=gsc: v.scalar_tensor_tensor(
                    sl[:, 512 * ch:512 * ch + 512],
                    pbuf[:, 1024 * ch:1024 * ch + 512], s_b1[:, fp:fp + 1],
                    otok[:, gsc:gsc + 512], ALU.add, ALU.mult),
                    reads=[("P", kofs + 2 * ch), "lv_b1", gk],
                    writes=[keys[min(ch, len(keys) - 1)]],
                    tag=f"F{l}h{hh}hg{fp}_{ch}")
        def w2buf(m8, k):
            if m8 % 2 == 0:
                return (s_wbuf[:, 128 * k:128 * (k + 1)], ("wb", k // 4))
            if k < 16:
                return (otok[:, 128 * k:128 * (k + 1)], OTK[k // 4])
            return (W5[:, 128 * (k - 16):128 * (k - 15)], ATK5[(k - 16) // 4])
        for m8 in range(8):
            if m8 % 2 == 0:
                kb.load("gd", s_wbuf[:, 0:2816].rearrange("p (k c) -> p k c", k=22),
                        L2w[m8], [("wb", q) for q in range(6)],
                        tag=f"F{l}h{hh}w2{m8}")
            else:
                kb.load("gd", otok[:, 0:2048].rearrange("p (k c) -> p k c", k=16),
                        L2w[m8][:, 0:16, :],
                        list(OTK), tag=f"F{l}h{hh}w2a{m8}")
                kb.load("gd", W5[:, 0:768].rearrange("p (k c) -> p k c", k=6),
                        L2w[m8][:, 16:22, :],
                        [ATK5[0], ATK5[1]], tag=f"F{l}h{hh}w2b{m8}")
            wkk = [w2buf(m8, k)[1] for k in range(22)]
            po = 1024 * (m8 % 2)
            for ch in range(2):
                fns = []
                rds = list(dict.fromkeys(wkk))
                for k in range(22):
                    sl, keys = hga(k)
                    rds += keys
                    lhs = w2buf(m8, k)[0]
                    fns.append(lambda t, k=k, sl=sl, ch=ch, lhs=lhs, po=po: t.matmul(
                        pA[:, po + 512 * ch:po + 512 * (ch + 1)],
                        lhs,
                        sl[:, 512 * ch:512 * ch + 512],
                        start=(k == 0), stop=(k == 21)))
                S.group("t", fns, reads=rds, writes=[("P", 2 * (m8 % 2) + ch)],
                        tag=f"F{l}h{hh}w2m{m8}_{ch}")
                S.op("v", lambda v, m8=m8, ch=ch, c0=c0, po=po: v.scalar_tensor_tensor(
                    xT[m8][:, c0 + 512 * ch:c0 + 512 * ch + 512],
                    pA[:, po + 512 * ch:po + 512 * (ch + 1)], s_b2v[:, m8:m8 + 1],
                    xT[m8][:, c0 + 512 * ch:c0 + 512 * ch + 512], ALU.add, ALU.add),
                    reads=[("P", 2 * (m8 % 2) + ch), "lv_b2", ("xT", m8, 2 * hh + ch)],
                    writes=[("xT", m8, 2 * hh + ch)], tag=f"F{l}h{hh}w2r{m8}_{ch}")


def build_l1attn(kb, groups4):
    nc, S = kb.nc, kb.S
    xT, W, P = kb.xT, kb.W, kb.P
    pA, pB = kb.pA, kb.pB
    s_cb, s_ones, s_stats = kb.s_cb, kb.s_ones, kb.s_stats
    s_delta, s_aT = kb.s_delta, kb.s_aT
    s_v, s_wbuf, s_zinv = kb.s_v, kb.s_wbuf, kb.s_zinv
    sb = kb.sb
    STK = ("stats",)
    DK = lambda c: ("delta", c)
    AK = lambda c: ("aT", c)
    FENCE = [("xT", d, c) for d in range(8) for c in range(4)]

    s_l1b = sb("s_l1b", [128, 2048], BF16)
    s_bq = sb("s_bq", [128, 24])
    s_bp = sb("s_bp", [128, 8])
    s_ccn = sb("s_ccn", [128, 4])
    s_ccpw = sb("s_ccpw", [128, 4])
    s_r4 = sb("s_r4", [128, 32])
    s_r4b = sb("s_r4b", [128, 32])
    ag_r_in = nc.dram_tensor("ag_r_in", [1, 128 * 32], F32)
    ag_r_out = nc.dram_tensor("ag_r_out", [4, 128 * 32], F32)
    ag_u_in = nc.dram_tensor("ag_u_in", [1, 128 * 32], BF16)
    ag_u_out = nc.dram_tensor("ag_u_out", [4, 128 * 32], BF16)

    for nm, dst, key in [("f_l1bias", s_l1b, ("l1b",)), ("f_l1bq", s_bq, ("l1bq",)),
                         ("f_l1bp", s_bp, ("l1bp",)), ("f_ccn", s_ccn, ("ccn",)),
                         ("f_ccpw", s_ccpw, ("ccpw",))]:
        kb.load("gd", dst[:, :], kb.ap[nm], key, reads=FENCE)
    S.checkpoint("gd")

    Lw = kb.ap["f_l1w"]
    LwT = kb.ap["f_l1wT"]
    LpT = kb.ap["f_l1projT"]
    WBH = lambda par: [("wb", 2 * par), ("wb", 2 * par + 1)]

    def wload(tag, tile3d, par):
        kb.load("gd", s_wbuf[:, 1024 * par:1024 * (par + 1)]
                .rearrange("p (k c) -> p k c", k=8),
                tile3d, WBH(par), tag=tag)

    def ain(d):
        sl = W[d % 4][:, 1024 * (d // 4):1024 * (d // 4) + 1024]
        return sl, [("W", d % 4, 2 * (d // 4) + i) for i in range(2)]

    def qsl(p):
        return W[8 + p // 2][:, 1024 * (p % 2):1024 * (p % 2) + 1024]

    def qkeys(p):
        return [("W", 8 + p // 2, 2 * (p % 2) + i) for i in range(2)]

    def ksl(p):
        return W[12 + p // 2][:, 1024 * (p % 2):1024 * (p % 2) + 1024]

    def kkeys(p):
        return [("W", 12 + p // 2, 2 * (p % 2) + i) for i in range(2)]

    def orl(d):
        sl = W[12 + d // 2][:, 1024 * (d % 2):1024 * (d % 2) + 1024]
        return sl, [("W", 12 + d // 2, 2 * (d % 2) + i) for i in range(2)]

    otok = W[4]
    OTK = [("W", 4, c) for c in range(4)]
    s_attnT = W[5]
    ATK = [("W", 5, c) for c in range(4)]

    def vw(k):
        if k < 4:
            return s_wbuf[:, 512 * k:512 * (k + 1)], ("wb", k)
        return W[5][:, 512 * (k - 4):512 * (k - 4) + 512], ATK[k - 4]

    for d in range(8):
        kb.store("gd", ag_r_in.ap()[0, :].rearrange("(p c) -> p c", p=128)
                 [:, 4 * d:4 * d + 4],
                 xT[d][:, 0:4], reads=[("xT", d, 0)], key=("agr",), tag=f"agr{d}")
    S.checkpoint("gd")
    S.cc(lambda g: g.collective_compute(
        "AllGather", ALU.bypass, replica_groups=groups4,
        ins=[ag_r_in.ap()[:, :]], outs=[ag_r_out.ap()[:, :]]),
        reads=[("agr",)], writes=["ag_r_out"], tag="agroll")
    S.op("v", lambda v: v.memset(s_r4[:, :], 0.0), reads=FENCE, writes=[("r4",)],
         tag="r4z")
    for j in range(4):
        kb.load("gd", s_stats[:, 0:32],
                ag_r_out.ap()[j, :].rearrange("(p c) -> p c", p=128),
                STK, reads=["ag_r_out"], tag=f"agrl{j}")
        S.op("v", lambda v, j=j: v.scalar_tensor_tensor(
            s_r4[:, :], s_stats[:, 0:32], s_ccn[:, j:j + 1], s_r4[:, :],
            ALU.mult, ALU.add),
            reads=[STK, ("ccn",), ("r4",)], writes=[("r4",)], tag=f"agrs{j}")

    for hh in range(2):
        for d in range(8):
            sl, keys = ain(d)
            if hh == 0:
                S.op("s", lambda sc, sl=sl, d=d: sc.activation(
                    sl[:, :], xT[d][:, 4:1028], AF.Copy),
                    reads=[("xT", d, 0), ("xT", d, 1), ("xT", d, 2)],
                    writes=keys, tag=f"A1h0ain{d}")
            else:
                S.op("s", lambda sc, sl=sl, d=d: sc.activation(
                    sl[:, 0:1020], xT[d][:, 1028:2048], AF.Copy),
                    reads=[("xT", d, 2), ("xT", d, 3)],
                    writes=keys, tag=f"A1h1ain{d}")
                S.op("v", lambda v, sl=sl, d=d: v.tensor_copy(
                    sl[:, 1020:1024], s_r4[:, 4 * d:4 * d + 4]),
                    reads=[("r4",)], writes=[keys[1]], tag=f"A1h1t{d}")
        for p in range(8):
            for side in range(2):
                par = p % 2
                wload(f"A1h{hh}w{side}_{p}", LwT[8 * side + p], par)
                dstsl = qsl(p) if side == 0 else ksl(p)
                dkeys = qkeys(p) if side == 0 else kkeys(p)
                po = 1024 * (p % 2)
                for ch in range(2):
                    fns = []
                    rds = WBH(par)
                    for k in range(8):
                        asl, akeys = ain(k)
                        rds.append(akeys[ch])
                        fns.append(lambda t, k=k, par=par, ch=ch, asl=asl,
                                   po=po: t.matmul(
                            pA[:, po + 512 * ch:po + 512 * (ch + 1)],
                            s_wbuf[:, 1024 * par + 128 * k:1024 * par + 128 * (k + 1)],
                            asl[:, 512 * ch:512 * ch + 512],
                            start=(k == 0), stop=(k == 7)))
                    S.group("t", fns, reads=rds,
                            writes=[("P", 2 * (p % 2) + ch)],
                            tag=f"A1h{hh}qk{side}_{p}_{ch}")
                    if side == 0:
                        S.op("v", lambda v, dstsl=dstsl, ch=ch, p=p,
                             po=po: v.tensor_scalar(
                            dstsl[:, 512 * ch:512 * ch + 512],
                            pA[:, po + 512 * ch:po + 512 * (ch + 1)],
                            s_bq[:, p:p + 1], None, ALU.add),
                            reads=[("P", 2 * (p % 2) + ch), ("l1bq",)],
                            writes=[dkeys[ch]], tag=f"A1h{hh}qe{p}_{ch}")
                    else:
                        S.op("v", lambda v, dstsl=dstsl, ch=ch, p=p,
                             po=po: v.tensor_scalar(
                            dstsl[:, 512 * ch:512 * ch + 512],
                            pA[:, po + 512 * ch:po + 512 * (ch + 1)],
                            s_bq[:, 8 + p:9 + p], 0.125, ALU.add, ALU.mult),
                            reads=[("P", 2 * (p % 2) + ch), ("l1bq",)],
                            writes=[dkeys[ch]], tag=f"A1h{hh}ke{p}_{ch}")
        for fc in range(2):
            for k in range(8):
                dst, key = vw(k)
                kb.load("gd", dst,
                        Lw[128 * k:128 * (k + 1),
                           2048 + 512 * fc:2048 + 512 * (fc + 1)],
                        key, tag=f"A1h{hh}vw{fc}_{k}")
            wrd = [vw(k)[1] for k in range(8)]
            for t8 in range(8):
                fns = []
                rds = list(wrd)
                for k in range(8):
                    asl, akeys = ain(k)
                    rds.append(akeys[(128 * t8) // 512])
                    fns.append(lambda t, t8=t8, k=k, asl=asl: t.matmul(
                        pA[:, 512 * (t8 % 4):512 * (t8 % 4) + 512],
                        asl[:, 128 * t8:128 * t8 + 128],
                        vw(k)[0], start=(k == 0), stop=(k == 7)))
                S.group("t", fns, reads=rds, writes=[("P", t8 % 4)],
                        tag=f"A1h{hh}v{fc}_{t8}")
                S.op("s", lambda sc, t8=t8, fc=fc: sc.activation(
                    s_v[:, 1040 * t8 + 520 * fc:1040 * t8 + 520 * fc + 520]
                    .rearrange("p (h e) -> p h e", e=65)[:, :, 0:64],
                    pA[:, 512 * (t8 % 4):512 * (t8 % 4) + 512]
                    .rearrange("p (h e) -> p h e", e=64), AF.Copy),
                    reads=[("P", t8 % 4)], writes=[("sv", t8)],
                    tag=f"A1h{hh}ve{fc}_{t8}")
        for sl_ in range(8):
            for b in range(4):
                fns = []
                rds = ["s_idb", ("l1b",)]
                for hi in range(4):
                    h = 4 * b + hi
                    p, hp = h // 2, h % 2
                    qap = qsl(p)[64 * hp:64 * hp + 64, 128 * sl_:128 * sl_ + 128]
                    kap = ksl(p)[64 * hp:64 * hp + 64, 128 * sl_:128 * sl_ + 128]
                    rds += qkeys(p) + kkeys(p)
                    dst = pB[:, 512 * b + 128 * hi:512 * b + 128 * hi + 128]
                    fns.append(lambda t, kap=kap, qap=qap, dst=dst, hi=hi: t.matmul(
                        dst, kap, qap, start=(hi == 0), stop=False))
                    fns.append(lambda t, dst=dst, h=h, hi=hi: t.matmul(
                        dst, kb.s_idb[:, :], s_l1b[:, 128 * h:128 * (h + 1)],
                        start=False, stop=(hi == 3)))
                S.group("t", fns, reads=rds, writes=[("P", 4 + b)],
                        tag=f"A1h{hh}s{sl_}_{b}")
                S.op("s", lambda sc, b=b: sc.activation(
                    s_attnT[:, 512 * b:512 * (b + 1)],
                    pB[:, 512 * b:512 * (b + 1)], AF.Exp),
                    reads=[("P", 4 + b)], writes=[ATK[b]],
                    tag=f"A1h{hh}e{sl_}_{b}")
                fns2 = []
                rds2 = [ATK[b]]
                for hi in range(4):
                    h = 4 * b + hi
                    vap = s_v[:, 1040 * sl_ + 65 * h:1040 * sl_ + 65 * h + 65]
                    rds2.append(("sv", sl_))
                    fns2.append(lambda t, h=h, hi=hi, vap=vap: t.matmul(
                        pA[:, 128 * h:128 * h + 65],
                        s_attnT[:, 128 * h:128 * (h + 1)], vap,
                        start=(hi == 0), stop=(hi == 3)))
                S.group("t", fns2, reads=rds2, writes=[("P", b)],
                        tag=f"A1h{hh}o{sl_}_{b}")
                S.op("v", lambda v, b=b: v.reciprocal(
                    s_zinv[:, 4 * b:4 * b + 4],
                    pA[:, 512 * b:512 * (b + 1)]
                    .rearrange("p (h e) -> p h e", e=128)[:, :, 64]),
                    reads=[("P", b)], writes=[("zinv",)], tag=f"A1h{hh}z{sl_}_{b}")
                S.op("v", lambda v, b=b: v.tensor_tensor(
                    otok[:, :].bitcast(F32)[:, 256 * b:256 * (b + 1)]
                    .rearrange("p (h e) -> p h e", e=64),
                    pA[:, 512 * b:512 * (b + 1)]
                    .rearrange("p (h e) -> p h e", e=128)[:, :, 0:64],
                    s_zinv[:, 4 * b:4 * b + 4].unsqueeze(2)
                    .broadcast_to([128, 4, 64]), ALU.mult),
                    reads=[("P", b), ("zinv",)], writes=[OTK[b]],
                    tag=f"A1h{hh}n{sl_}_{b}")
            for j8 in range(8):
                tb_ = 4 + j8 % 2
                S.op("t", lambda t, j8=j8, tb_=tb_: t.transpose(
                    pB[:, 512 * (tb_ - 4):512 * (tb_ - 4) + 128],
                    otok[:, :].bitcast(F32)[:, 128 * j8:128 * (j8 + 1)],
                    kb.s_id[:, :]),
                    reads=[OTK[j8 // 2], "s_id"], writes=[("P", tb_)],
                    tag=f"A1h{hh}tr{sl_}_{j8}")
                bt = s_delta if j8 < 4 else s_aT
                jj = j8 % 4
                dstap = bt[:, :].bitcast(BF16)[:, 1024 * jj + 128 * sl_:
                                               1024 * jj + 128 * sl_ + 128]
                S.op("v", lambda v, dstap=dstap, tb_=tb_, j8=j8: v.tensor_scalar(
                    dstap, pB[:, 512 * (tb_ - 4):512 * (tb_ - 4) + 128],
                    s_bq[:, 16 + j8:17 + j8], None, ALU.add),
                    reads=[("P", tb_), ("l1bq",)],
                    writes=[DK(jj) if j8 < 4 else AK(jj)],
                    tag=f"A1h{hh}te{sl_}_{j8}")
        for m8 in range(8):
            wload(f"A1h{hh}pw{m8}", LpT[m8], m8 % 2)
            osl, okeys = orl(m8)
            po = 1024 * (m8 % 2)
            for ch in range(2):
                fns = []
                for k in range(8):
                    bt = s_delta if k < 4 else s_aT
                    jj = k % 4
                    rhs = bt[:, :].bitcast(BF16)[:, 1024 * jj + 512 * ch:
                                                 1024 * jj + 512 * ch + 512]
                    fns.append(lambda t, k=k, m8=m8, ch=ch, rhs=rhs,
                               po=po: t.matmul(
                        pA[:, po + 512 * ch:po + 512 * (ch + 1)],
                        s_wbuf[:, 1024 * (m8 % 2) + 128 * k:
                               1024 * (m8 % 2) + 128 * (k + 1)],
                        rhs, start=(k == 0), stop=(k == 7)))
                S.group("t", fns,
                        reads=WBH(m8 % 2) + [DK(c) for c in range(4)]
                        + [AK(c) for c in range(4)],
                        writes=[("P", 2 * (m8 % 2) + ch)], tag=f"A1h{hh}pj{m8}_{ch}")
                S.op("s", lambda sc, osl=osl, ch=ch, po=po: sc.activation(
                    osl[:, 512 * ch:512 * ch + 512],
                    pA[:, po + 512 * ch:po + 512 * (ch + 1)], AF.Copy),
                    reads=[("P", 2 * (m8 % 2) + ch)], writes=[okeys[ch]],
                    tag=f"A1h{hh}pe{m8}_{ch}")
        for d in range(8):
            osl, okeys = orl(d)
            if hh == 0:
                S.op("v", lambda v, d=d, osl=osl: v.scalar_tensor_tensor(
                    xT[d][:, 4:1028], osl[:, 0:1024], s_bp[:, d:d + 1],
                    xT[d][:, 4:1028], ALU.add, ALU.add),
                    reads=okeys + [("l1bp",)] + [("xT", d, c) for c in range(3)],
                    writes=[("xT", d, c) for c in range(3)], tag=f"A1r0_{d}")
            else:
                S.op("v", lambda v, d=d, osl=osl: v.scalar_tensor_tensor(
                    xT[d][:, 1028:2048], osl[:, 0:1020], s_bp[:, d:d + 1],
                    xT[d][:, 1028:2048], ALU.add, ALU.add),
                    reads=okeys + [("l1bp",)] + [("xT", d, 2), ("xT", d, 3)],
                    writes=[("xT", d, 2), ("xT", d, 3)], tag=f"A1r1_{d}")

    for d in range(8):
        osl, okeys = orl(d)
        kb.store("gd", ag_u_in.ap()[0, :].rearrange("(p c) -> p c", p=128)
                 [:, 4 * d:4 * d + 4],
                 osl[:, 1020:1024], reads=okeys, key=("agu",), tag=f"agu{d}")
    S.checkpoint("gd")
    S.cc(lambda g: g.collective_compute(
        "AllGather", ALU.bypass, replica_groups=groups4,
        ins=[ag_u_in.ap()[:, :]], outs=[ag_u_out.ap()[:, :]]),
        reads=[("agu",)], writes=["ag_u_out"], tag="agunroll")
    S.op("v", lambda v: v.memset(s_r4b[:, :], 0.0), writes=[("r4b",)], tag="r4bz")
    for j in range(4):
        kb.load("gd", s_attnT[:, 0:32],
                ag_u_out.ap()[j, :].rearrange("(p c) -> p c", p=128),
                ATK[0], reads=["ag_u_out"], tag=f"agul{j}")
        S.op("v", lambda v, j=j: v.scalar_tensor_tensor(
            s_r4b[:, :], s_attnT[:, 0:32], s_ccpw[:, j:j + 1], s_r4b[:, :],
            ALU.mult, ALU.add),
            reads=[ATK[0], ("ccpw",), ("r4b",)], writes=[("r4b",)], tag=f"agus{j}")
    for d in range(8):
        S.op("v", lambda v, d=d: v.scalar_tensor_tensor(
            xT[d][:, 0:4], s_r4b[:, 4 * d:4 * d + 4], s_bp[:, d:d + 1],
            xT[d][:, 0:4], ALU.add, ALU.add),
            reads=[("r4b",), ("l1bp",), ("xT", d, 0)],
            writes=[("xT", d, 0)], tag=f"A1patch{d}")

    build_topln(kb, tag="ln3")


def build_l1ffn(kb):
    nc, S = kb.nc, kb.S
    xT, W, P = kb.xT, kb.W, kb.P
    pA, pB = kb.pA, kb.pB
    s_cb, s_stats = kb.s_cb, kb.s_stats
    s_delta, s_aT = kb.s_delta, kb.s_aT
    s_wbuf = kb.s_wbuf
    sb = kb.sb
    STK = ("stats",)
    DK = lambda c: ("delta", c)
    AK = lambda c: ("aT", c)
    FENCE = [("xT", d, c) for d in range(8) for c in range(4)]
    s_wd2 = sb("s_wd2", [128, 512], BF16)
    s_wbuf2 = sb("s_wbuf2", [128, 2816], BF16)
    WB2 = lambda q: ("wb2", q)
    Wg = kb.ap["f_wgT"]
    Wu = kb.ap["f_wuT"]
    Wd = kb.ap["f_wdT"]

    def hid(i):
        if i < 16:
            sl = W[8 + i // 2][:, 1024 * (i % 2):1024 * (i % 2) + 1024]
            keys = [("W", 8 + i // 2, 2 * (i % 2) + j) for j in range(2)]
        elif i < 20:
            jj = i - 16
            sl = s_delta[:, :].bitcast(BF16)[:, 1024 * jj:1024 * (jj + 1)]
            keys = [DK(jj)]
        else:
            jj = i - 20
            sl = s_aT[:, :].bitcast(BF16)[:, 1024 * jj:1024 * (jj + 1)]
            keys = [AK(jj)]
        return sl, keys

    for d in range(8):
        S.op("s", lambda sc, d=d: sc.activation(W[d][:, :], xT[d][:, :], AF.Copy),
             reads=[("xT", d, c) for c in range(4)],
             writes=[("W", d, c) for c in range(4)], tag=f"Fx3c{d}")

    for hh in range(2):
        c0 = 1024 * hh
        for hi in range(24):
            buf = s_wbuf if hi % 2 == 0 else s_wbuf2
            KW = (lambda q: ("wb", q)) if hi % 2 == 0 else WB2
            kb.load("gd", buf[:, 0:1024].rearrange("p (k c) -> p k c", k=8),
                    Wg[hi], [KW(0), KW(1)], tag=f"Fg{hh}_{hi}")
            kb.load("gd", buf[:, 1024:2048].rearrange("p (k c) -> p k c", k=8),
                    Wu[hi], [KW(2), KW(3)], tag=f"Fu{hh}_{hi}")
            hsl, hkeys = hid(hi)
            for ch in range(2):
                cc = c0 + 512 * ch
                pbuf = pA if hi % 2 == 0 else pB
                kofs = 0 if hi % 2 == 0 else 4
                ssc = 1024 * (hi % 2)
                for part in range(2):
                    fns = []
                    rds = [KW(2 * part), KW(2 * part + 1)]
                    for k in range(8):
                        rds.append(("W", k, cc // 512))
                        fns.append(lambda t, k=k, part=part, ch=ch, cc=cc,
                                   buf=buf, pbuf=pbuf: t.matmul(
                            pbuf[:, 1024 * part + 512 * ch:
                                 1024 * part + 512 * ch + 512],
                            buf[:, 1024 * part + 128 * k:
                                1024 * part + 128 * (k + 1)],
                            W[k][:, cc:cc + 512], start=(k == 0), stop=(k == 7)))
                    S.group("t", fns, reads=rds,
                            writes=[("P", kofs + 2 * part + ch)],
                            tag=f"F1{hh}m{hi}_{ch}_{part}")
                S.op("s", lambda sc, ch=ch, pbuf=pbuf, ssc=ssc: sc.activation(
                    s_stats[:, ssc + 512 * ch:ssc + 512 * ch + 512],
                    pbuf[:, 512 * ch:512 * (ch + 1)], AF.Silu,
                    scale=s_cb[:, CB_I127:CB_I127 + 1]),
                    reads=[("P", kofs + ch), "s_cb"], writes=[STK],
                    tag=f"F1{hh}sig{hi}_{ch}")
                S.op("v", lambda v, hsl=hsl, ch=ch, pbuf=pbuf,
                     ssc=ssc: v.scalar_tensor_tensor(
                    hsl[:, 512 * ch:512 * ch + 512],
                    pbuf[:, 1024 + 512 * ch:1024 + 512 * (ch + 1)],
                    s_cb[:, CB_I127:CB_I127 + 1],
                    s_stats[:, ssc + 512 * ch:ssc + 512 * ch + 512],
                    ALU.mult, ALU.mult),
                    reads=[("P", kofs + 2 + ch), "s_cb", STK],
                    writes=[hkeys[min(ch, len(hkeys) - 1)]],
                    tag=f"F1{hh}h{hi}_{ch}")
        for m8 in range(8):
            buf = s_wbuf if m8 % 2 == 0 else s_wbuf2
            KW = (lambda q: ("wb", q)) if m8 % 2 == 0 else WB2
            d2sl = s_wd2[:, 256 * (m8 % 2):256 * (m8 % 2) + 256]
            d2k = ("wd2", m8 % 2)
            kb.load("gd", buf[:, 0:2816].rearrange("p (k c) -> p k c", k=22),
                    Wd[m8][:, 0:22, :],
                    [KW(q) for q in range(6)],
                    reads=FENCE if (hh == 0 and m8 == 0) else [],
                    tag=f"Fd{hh}_{m8}")
            kb.load("gd", d2sl.rearrange("p (k c) -> p k c", k=2),
                    Wd[m8][:, 22:24, :],
                    d2k, reads=FENCE if (hh == 0 and m8 == 0) else [],
                    tag=f"Fd2{hh}_{m8}")
            po = 1024 * (m8 % 2)
            for ch in range(2):
                fns = []
                rds = [KW(q) for q in range(6)] + [d2k]
                for k in range(24):
                    hsl, hkeys = hid(k)
                    rds += hkeys
                    lhs = (buf[:, 128 * k:128 * (k + 1)] if k < 22
                           else d2sl[:, 128 * (k - 22):128 * (k - 21)])
                    fns.append(lambda t, k=k, ch=ch, hsl=hsl, lhs=lhs,
                               po=po: t.matmul(
                        pB[:, po + 512 * ch:po + 512 * (ch + 1)], lhs,
                        hsl[:, 512 * ch:512 * ch + 512],
                        start=(k == 0), stop=(k == 23)))
                S.group("t", fns, reads=rds,
                        writes=[("P", 4 + 2 * (m8 % 2) + ch)],
                        tag=f"F1{hh}o{m8}_{ch}")
                S.op("v", lambda v, m8=m8, ch=ch, c0=c0, po=po: v.scalar_tensor_tensor(
                    xT[m8][:, c0 + 512 * ch:c0 + 512 * ch + 512],
                    pB[:, po + 512 * ch:po + 512 * (ch + 1)],
                    s_cb[:, CB_I127:CB_I127 + 1],
                    xT[m8][:, c0 + 512 * ch:c0 + 512 * ch + 512],
                    ALU.mult, ALU.add),
                    reads=[("P", 4 + 2 * (m8 % 2) + ch), "s_cb",
                           ("xT", m8, 2 * hh + ch)],
                    writes=[("xT", m8, 2 * hh + ch)], tag=f"F1{hh}r{m8}_{ch}")

    build_topln(kb, tag="ln4")
